# version 13
# baseline (speedup 1.0000x reference)
import numpy as np
import ml_dtypes
from contextlib import ExitStack

import concourse.bass as bass
import concourse.mybir as mybir
from concourse.bass_utils import run_bass_kernel_spmd
from concourse.bass_types import AP

F32 = mybir.dt.float32
BF16 = mybir.dt.bfloat16
I32 = mybir.dt.int32
ALU = mybir.AluOpType
ACT = mybir.ActivationFunctionType
AX = mybir.AxisListType

D = 1024
NH = 32
HD = 64
DIN = 2048
DSTATE = 128
DCONV = 4
CONV_DIM = DIN + 2 * DSTATE
DINPROJ = 2 * DIN + 2 * DSTATE + NH
SC_HID = 256
SUMD = 64
NSLOT = 64
TAU1, TAU2 = 0.5, 0.3
EPS = 1e-5
L = 128
SUP = 256
NCORES = 8
IC = 2816
FFN_SH = [384, 384, 384, 384, 384, 384, 256, 256]
WD_SH = D // NCORES
WQ_SHARD_ELEMS = 384 * D * 2 + WD_SH * IC
ENGINES = ("sync", "tensor", "vector", "scalar", "gpsimd")

MIDS = [0.25, 0.75, 1.25, 1.75]
DEBUG = []
NQ = 8
PHASES = 8
NO_COLL = False
WQL = 5
FRL = 9
BBL = 9
PXL = 9


class Op:
    __slots__ = ("eng", "fn", "extra", "asyn", "qcls", "signal", "semval", "waits",
                 "tag", "idx")

    def __init__(self, eng, fn, extra, asyn, qcls, tag, idx):
        self.eng, self.fn, self.extra, self.asyn, self.qcls = eng, fn, extra, asyn, qcls
        self.signal, self.semval, self.waits, self.tag, self.idx = False, None, None, tag, idx


class Buf:
    __slots__ = ("name", "writer", "readers")

    def __init__(self, name):
        self.name, self.writer, self.readers = name, None, []


class Prog:

    def __init__(self, nc):
        self.nc = nc
        self.ops = []
        self.bufs = {}

    def buf(self, name):
        if name not in self.bufs:
            self.bufs[name] = Buf(name)
        return self.bufs[name]

    def op(self, eng, fn, reads=(), writes=(), after=(), dma=False, coll=False,
           tag=""):
        reads = list(reads)
        writes = list(writes)
        for b in reads:
            if isinstance(b, str) and len(b) == 5 and b.startswith("ps") \
                    and b[3] == "q" and b not in writes:
                writes.append(b)
        reads = [self.buf(b) if isinstance(b, str) else b for b in reads]
        writes = [self.buf(b) if isinstance(b, str) else b for b in writes]
        deps = list(after)
        for b in reads:
            if b.writer is not None:
                deps.append(b.writer)
        for b in writes:
            if b.writer is not None:
                deps.append(b.writer)
            deps.extend(b.readers)
        qcls = "l" if coll else ("q" if dma else "c")
        o = Op(eng, fn, [], dma or coll, qcls, tag, len(self.ops))
        seen = set()
        for d in deps:
            if d is not None and d is not o and id(d) not in seen:
                seen.add(id(d))
                o.extra.append(d)
        self.ops.append(o)
        for b in reads:
            if o not in b.readers:
                b.readers.append(o)
        for b in writes:
            b.writer = o
            b.readers = []
        return o

    def dma(self, eng, out, in_, reads=(), writes=(), after=(), tag=""):
        return self.op(eng, lambda e, out=out, in_=in_: e.dma_start(out=out, in_=in_),
                       reads=reads, writes=writes, after=after, dma=True, tag=tag)

    def coll(self, fn, reads=(), writes=(), after=(), tag=""):
        if NO_COLL:
            return self.op("gpsimd", lambda e: e.nop(), reads=reads, writes=writes,
                           after=after, tag=tag + "_skip")
        return self.op("gpsimd", fn, reads=reads, writes=writes, after=after,
                       coll=True, tag=tag)

    def fence(self):
        last = {}
        for o in self.ops:
            last[o.eng] = o
        return list(last.values())

    def finalize(self, block, sems):
        for o in self.ops:
            if o.asyn:
                o.signal = True
            for d in o.extra:
                if d.eng == o.eng == "tensor" and not d.asyn:
                    continue
                d.signal = True
        rr = {}
        for o in self.ops:
            if o.qcls == "q":
                i = rr.get(o.eng, 0)
                o.qcls = f"q{i % NQ}"
                rr[o.eng] = i + 1
        qcs = tuple(f"q{i}" for i in range(NQ))
        cnt = {(e, c): 0 for e in ENGINES for c in ("c", "l") + qcs}
        for o in self.ops:
            if o.signal:
                k = (o.eng, o.qcls)
                cnt[k] += 16 if o.qcls.startswith("q") else 1
                o.semval = cnt[k]
        water = {e: {} for e in ENGINES}
        prev_async = {}
        for o in self.ops:
            need = {}
            if o.asyn:
                qk = (o.eng, o.qcls)
                if prev_async.get(qk):
                    need[qk] = prev_async[qk]
                prev_async[qk] = o.semval
            for d in o.extra:
                if d.semval is None:
                    continue
                if d.eng == o.eng == "tensor" and not d.asyn:
                    continue
                k = (d.eng, d.qcls)
                if d.semval > need.get(k, 0):
                    need[k] = d.semval
            w = []
            for k, v in sorted(need.items()):
                if v > water[o.eng].get(k, 0):
                    w.append((k, v))
                    water[o.eng][k] = v
            o.waits = w
        streams = {e: [o for o in self.ops if o.eng == e] for e in ENGINES}

        def emitter(ename):
            def run(eng):
                for o in streams[ename]:
                    for (k, v) in o.waits:
                        eng.wait_ge(sems[k], v)
                    inst = o.fn(eng)
                    if o.signal:
                        inst.then_inc(sems[(o.eng, o.qcls)],
                                      16 if o.qcls.startswith("q") else 1)
                for qc in qcs + ("l",):
                    if cnt[(ename, qc)]:
                        eng.wait_ge(sems[(ename, qc)], cnt[(ename, qc)])
            return run

        for e in ENGINES:
            if streams[e]:
                getattr(block, e)(emitter(e))


def mkap(base, offset_elems, dims):
    if isinstance(base, AP):
        t, off = base.tensor, base.offset
    else:
        t, off = base, 0
    return AP(t, off + int(offset_elems), [[int(s), int(n)] for (s, n) in dims])


def ts(i, n=128):
    return bass.ts(i, n)


def build(T=2048, dbg=()):
    nc = bass.Bass("TRN2", target_bir_lowering=False)
    NCH = T // L
    NSUPS = T // SUP
    NTILE = T // 128
    ALLT = NCORES * T
    ALLP = ALLT // 128
    ALLF = ALLT // ALLP if ALLP == 128 else 128
    if ALLP > 128:
        raise ValueError("T too big")
    SCPAD = T + 8

    g = {}

    def dram(name, shape, dt=F32, kind="Internal", **kw):
        g[name] = nc.dram_tensor(name, shape, dt, kind=kind, **kw)
        return g[name]

    dram("x", [T, D], kind="ExternalInput")
    dram("norm_w", [D], kind="ExternalInput")
    dram("in_proj_w", [D, DINPROJ], kind="ExternalInput")
    dram("conv_w", [CONV_DIM, DCONV], kind="ExternalInput")
    dram("conv_b", [CONV_DIM], kind="ExternalInput")
    dram("dt_bias", [NH], kind="ExternalInput")
    dram("A_log", [NH], kind="ExternalInput")
    dram("Dvec", [NH], kind="ExternalInput")
    dram("ssm_norm_w", [DIN], kind="ExternalInput")
    dram("out_proj_w", [DIN, D], kind="ExternalInput")
    dram("scorer_w1", [D, SC_HID], kind="ExternalInput")
    dram("scorer_w2", [SC_HID], kind="ExternalInput")
    dram("summ_w", [D, SUMD], kind="ExternalInput")
    dram("q_w", [D, SUMD], kind="ExternalInput")
    dram("k_w", [SUMD, SUMD], kind="ExternalInput")
    dram("v_w", [SUMD, D], kind="ExternalInput")
    dram("gate_w", [2 * D, D], kind="ExternalInput")
    dram("cvec", [D], kind="ExternalInput")
    dram("wmod", [D, 6 * D], kind="ExternalInput")
    dram("wg_shard", [384, D], kind="ExternalInput")
    dram("wu_shard", [384, D], kind="ExternalInput")
    dram("wd_shard", [WD_SH, IC], kind="ExternalInput")
    dram("rowsel", [128, NTILE], BF16, kind="ExternalInput")
    dram("out", [T, D], kind="ExternalOutput")

    dram("hbm_rows", [64 * T])
    dram("mod_hbm", [6 * D])
    dram("x2p_spill", [D, T])
    dram("y_spill", [D, T], BF16)
    dram("q_spill", [SUMD, T], BF16)
    dram("summ_spill", [T, SUMD], BF16)
    dram("lred", [3 * 16384])
    dram("cc_sc_in", [SCPAD])
    dram("cc_sc_out", [NCORES * SCPAD], addr_space="Shared")
    dram("cc_pool_in", [NSLOT * SUMD + 16])
    dram("cc_pool_out", [NSLOT * SUMD + 16])
    g["cc_wq_in"] = nc.dram_tensor("cc_wq_in", [WQ_SHARD_ELEMS], BF16)
    g["cc_wq_out"] = nc.dram_tensor("cc_wq_out", [NCORES * WQ_SHARD_ELEMS], BF16,
                                    addr_space="Shared")

    dbgh = {}
    for (name, shape, dt_) in dbg:
        dbgh[name] = dram("dbg_" + name, list(shape), dt_, kind="ExternalOutput")

    g["c_ident"] = nc.inline_tensor(np.eye(128, dtype=ml_dtypes.bfloat16), "c_ident")
    g["c_ident32"] = nc.inline_tensor(np.eye(128, dtype=np.float32), "c_ident32")
    g["c_onescol"] = nc.inline_tensor(np.ones((128, 1), dtype=ml_dtypes.bfloat16), "c_onescol")
    g["c_onesrow32"] = nc.inline_tensor(np.ones((1, 128), dtype=np.float32), "c_onesrow32")
    tril = (np.arange(128)[:, None] <= np.arange(128)[None, :]).astype(np.float32)
    g["c_tril"] = nc.inline_tensor(tril, "c_tril")
    g["c_iota64"] = nc.inline_tensor(
        np.repeat(np.arange(64, dtype=np.float32)[None, :], 128, 0), "c_iota64")
    sup_m = (np.arange(128)[:, None] < np.arange(128)[None, :]).astype(ml_dtypes.bfloat16)
    g["c_strictup"] = nc.inline_tensor(sup_m, "c_strictup")

    p = Prog(nc)
    es = ExitStack()
    sems = {}
    for e in ENGINES:
        clss = ["c"]
        if e in ("sync", "scalar", "gpsimd"):
            clss += [f"q{i}" for i in range(NQ)]
        if e == "gpsimd":
            clss.append("l")
        for cls_ in clss:
            sems[(e, cls_)] = es.enter_context(nc.semaphore(f"sem_{e}_{cls_}"))

    MAP0 = (int(nc.sbuf_base) + 31) & ~31
    SB_LIMIT = int(nc.sbuf_top)
    cur = [MAP0]
    tensors = {}
    _off = {}

    def alloc(name, shape, dt=F32, at=None):
        nbytes = int(np.prod(shape[1:])) * mybir.dt.size(dt)
        nbytes = (nbytes + 31) & ~31
        o = cur[0] if at is None else at
        _off[name] = o
        t = nc.alloc_sbuf_tensor_at(name, list(shape), dt, offset=o)
        if at is None:
            cur[0] += nbytes
            assert cur[0] <= SB_LIMIT, f"sbuf overflow at {name}: {cur[0]}"
        else:
            assert o + nbytes <= SB_LIMIT, f"sbuf overlay overflow {name}"
        tensors[name] = t
        return t

    Pm = {}
    for (n, sh, dt_) in [
        ("ident", [128, 128], BF16), ("ident32", [128, 128], F32),
        ("onescol", [128, 1], BF16), ("onesrow32", [1, 128], F32),
        ("tril", [128, 128], F32), ("iota64", [128, 64], F32),
        ("strictup", [128, 128], BF16),
        ("modc", [128, 64], F32), ("gacond", [128, 8], F32),
        ("normw", [128, 8], F32), ("ssmw", [128, 16], F32),
        ("convw", [128, 18 * DCONV], F32), ("convb", [128, 18], F32),
        ("dtb", [32, 1], F32), ("negA", [32, 1], F32), ("Dcol", [128, 32], F32),
        ("kw", [64, 64], BF16),
        ("scores16", [16, 128], F32), ("S", [128, DIN], F32),
        ("srow32", [32, 1792], F32), ("dcrep", [128, NH], F32),
        ("cols", [128, 16], F32), ("colsB", [128, 16], BF16),
        ("maskrep", [128, 64], F32), ("pool", [64, 64], F32),
        ("m8", [128, 8], F32), ("halo", [128, 18 * 3], F32),
        ("kconst", [128, 4], F32),
        ("sb6464", [64, 128], BF16),
        ("kkT", [64, 64], BF16), ("poolT", [64, 64], BF16),
    ]:
        Pm[n] = alloc(n, sh, dt_)
    PERM_TOP = cur[0]

    Aa = {}
    cur[0] = PERM_TOP
    for (n, sh, dt_) in [
        ("w_inproj", [128, 8 * DINPROJ], BF16),
        ("w_outproj", [128, 16 * D], BF16),
        ("w_sc1", [128, 8 * SC_HID], BF16),
        ("w_summ", [128, 8 * SUMD], BF16),
        ("w_q", [128, 8 * SUMD], BF16),
        ("w_s2", [128, 2], BF16),
        ("x_tok", [128, D], F32),
        ("hT", [128, 8 * SUP], BF16),
        ("ztil", [128, 16 * SUP], BF16),
        ("acc0", [128, SUP + 8], F32),
        ("xst", [128, 16 * SUP], BF16),
        ("bct", [128, 2 * SUP], BF16),
        ("xsD", [128, 4 * L], F32),
        ("Mt", [128, 8 * L], BF16),
        ("csrep", [128, 8 * L], F32),
        ("erep2", [128, 4 * L], F32),
        ("yw", [128, 16 * L], F32),
        ("ycore", [128, 16 * L], BF16),
        ("x2pT", [128, 8 * SUP], F32),
        ("xd", [128, DIN], BF16),
        ("xdp", [128, DIN], BF16),
        ("btok", [128, DSTATE], BF16),
        ("Gt", [128, L], BF16),
        ("cstok", [128, NH], F32),
        ("dttok", [128, NH], F32),
        ("ddtok", [128, NH], F32),
        ("dstok", [128, NH], F32),
        ("dsup", [32, SUP], F32),
        ("cssup", [32, SUP], F32),
        ("essup", [32, SUP], F32),
        ("dssup", [32, SUP], F32),
        ("yTb", [128, 8 * L], BF16),
        ("r1", [128, 2 * L], BF16),
        ("rstd_rep", [128, L], F32),
        ("Sb", [128, DIN], BF16),
    ]:
        Aa[n] = alloc(n, sh, dt_)
    Aa["wmodbuf"] = alloc("wmodbuf", [128, 2048], BF16, at=_off["xd"])
    A_TOP = cur[0]
    assert A_TOP <= SB_LIMIT, f"phase A overflow: {A_TOP}"

    Cc = {}
    cur[0] = PERM_TOP
    for (n, sh, dt_) in [
        ("w_gate", [128, 16 * D], BF16),
        ("probsT", [64, T], BF16),
        ("logits", [128, 64], F32),
        ("prb", [128, 64], BF16),
        ("yrel", [128, 8 * 512], BF16),
        ("retrT", [128, 8 * 512], BF16),
        ("gterm", [128, 512], BF16),
        ("x2slab", [128, 8 * 512], F32),
        ("allsc", [128, 576], F32),
        ("vw", [64, D], BF16),
        ("vvb", [64, D], BF16),
        ("qT", [64, T], BF16),
        ("summ_tok", [128, (T // 128) * SUMD], BF16),
    ]:
        Cc[n] = alloc(n, sh, dt_)
    C_TOP = cur[0]
    assert C_TOP <= SB_LIMIT

    Bb = {}
    cur[0] = PERM_TOP
    TG = 1024
    for (n, sh, dt_) in [
        ("x2f", [128, 8 * 512], F32),
        ("x2tok", [128, 4 * D], F32),
        ("h2", [128, D], F32),
        ("h2q", [128, 4 * D], BF16),
        ("h2qT", [128, 8 * TG], BF16),
        ("cb1rep", [128, D], F32),
        ("sbrep", [128, D], F32),
        ("q1", [128, D], F32),
        ("q2", [128, D], F32),
        ("q3", [128, D], F32),
        ("qm", [128, D], I32),
        ("qsc", [128, 128], F32),
        ("wgu", [128, 2 * 8 * 384], BF16),
        ("wguB", [128, 2 * 8 * 384], BF16),
        ("hq", [128, 22 * TG], BF16),
        ("hffn", [128, 3 * TG], F32),
        ("hffnB", [128, 3 * TG], F32),
        ("hfq", [128, 384], BF16),
        ("wd", [128, 22 * 128], BF16),
        ("outtok", [128, 512], F32),
        ("ftmp", [128, 512], F32),
    ]:
        Bb[n] = alloc(n, sh, dt_)
    B_TOP = cur[0]
    assert B_TOP <= SB_LIMIT, f"phase B overflow: {B_TOP}"

    PS = [es.enter_context(nc.psum_tensor(f"pb{i}", [128, 512], F32)) for i in range(8)]

    ctx = dict(nc=nc, p=p, g=g, Pm=Pm, Aa=Aa, Cc=Cc, Bb=Bb, PS=PS, T=T, NCH=NCH,
               NSUPS=NSUPS, NTILE=NTILE, dbgh=dbgh, SCPAD=SCPAD)

    ph = PHASES
    if ph >= 1:
        phase0(ctx)
    if ph >= 2:
        wquant(ctx)
    if ph >= 3:
        for s in range(NSUPS):
            frontend_sup(ctx, s)
            if ph >= 4:
                for ci in (0, 1):
                    ssd_chunk(ctx, 2 * s + ci)
                    if ph >= 5:
                        chunk_epilogue(ctx, 2 * s + ci)
    if ph >= 6:
        pool_exchange(ctx)
    if ph >= 7:
        phase_c(ctx)
    if ph >= 8:
        phase_b(ctx)

    with nc.Block() as block:
        p.finalize(block, sems)
    es.close()
    assert int(nc.sbuf_base) <= MAP0, \
        f"lazy sbuf alloc moved base into manual map: {nc.sbuf_base} > {MAP0}"
    return nc


def phase0(ctx):
    p, g, Pm, Aa, PS = ctx["p"], ctx["g"], ctx["Pm"], ctx["Aa"], ctx["PS"]

    for (ci_, v_) in ((0, EPS), (1, 1.0 / D), (2, 1.0 / DIN), (3, -1.0)):
        p.op("vector", lambda e, ci_=ci_, v_=v_: e.memset(Pm["kconst"][:, ci_:ci_ + 1], v_),
             writes=["kconst"], tag="kconst")
    p.op("vector", lambda e: e.memset(Pm["srow32"][0:1, 1572:1588], 0.0),
         writes=["scpad0"], tag="scpad0")
    p.dma("scalar", mkap(g["cc_sc_in"].ap(), ctx["T"], [(8, 1), (1, 8)]),
          Pm["srow32"][0:1, 1572:1580], reads=["scpad0"], writes=["scpad"],
          tag="st_scpad")
    p.dma("scalar", mkap(g["cc_pool_in"].ap(), NSLOT * SUMD, [(16, 1), (1, 16)]),
          Pm["srow32"][0:1, 1572:1588], reads=["scpad0"], writes=["poolpad"],
          tag="st_poolpad")
    for dst, src in [("ident", "c_ident"), ("ident32", "c_ident32"),
                     ("onescol", "c_onescol"), ("onesrow32", "c_onesrow32"),
                     ("tril", "c_tril"), ("iota64", "c_iota64"),
                     ("strictup", "c_strictup")]:
        p.dma("sync", Pm[dst][:, :], g[src].ap(), writes=["c_" + dst], tag="ldc")

    for k in range(8):
        p.dma("gpsimd", Aa["w_inproj"][:, k * DINPROJ:(k + 1) * DINPROJ],
              g["in_proj_w"][ts(k), :], writes=["winproj"], tag="ld_inp")
    for k in range(16):
        p.dma("gpsimd", Aa["w_outproj"][:, k * D:(k + 1) * D],
              g["out_proj_w"][ts(k), :], writes=["woutproj_raw"], tag="ld_outp")
    for k in range(8):
        p.dma("gpsimd", Aa["w_sc1"][:, k * SC_HID:(k + 1) * SC_HID],
              g["scorer_w1"][ts(k), :], writes=["wsc1"], tag="ld_sc1")
        p.dma("gpsimd", Aa["w_summ"][:, k * SUMD:(k + 1) * SUMD],
              g["summ_w"][ts(k), :], writes=["wsumm"], tag="ld_summ")
        p.dma("gpsimd", Aa["w_q"][:, k * SUMD:(k + 1) * SUMD],
              g["q_w"][ts(k), :], writes=["wqw"], tag="ld_q")
    load_cols(ctx, Aa["w_s2"], g["scorer_w2"].ap(), SC_HID, "ws2")
    p.dma("gpsimd", Pm["kw"][:, :], g["k_w"][:, :], writes=["wkw"], tag="ld_kw")
    for k in range(18):
        p.dma("sync", Pm["convw"][:, k * DCONV:(k + 1) * DCONV],
              g["conv_w"][ts(k), :], writes=["convw"], tag="ld_convw")
    load_cols(ctx, Pm["convb"], g["conv_b"].ap(), CONV_DIM, "convb")
    load_cols(ctx, Pm["normw"], g["norm_w"].ap(), D, "normw")
    load_cols(ctx, Pm["ssmw"], g["ssm_norm_w"].ap(), DIN, "ssmw")
    p.dma("sync", Pm["dtb"][:, :], mkap(g["dt_bias"].ap(), 0, [(1, 32), (1, 1)]),
          writes=["dtb"], tag="ld_dtb")
    p.dma("sync", Pm["cols"][:32, 0:1], mkap(g["A_log"].ap(), 0, [(1, 32), (1, 1)]),
          writes=["alog"], tag="ld_alog")
    p.dma("sync", Pm["cols"][:32, 1:2], mkap(g["Dvec"].ap(), 0, [(1, 32), (1, 1)]),
          writes=["Dv"], tag="ld_D")
    p.op("tensor", lambda e: e.transpose(PS[7][0:1, 0:32], Pm["cols"][:32, 1:2],
                                         Pm["ident32"][0:32, 0:32]),
         reads=["Dv", "c_ident32"], writes=psq(7, 0), tag="tr_D")
    p.op("vector", lambda e: e.tensor_copy(Pm["srow32"][0:1, 1664:1696], PS[7][0:1, 0:32]),
         reads=psq(7, 0), writes=["Drow"], tag="ev_D")
    p.op("tensor", lambda e: e.matmul(PS[7][:, 0:32], Pm["onesrow32"][:, :],
                                      Pm["srow32"][0:1, 1664:1696], start=True, stop=True),
         reads=["Drow", "c_onesrow32"], writes=psq(7, 0), tag="mm_Drep")
    p.op("vector", lambda e: e.tensor_copy(Pm["Dcol"][:, :], PS[7][:, 0:32]),
         reads=psq(7, 0), writes=["Dcol"], tag="ev_Drep")

    p.op("scalar", lambda e: e.activation(Pm["negA"][:, :], Pm["cols"][:32, 0:1], ACT.Exp,
                                          bias=0.0, scale=1.0),
         reads=["alog"], writes=["negA_"], tag="negA_exp")
    p.op("vector", lambda e: e.tensor_scalar(Pm["negA"][:, :], Pm["negA"][:, :], -1.0,
                                             None, ALU.mult),
         reads=["negA_"], writes=["negA"], tag="negA_neg")

    for k in range(16):
        p.op("vector", lambda e, k=k: e.tensor_scalar(
                Aa["w_outproj"][:, k * D:(k + 1) * D], Aa["w_outproj"][:, k * D:(k + 1) * D],
                Pm["ssmw"][:, k:k + 1], None, ALU.mult),
             reads=["woutproj_raw", "ssmw"], writes=["woutproj"], tag="fold_ssm")

    load_cols(ctx, _ColsView(Pm["cols"], 8), g["cvec"].ap(), D, "ccol")
    p.op("scalar", lambda e: e.activation(Pm["cols"][:, 8:16], Pm["cols"][:, 8:16],
                                          ACT.Silu, bias=0.0, scale=1.0),
         reads=["ccol"], writes=["siluc"], tag="siluc")
    p.op("vector", lambda e: e.tensor_copy(Pm["colsB"][:, 8:16], Pm["cols"][:, 8:16]),
         reads=["siluc"], writes=["silucB"], tag="silucB")

    modrow = Aa["yw"]
    for blk in range(3):
        for k in range(8):
            p.dma("gpsimd", Aa["wmodbuf"][:, :],
                  g["wmod"][ts(k), blk * 2048:(blk + 1) * 2048],
                  writes=["wmodbuf"], tag="ld_wmod")
            for j in range(4):
                p.op("tensor", lambda e, k=k, j=j: e.matmul(
                        PS[4 + j][0:1, :], Pm["colsB"][:, 8 + k:9 + k],
                        Aa["wmodbuf"][:, j * 512:(j + 1) * 512],
                        start=(k == 0), stop=(k == 7)),
                     reads=["wmodbuf", "silucB"], writes=[f"modps{j}"] + psq(4 + j, 0, 4), tag="mm_mod")
        for j in range(4):
            p.op("vector", lambda e, j=j: e.tensor_copy(
                    modrow[0:1, j * 512:(j + 1) * 512], PS[4 + j][0:1, :]),
                 reads=[f"modps{j}"] + psq(4 + j, 0, 4), writes=["modrow"],
                 tag="ev_mod")
        p.dma("scalar", mkap(g["mod_hbm"].ap(), blk * 2048, [(2048, 1), (1, 2048)]),
              modrow[0:1, :], reads=["modrow"], writes=[f"modhbm{blk}"], tag="st_mod")
    load_cols(ctx, _ColsView(Pm["modc"], 0), g["mod_hbm"].ap(), 6 * D, "modc",
              extra_reads=["modhbm0", "modhbm1", "modhbm2"])
    p.op("vector", lambda e: e.tensor_scalar(Pm["modc"][:, 8:16], Pm["modc"][:, 8:16],
                                             1.0, None, ALU.add),
         reads=["modc"], writes=["modc"], tag="ca1")
    p.op("vector", lambda e: e.tensor_scalar(Pm["modc"][:, 32:40], Pm["modc"][:, 32:40],
                                             1.0, None, ALU.add),
         reads=["modc"], writes=["modc"], tag="cb1")
    for c0 in (16, 40):
        cc = Pm["cols"]
        p.op("vector", lambda e, c0=c0: e.tensor_scalar(cc[:, 0:8], Pm["modc"][:, c0:c0 + 8],
                                                        0.25, None, ALU.max),
             reads=["modc"], writes=["gcols"], tag="gc1")
        p.op("vector", lambda e, c0=c0: e.tensor_scalar(cc[:, 8:16], Pm["modc"][:, c0:c0 + 8],
                                                        0.0, None, ALU.is_ge),
             reads=["modc", "gcols"], writes=["gcols"], tag="gc2")
        p.op("vector", lambda e: e.tensor_tensor(cc[:, 0:8], cc[:, 0:8], cc[:, 8:16],
                                                 ALU.mult),
             reads=["gcols"], writes=["gcols"], tag="gc3")
        p.op("vector", lambda e, c0=c0: e.tensor_scalar(cc[:, 8:16], Pm["modc"][:, c0:c0 + 8],
                                                        -0.25, None, ALU.min),
             reads=["modc", "gcols"], writes=["gcols"], tag="gc4")
        p.op("vector", lambda e, c0=c0: e.scalar_tensor_tensor(
                cc[:, 8:16], Pm["modc"][:, c0:c0 + 8], 0.0, cc[:, 8:16],
                ALU.is_lt, ALU.mult),
             reads=["modc", "gcols"], writes=["gcols"], tag="gc5")
        p.op("vector", lambda e, c0=c0: e.tensor_tensor(
                Pm["modc"][:, c0:c0 + 8], cc[:, 0:8], cc[:, 8:16], ALU.add),
             reads=["gcols"], writes=["modc"], tag="gc6")
    p.op("vector", lambda e: e.tensor_tensor(Pm["modc"][:, 48:56], Pm["modc"][:, 16:24],
                                             Pm["modc"][:, 8:16], ALU.mult),
         reads=["modc"], writes=["modc"], tag="gc1col")
    p.op("vector", lambda e: e.tensor_tensor(Pm["modc"][:, 56:64], Pm["modc"][:, 16:24],
                                             Pm["modc"][:, 0:8], ALU.mult),
         reads=["modc"], writes=["modc"], tag="gscol")




def apx(t, off, dims):
    pitch = t.shape[1] if hasattr(t, "shape") else None
    return AP(t, int(off), [[pitch, 128]] + [[int(s), int(n)] for (s, n) in dims])


def apxp(t, off, nparts, dims):
    pitch = t.shape[1]
    return AP(t, int(off), [[pitch, nparts]] + [[int(s), int(n)] for (s, n) in dims])


def rep16(t, col_off, nb):
    return apx(t, col_off, [(1, nb), (0, 16)])


def psq(i, q0, nq=1):
    return [f"ps{i}q{q}" for q in range(q0, q0 + nq)]


def dump(ctx, name, src_ap, reads):
    if name in ctx["dbgh"]:
        ctx["p"].dma("scalar", ctx["dbgh"][name].ap(), src_ap, reads=reads,
                     tag="dbg_" + name)


def quant_chain(ctx, eng, src, dst, ncols, scr, reads, wtag, dst_extra=()):
    p = ctx["p"]
    nb = ncols // 16
    (sct, bmo, sco, rco) = scr["sc"]
    u = wtag
    p.op(eng, lambda e: e.tensor_reduce(sct[:, bmo:bmo + nb],
                                        src.rearrange("p (b k) -> p b k", k=16),
                                        AX.X, ALU.max, apply_absolute_value=True),
         reads=reads, writes=[u + "bm"], tag=u + "bm")
    p.op(eng, lambda e: e.tensor_scalar(sct[:, sco:sco + nb], sct[:, bmo:bmo + nb],
                                        1.0 / 6.0, 1e-12, ALU.mult, ALU.max),
         reads=[u + "bm"], writes=[u + "sc"], tag=u + "sc")
    p.op("vector", lambda e: e.reciprocal(sct[:, rco:rco + nb], sct[:, sco:sco + nb]),
         reads=[u + "sc"], writes=[u + "rc"], tag=u + "rc")
    p.op(eng, lambda e: e.tensor_tensor(scr["xn"], src, rep16(sct, rco, nb), ALU.mult),
         reads=reads + [u + "rc"], writes=[u + "xn"], tag=u + "xn")
    p.op(eng, lambda e: e.scalar_tensor_tensor(scr["a"], scr["xn"], -1.0, scr["xn"],
                                               ALU.mult, ALU.max),
         reads=[u + "xn"], writes=[u + "a"], tag=u + "a")
    p.op(eng, lambda e: e.tensor_scalar(scr["acc"], scr["a"], MIDS[0], None, ALU.is_gt),
         reads=[u + "a"], writes=[u + "acc"], tag=u + "s0")
    for m in MIDS[1:]:
        p.op(eng, lambda e, m=m: e.scalar_tensor_tensor(scr["acc"], scr["a"], m,
                                                        scr["acc"], ALU.is_gt, ALU.add),
             reads=[u + "a"], writes=[u + "acc"], tag=u + "s")
    for (m, w) in ((2.5, 2.0), (3.5, 2.0), (5.0, 4.0)):
        p.op(eng, lambda e, m=m, w=w: e.tensor_scalar(scr["hi"], scr["a"], m, w,
                                                      ALU.is_gt, ALU.mult),
             reads=[u + "a"], writes=[u + "hi"], tag=u + "h")
        p.op(eng, lambda e: e.tensor_tensor(scr["acc"], scr["acc"], scr["hi"], ALU.add),
             reads=[u + "acc", u + "hi"], writes=[u + "acc"], tag=u + "ha")
    p.op(eng, lambda e: e.tensor_scalar(scr["m"], scr["xn"], 0.0, None, ALU.is_lt),
         reads=[u + "xn"], writes=[u + "m"], tag=u + "m")
    p.op(eng, lambda e: e.scalar_tensor_tensor(scr["xn"], scr["acc"], 0.5,
                                               rep16(sct, sco, nb), ALU.mult, ALU.mult),
         reads=[u + "acc", u + "sc"], writes=[u + "qv", u + "xn"], tag=u + "qv")
    p.op("scalar", lambda e: e.activation(scr["hi"], scr["xn"], ACT.Identity,
                                          bias=0.0,
                                          scale=ctx["Pm"]["kconst"][:, 3:4]),
         reads=[u + "qv", "kconst"], writes=[u + "nq", u + "hi"], tag=u + "nq")
    p.op("scalar", lambda e: e.copy(dst, scr["xn"]),
         reads=[u + "qv"], writes=[wtag] + list(dst_extra), tag=u + "sel0")
    p.op(eng, lambda e: e.copy_predicated(dst, scr["m"], scr["hi"]),
         reads=[u + "nq", u + "m", wtag], writes=[wtag] + list(dst_extra),
         tag=u + "sel")



class _ColsView:
    def __init__(self, t, c0):
        self.t, self.c0 = t, c0

    def __getitem__(self, key):
        rows, cols = key
        c = slice(cols.start + self.c0, cols.stop + self.c0)
        return self.t[rows, c]


def load_cols(ctx, dst, src_flat_ap, n, bufw, after=(), extra_reads=()):
    p, Aa, Pm, PS = ctx["p"], ctx["Aa"], ctx["Pm"], ctx["PS"]
    nrow = (n + 1023) // 1024
    stage = Aa["csrep"]
    for r in range(nrow):
        w = min(1024, n - r * 1024)
        p.dma("sync", stage[0:1, 0:w],
              mkap(src_flat_ap, r * 1024, [(w, 1), (1, w)]),
              reads=list(extra_reads), writes=[bufw + "stg", "csrep"], after=after,
              tag=bufw + "stg")
        for cj in range(w // 128):
            j = r * 8 + cj
            q = j % 4
            p.op("tensor", lambda e, cj=cj, q=q: e.transpose(
                    PS[7][:, 256 + q * 32:256 + q * 32 + 1],
                    stage[0:1, cj * 128:(cj + 1) * 128], Pm["ident32"][0:1, 0:1]),
                 reads=[bufw + "stg", "csrep", "c_ident32"], writes=psq(7, 0, 4),
                 tag=bufw + "tr")
            p.op("vector", lambda e, j=j, q=q: e.tensor_copy(
                    dst[:, j:j + 1], PS[7][:, 256 + q * 32:256 + q * 32 + 1]),
                 reads=psq(7, 2 + q // 2), writes=[bufw], tag=bufw + "ev")


def wquant(ctx):
    p, g, Pm, Aa, PS = ctx["p"], ctx["g"], ctx["Pm"], ctx["Aa"], ctx["PS"]
    WQ = WQ_SHARD_ELEMS

    def one_tile(u, src_hbm_ap, ncols, dst_dma_fn, reads_extra):
        src = Aa["x2pT"][:, 0:ncols]
        p.dma("sync", src, src_hbm_ap, writes=[u + "src", "wq_src"], tag=u + "ld")
        if WQL < 2:
            return
        scr = dict(xn=Aa["csrep"][:, 0:ncols], a=Aa["yw"][:, 0:ncols],
                   acc=Aa["yw"][:, 1024:1024 + ncols],
                   hi=Aa["x2pT"][:, 1024:1024 + ncols],
                   sc=(Aa["xsD"], 0, 64, 128),
                   m=Aa["xst"].bitcast(I32)[:, 0:ncols])
        quant_chain(ctx, "vector", src, Aa["ycore"][:, 0:ncols], ncols, scr,
                    [u + "src", "wq_src"] + reads_extra, u, dst_extra=("wq_dst",))
        if WQL < 3:
            return
        ntile = ncols // 128
        for b0 in range(0, ntile, 4):
            nb = min(4, ntile - b0)
            for j in range(nb):
                i = b0 + j
                p.op("tensor", lambda e, i=i, j=j: e.transpose(
                        PS[0].bitcast(BF16)[:, j * 128:j * 128 + 128],
                        Aa["ycore"][:, i * 128:(i + 1) * 128], Pm["ident"][:, :]),
                     reads=[u, "wq_dst", "c_ident"], writes=[f"ps0q{j}", "wqbank"],
                     tag=u + "tr")
            if WQL >= 4:
                p.op("scalar", lambda e, b0=b0, nb=nb: e.copy(
                        Aa["ztil"][:, b0 * 128:(b0 + nb) * 128],
                        PS[0].bitcast(BF16)[:, 0:nb * 128]),
                     reads=[f"ps0q{j}" for j in range(nb)] + ["wqbank"],
                     writes=[u + "tT"] + ["wq_ztil" + str(b0 + j) for j in range(nb)],
                     tag=u + "ev")
            if WQL >= 5:
                for j in range(nb):
                    dst_dma_fn(b0 + j)

    for (mat, base) in (("wg", 0), ("wu", 384 * D)):
        for r in range(3):
            u = f"q{mat}{r}"

            def dst_fn(i, r=r, base=base, u=u):
                p.dma("scalar",
                      mkap(g["cc_wq_in"].ap(), base + i * 128 * 384 + r * 128,
                           [(384, 128), (1, 128)]),
                      Aa["ztil"][:, i * 128:(i + 1) * 128],
                      reads=[u + "tT", "wq_ztil" + str(i)], writes=["wqin"],
                      tag=u + "st")

            src_hbm = g[mat[0:2] + "_shard" if False else ("wg_shard" if mat == "wg" else "wu_shard")][ts(r), :]
            one_tile(u, src_hbm, D, dst_fn, [])
    wdbase = 2 * 384 * D
    for (si, (c0, w)) in enumerate(((0, 1024), (1024, 1024), (2048, 768))):
        u = f"qwd{si}"

        def dst_fn(i, c0=c0, u=u):
            icc = (c0 // 128) + i
            p.dma("scalar",
                  mkap(g["cc_wq_in"].ap(), wdbase + icc * 128 * 128, [(128, 128), (1, 128)]),
                  Aa["ztil"][:, i * 128:(i + 1) * 128],
                  reads=[u + "tT", "wq_ztil" + str(i)], writes=["wqin"],
                  tag=u + "st")

        one_tile(u, g["wd_shard"][:, c0:c0 + w], w, dst_fn, [])

    _last = p.ops[-1]
    for r in ("csrep", "yw", "x2pT", "xsD", "xst", "ycore", "ztil"):
        p.buf(r).readers.append(_last)
    p.coll(lambda e: e.collective_compute(
            "AllGather", ALU.bypass, replica_groups=[list(range(NCORES))],
            ins=[g["cc_wq_in"].ap().opt()], outs=[g["cc_wq_out"].ap().opt()]),
           reads=["wqin"], writes=["wqout"], tag="ag_wq")


def frontend_sup(ctx, s):
    p, g, Pm, Aa, PS = ctx["p"], ctx["g"], ctx["Pm"], ctx["Aa"], ctx["PS"]
    T = ctx["T"]
    if s == 0:
        p.op("vector", lambda e: e.memset(Pm["halo"][:, :], 0.0), writes=["halo_all"],
             tag="halo0")
        p.op("vector", lambda e: e.memset(Pm["S"][:, :], 0.0), writes=["S"], tag="S0")
        for cc in range(18):
            p.buf(f"halo{cc}").writer = p.bufs["halo_all"].writer

    C = Pm["cols"]
    for tt in range(2):
        j = 2 * s + tt
        xb = "xtok"
        p.dma("sync", Aa["x_tok"][:, :], g["x"][ts(j), :], writes=[xb, "u" + xb],
              tag="ld_x")
        p.op("scalar", lambda e: e.activation(Aa["x_tok"][:, :], Aa["x_tok"][:, :],
                                              ACT.Identity, bias=0.0, scale=1.0,
                                              accum_out=C[:, 0:1]),
             reads=[xb], writes=[xb, "lnsum"], tag="ln_s1")
        p.op("scalar", lambda e, tt=tt: e.activation(
                Aa["ztil"][:, tt * 1024:(tt + 1) * 1024], Aa["x_tok"][:, :],
                ACT.Square, bias=0.0, scale=1.0, accum_out=C[:, 1:2]),
             reads=[xb], writes=["lnsq", "ztil"], tag="ln_s2")
        p.op("vector", lambda e: e.tensor_scalar(C[:, 2:3], C[:, 0:1], 1.0 / D, None,
                                                 ALU.mult),
             reads=["lnsum"], writes=["lnm"], tag="ln_m")
        p.op("vector", lambda e: e.tensor_tensor(C[:, 3:4], C[:, 2:3], C[:, 2:3],
                                                 ALU.mult),
             reads=["lnm"], writes=["lnm2"], tag="ln_m2")
        p.op("vector", lambda e: e.scalar_tensor_tensor(C[:, 4:5], C[:, 1:2], 1.0 / D,
                                                        C[:, 3:4], ALU.mult, ALU.subtract),
             reads=["lnsq", "lnm2"], writes=["lnvar"], tag="ln_var")
        p.op("scalar", lambda e: e.activation(C[:, 5:6], C[:, 4:5], ACT.Sqrt,
                                              bias=Pm["kconst"][:, 0:1], scale=1.0),
             reads=["lnvar", "kconst"], writes=["lnsd"], tag="ln_sd")
        p.op("vector", lambda e: e.reciprocal(C[:, 6:7], C[:, 5:6]),
             reads=["lnsd"], writes=["lnrstd"], tag="ln_rstd")
        if FRL == 11:
            continue
        for k in range(8):
            bk, qt = 4 + k // 4, k % 4
            p.op("tensor", lambda e, k=k, bk=bk, qt=qt: e.transpose(
                    PS[bk][:, qt * 128:(qt + 1) * 128],
                    Aa["x_tok"][:, k * 128:(k + 1) * 128], Pm["ident32"][:, :]),
                 reads=[xb, "c_ident32"], writes=psq(bk, 0, 4), tag="tr_x")
        if FRL == 12:
            continue
        p.op("vector", lambda e: e.tensor_scalar(Aa["x_tok"][:, :], Aa["x_tok"][:, :],
                                                 C[:, 2:3], C[:, 6:7],
                                                 ALU.subtract, ALU.mult),
             reads=[xb, "lnm", "lnrstd"], writes=["u" + xb, xb], tag="ln_u")
        for k in range(8):
            bk, qt = 6 + k // 4, k % 4
            p.op("tensor", lambda e, k=k, bk=bk, qt=qt: e.transpose(
                    PS[bk][:, qt * 128:(qt + 1) * 128],
                    Aa["x_tok"][:, k * 128:(k + 1) * 128], Pm["ident32"][:, :]),
                 reads=["u" + xb, "c_ident32"], writes=psq(bk, 0, 4), tag="tr_u")
        if FRL == 13:
            continue
        for k in range(8):
            bku, qtu = 6 + k // 4, k % 4
            bkx, qtx = 4 + k // 4, k % 4
            hsl = Aa["hT"][:, k * SUP + tt * 128: k * SUP + tt * 128 + 128]
            x2sl = Aa["x2pT"][:, k * SUP + tt * 128: k * SUP + tt * 128 + 128]
            p.op("scalar", lambda e, k=k, bku=bku, qtu=qtu, hsl=hsl: e.activation(
                    hsl, PS[bku][:, qtu * 128:(qtu + 1) * 128], ACT.Identity,
                    bias=Pm["modc"][:, k:k + 1], scale=Pm["modc"][:, 8 + k:9 + k]),
                 reads=psq(bku, qtu) + ["modc"], writes=["hT"], tag="h_ev")
            if FRL == 14:
                continue
            p.op("scalar", lambda e, k=k, tt=tt, bku=bku, qtu=qtu: e.activation(
                    Aa["ztil"][:, 2048 + k * SUP + tt * 128: 2048 + k * SUP + tt * 128 + 128],
                    PS[bku][:, qtu * 128:(qtu + 1) * 128], ACT.Square,
                    bias=Pm["modc"][:, k:k + 1], scale=Pm["modc"][:, 8 + k:9 + k]),
                 reads=psq(bku, qtu) + ["modc"], writes=["ztil"], tag="h_sq")
            if FRL == 15:
                continue
            p.op("scalar", lambda e, k=k, bkx=bkx, qtx=qtx, x2sl=x2sl: e.copy(
                    x2sl, PS[bkx][:, qtx * 128:(qtx + 1) * 128]),
                 reads=psq(bkx, qtx), writes=["x2pT"], tag="x2p0")
            if FRL == 16:
                continue
            p.op("vector", lambda e, k=k, hsl=hsl, x2sl=x2sl:
                 e.scalar_tensor_tensor(
                    x2sl, hsl, Pm["modc"][:, 16 + k:17 + k], x2sl,
                    ALU.mult, ALU.add),
                 reads=["hT", "modc", "x2pT"], writes=["x2pT"], tag="x2p1")
    if FRL < 2:
        return
    for k in range(8):
        p.op("tensor", lambda e, k=k: e.matmul(
                PS[0][0:1, 0:256], Pm["onescol"][:, :],
                Aa["ztil"][:, 2048 + k * SUP: 2048 + (k + 1) * SUP],
                start=(k == 0), stop=(k == 7)),
             reads=["ztil", "c_onescol"], writes=psq(0, 0, 4), tag="mm_rms")
    p.op("scalar", lambda e: e.activation(Pm["srow32"][0:1, 640:896], PS[0][0:1, 0:256],
                                          ACT.Sqrt, bias=Pm["kconst"][0:1, 0:1],
                                          scale=Pm["kconst"][0:1, 1:2]),
         reads=psq(0, 0, 2) + ["kconst"], writes=["rmsd"], tag="rms_sd")
    p.op("vector", lambda e: e.reciprocal(Pm["srow32"][0:1, 896:1152],
                                          Pm["srow32"][0:1, 640:896]),
         reads=["rmsd"], writes=["rmsr"], tag="rms_r")
    p.op("tensor", lambda e: e.matmul(PS[1][:, 0:256], Pm["onesrow32"][:, :],
                                      Pm["srow32"][0:1, 896:1152], start=True, stop=True),
         reads=["rmsr", "c_onesrow32"], writes=psq(1, 0, 2), tag="mm_rrep")
    for k in range(8):
        p.op("vector", lambda e, k=k: e.tensor_tensor(
                Aa["hT"][:, k * SUP:(k + 1) * SUP], Aa["hT"][:, k * SUP:(k + 1) * SUP],
                PS[1][:, 0:256], ALU.mult),
             reads=["hT"] + psq(1, 0, 2), writes=["hT"], tag="xn1")
        p.op("vector", lambda e, k=k: e.tensor_scalar(
                Aa["hT"][:, k * SUP:(k + 1) * SUP], Aa["hT"][:, k * SUP:(k + 1) * SUP],
                Pm["normw"][:, k:k + 1], None, ALU.mult),
             reads=["hT", "normw"], writes=["hT"], tag="xn2")
    if s == 0:
        dump(ctx, "hT", Aa["hT"][:, :], ["hT"])
        dump(ctx, "xnT", Aa["hT"][:, :], ["hT"])

    if FRL < 3:
        return
    for oc in range(35):
        bk = 2 + oc % 6
        wdt = 128 if oc < 34 else 32
        pslice = PS[bk][0:wdt, 0:256]
        pb = psq(bk, 0, 4)
        for k in range(8):
            p.op("tensor", lambda e, oc=oc, k=k, pslice=pslice, wdt=wdt: e.matmul(
                    pslice, Aa["w_inproj"][:, k * DINPROJ + oc * 128:
                                           k * DINPROJ + oc * 128 + wdt],
                    Aa["hT"][:, k * SUP:(k + 1) * SUP],
                    start=(k == 0), stop=(k == 7)),
                 reads=["hT", "winproj"], writes=pb, tag=f"mm_ip{oc}")
        if FRL < 4:
            continue
        if oc < 16:
            p.op("scalar", lambda e, oc=oc, pslice=pslice: e.activation(
                    Aa["ztil"][:, oc * SUP:(oc + 1) * SUP], pslice, ACT.Silu,
                    bias=0.0, scale=1.0),
                 reads=pb, writes=["ztil"], tag="z_silu")
        elif oc < 34:
            cc = oc - 16
            acc = Aa["acc0"]
            accb = "acc0"
            wv = Pm["convw"]
            p.op("vector", lambda e, cc=cc, acc=acc, pslice=pslice: e.tensor_scalar(
                    acc[:, 0:SUP], pslice, wv[:, cc * 4 + 3:cc * 4 + 4],
                    Pm["convb"][:, cc:cc + 1], ALU.mult, ALU.add),
                 reads=pb + ["convw", "convb"], writes=[accb], tag="cv3")
            for kk in range(3):
                dsh = 3 - kk
                p.op("vector", lambda e, cc=cc, kk=kk, dsh=dsh, acc=acc, pslice=pslice:
                     e.scalar_tensor_tensor(
                        acc[:, dsh:SUP], pslice[:, 0:SUP - dsh],
                        wv[:, cc * 4 + kk:cc * 4 + kk + 1], acc[:, dsh:SUP],
                        ALU.mult, ALU.add),
                     reads=pb + ["convw", accb], writes=[accb], tag=f"cv{kk}")
                p.op("vector", lambda e, cc=cc, kk=kk, dsh=dsh, acc=acc:
                     e.scalar_tensor_tensor(
                        acc[:, 0:dsh], Pm["halo"][:, cc * 3 + kk:cc * 3 + kk + dsh],
                        wv[:, cc * 4 + kk:cc * 4 + kk + 1], acc[:, 0:dsh],
                        ALU.mult, ALU.add),
                     reads=["convw", f"halo{cc}", accb], writes=[accb], tag=f"cvh{kk}")
            p.op("vector", lambda e, cc=cc, pslice=pslice: e.tensor_copy(
                    Pm["halo"][:, cc * 3:cc * 3 + 3], pslice[:, SUP - 3:SUP]),
                 reads=pb, writes=[f"halo{cc}"], tag="cv_halo")
            if cc < 16:
                dst = Aa["xst"][:, cc * SUP:(cc + 1) * SUP]
                wb = "xst"
            else:
                dst = Aa["bct"][:, (cc - 16) * SUP:(cc - 15) * SUP]
                wb = "bct"
            p.op("scalar", lambda e, acc=acc, dst=dst: e.activation(
                    dst, acc[:, 0:SUP], ACT.Silu, bias=0.0, scale=1.0),
                 reads=[accb], writes=[wb], tag="cv_silu")
        else:
            p.op("scalar", lambda e, pslice=pslice: e.activation(
                    Pm["srow32"][:, 0:SUP], pslice, ACT.Exp,
                    bias=Pm["dtb"][:, :], scale=1.0),
                 reads=pb + ["dtb"], writes=["dtexp"], tag="dt_exp")
            p.op("scalar", lambda e: e.activation(Aa["dsup"][:, :], Pm["srow32"][:, 0:SUP],
                                                  ACT.Ln, bias=1.0, scale=1.0),
                 reads=["dtexp"], writes=["dsup"], tag="dt_ln")
    if FRL < 5:
        return
    for ci in range(2):
        c = 2 * s + ci
        b0 = ci * L
        p.op("vector", lambda e, b0=b0: e.tensor_scalar(
                Pm["srow32"][:, 256:256 + L], Aa["dsup"][:, b0:b0 + L],
                Pm["negA"][:, :], None, ALU.mult),
             reads=["dsup", "negA"], writes=["dA"], tag="dA")
        p.op("vector", lambda e, b0=b0: e.tensor_tensor_scan(
                Aa["cssup"][:, b0:b0 + L], Pm["srow32"][:, 256:256 + L],
                Pm["srow32"][:, 256:256 + L], 0.0, ALU.add, ALU.bypass),
             reads=["dA"], writes=["cssup"], tag="cs")
        p.op("scalar", lambda e, b0=b0: e.activation(
                Aa["essup"][:, b0:b0 + L], Aa["cssup"][:, b0:b0 + L], ACT.Exp,
                bias=0.0, scale=1.0),
             reads=["cssup"], writes=["essup"], tag="ecs")
        p.op("vector", lambda e, b0=b0: e.tensor_copy(
                Pm["srow32"][:, 384:385], Aa["cssup"][:, b0 + L - 1:b0 + L]),
             reads=["cssup"], writes=["ncl"], tag="ncl")
        p.op("scalar", lambda e, b0=b0: e.activation(
                Aa["dssup"][:, b0:b0 + L], Aa["cssup"][:, b0:b0 + L], ACT.Exp,
                bias=Pm["srow32"][:, 384:385], scale=Pm["kconst"][0:32, 3:4]),
             reads=["cssup", "ncl", "kconst"], writes=["dssup"], tag="ds")
        p.op("scalar", lambda e, b0=b0: e.activation(
                Pm["srow32"][:, 385:386], Aa["cssup"][:, b0 + L - 1:b0 + L], ACT.Exp,
                bias=0.0, scale=1.0),
             reads=["cssup"], writes=["dcc"], tag="dc")
        p.op("vector", lambda e, b0=b0: e.tensor_tensor(
                Pm["srow32"][:, 512:512 + L], Aa["dsup"][:, b0:b0 + L],
                Aa["dssup"][:, b0:b0 + L], ALU.mult),
             reads=["dsup", "dssup"], writes=["ddt"], tag="ddt")
        p.dma("scalar", mkap(g["hbm_rows"].ap(), c * 8192, [(128, 32), (1, 128)]),
              Aa["cssup"][:, b0:b0 + L], reads=["cssup"], writes=[f"cshbm{c}"], tag="st_cs")
        p.dma("scalar", mkap(g["hbm_rows"].ap(), c * 8192 + 4096, [(128, 32), (1, 128)]),
              Aa["essup"][:, b0:b0 + L], reads=["essup"], writes=[f"eshbm{c}"], tag="st_es")
    if s == 0:
        dump(ctx, "xst", Aa["xst"][:, :], ["xst"])
        dump(ctx, "bct", Aa["bct"][:, :], ["bct"])
        dump(ctx, "dsup", Aa["dsup"][:, :], ["dsup"])
        dump(ctx, "cssup", Aa["cssup"][:, :], ["cssup"])


def ssd_chunk(ctx, c):
    p, g, Pm, Aa, PS = ctx["p"], ctx["g"], ctx["Pm"], ctx["Aa"], ctx["PS"]
    s, ci = c // 2, c % 2
    b0 = ci * L

    srcs = [(Aa["cssup"][:, b0:b0 + L], "cssup", Aa["cstok"], "cstok"),
            (Aa["dsup"][:, b0:b0 + L], "dsup", Aa["dttok"], "dttok"),
            (Aa["dssup"][:, b0:b0 + L], "dssup", Aa["dstok"], "dstok"),
            (Pm["srow32"][:, 512 + ci * L:512 + ci * L + L] if False else
             Pm["srow32"][:, 512:512 + L], "ddt", Aa["ddtok"], "ddtok")]
    p.op("vector", lambda e: e.tensor_tensor(
            Pm["srow32"][:, 512:512 + L], Aa["dsup"][:, b0:b0 + L],
            Aa["dssup"][:, b0:b0 + L], ALU.mult),
         reads=["dsup", "dssup"], writes=["ddt"], tag="ddt")
    for (i, (srcap, rb, dstt, db)) in enumerate(srcs):
        p.op("tensor", lambda e, srcap=srcap, i=i: e.transpose(
                PS[7][:, i * 128:i * 128 + 32], srcap, Pm["ident32"][0:32, 0:32]),
             reads=[rb, "c_ident32"], writes=psq(7, 0, 4), tag="tr_row")
        p.op("vector", lambda e, dstt=dstt, i=i: e.tensor_copy(
                dstt[:, :], PS[7][:, i * 128:i * 128 + 32]),
             reads=psq(7, i), writes=[db], tag="ev_row")
    p.op("scalar", lambda e: e.activation(
            Pm["srow32"][:, 385:386], Aa["cssup"][:, b0 + L - 1:b0 + L], ACT.Exp,
            bias=0.0, scale=1.0),
         reads=["cssup"], writes=["dcc"], tag="dc")
    p.op("tensor", lambda e: e.transpose(PS[7][0:1, 448:480], Pm["srow32"][:, 385:386],
                                         Pm["ident32"][0:32, 0:32]),
         reads=["dcc", "c_ident32"], writes=psq(7, 0, 4), tag="tr_dc")
    p.op("vector", lambda e: e.tensor_copy(Pm["srow32"][0:1, 1408:1440],
                                           PS[7][0:1, 448:480]),
         reads=psq(7, 3), writes=["dcrow"], tag="ev_dc")
    p.op("tensor", lambda e: e.matmul(PS[7][:, 448:480], Pm["onesrow32"][:, :],
                                      Pm["srow32"][0:1, 1408:1440], start=True, stop=True),
         reads=["dcrow", "c_onesrow32"], writes=psq(7, 0, 4), tag="mm_dcrep")
    p.op("vector", lambda e: e.tensor_copy(Pm["dcrep"][:, :], PS[7][:, 448:480]),
         reads=psq(7, 3), writes=["dcrep"], tag="ev_dcrep")

    p.op("tensor", lambda e: e.matmul(PS[0][:, 0:128],
                                      Aa["bct"][:, ci * L:ci * L + L],
                                      Aa["bct"][:, SUP + ci * L:SUP + ci * L + L],
                                      start=True, stop=True),
         reads=["bct"], writes=psq(0, 0, 2), tag="mm_G")
    p.op("vector", lambda e: e.tensor_tensor(Aa["Gt"][:, :], PS[0][:, 0:128],
                                             Pm["tril"][:, :], ALU.mult),
         reads=psq(0, 0) + ["c_tril"], writes=["Gt"], tag="ev_G")
    p.op("tensor", lambda e: e.transpose(PS[0].bitcast(BF16)[:, 256:384],
                                         Aa["bct"][:, ci * L:ci * L + L],
                                         Pm["ident"][:, :]),
         reads=["bct", "c_ident"], writes=psq(0, 0, 2), tag="tr_btok")
    p.op("vector", lambda e: e.tensor_copy(Aa["btok"][:, :],
                                           PS[0].bitcast(BF16)[:, 256:384]),
         reads=psq(0, 1), writes=["btok"], tag="ev_btok")
    for w in range(4):
        for i in range(4):
            kk = 4 * w + i
            p.op("tensor", lambda e, kk=kk, i=i: e.transpose(
                    PS[1].bitcast(BF16)[:, i * 128:i * 128 + 128],
                    Aa["xst"][:, kk * SUP + ci * L: kk * SUP + ci * L + L],
                    Pm["ident"][:, :]),
                 reads=["xst", "c_ident"], writes=psq(1, i // 2), tag="tr_xs")
        p.op("vector", lambda e, w=w: e.tensor_tensor(
                Aa["xd"][:, w * 512:(w + 1) * 512],
                PS[1].bitcast(BF16)[:, 0:512],
                apx(Aa["dttok"], 8 * w, [(1, 8), (0, 64)]), ALU.mult),
             reads=psq(1, 0, 2) + ["dttok"], writes=["xd"], tag="ev_xd")
        p.op("vector", lambda e, w=w: e.tensor_tensor(
                Aa["xdp"][:, w * 512:(w + 1) * 512], Aa["xd"][:, w * 512:(w + 1) * 512],
                apx(Aa["dstok"], 8 * w, [(1, 8), (0, 64)]), ALU.mult),
             reads=["xd", "dstok"], writes=["xdp"], tag="ev_xdp")

    for qh in range(4):
        p.dma("sync", Aa["csrep"][:, :],
              mkap(g["hbm_rows"].ap(), c * 8192 + qh * 1024,
                   [(0, 128), (128, 8), (1, 128)]),
              reads=[f"cshbm{c}"], writes=["csrep"], tag="bc_cs")
        for eh in range(2):
            p.dma("sync", Aa["erep2"][eh * 64:(eh + 1) * 64, :],
                  mkap(g["hbm_rows"].ap(), c * 8192 + 4096 + qh * 1024 + eh * 128,
                       [(0, 64), (256, 4), (1, 128)]),
                  reads=[f"eshbm{c}"], writes=["erep2"], tag="bc_e")
        p.op("vector", lambda e, qh=qh: e.tensor_tensor(
                Aa["csrep"][:, :], Aa["csrep"][:, :],
                apx(Aa["cstok"], qh * 8, [(1, 8), (0, 128)]), ALU.subtract),
             reads=["csrep", "cstok"], writes=["csrep"], tag="seg")
        p.op("vector", lambda e: e.tensor_scalar(Aa["csrep"][:, :], Aa["csrep"][:, :],
                                                 0.0, None, ALU.min),
             reads=["csrep"], writes=["csrep"], tag="seg_cl")
        p.op("scalar", lambda e: e.activation(Aa["csrep"][:, :], Aa["csrep"][:, :],
                                              ACT.Exp, bias=0.0, scale=1.0),
             reads=["csrep"], writes=["csrep"], tag="seg_exp")
        p.op("vector", lambda e: e.tensor_tensor(
                Aa["Mt"][:, :], Aa["csrep"][:, :],
                apx(Aa["Gt"], 0, [(0, 8), (1, 128)]), ALU.mult),
             reads=["csrep", "Gt"], writes=["Mt"], tag="M")
        for i in range(4):
            cc = qh * 4 + i
            for half in range(2):
                pr = slice(half * 64, half * 64 + 64)
                p.op("vector", lambda e, cc=cc, i=i, pr=pr, half=half: e.tensor_scalar(
                        Aa["xsD"][pr, i * 128:(i + 1) * 128],
                        Aa["xst"][pr, cc * SUP + ci * L: cc * SUP + ci * L + L],
                        Pm["Dcol"][pr, 2 * cc + half:2 * cc + half + 1], None, ALU.mult),
                     reads=["xst", "Dcol"], writes=["xsD"], tag="xsD")
        for j in range(4):
            jj = qh * 4 + j
            for half in range(2):
                h = 2 * jj + half
                if c > 0:
                    p.op("tensor", lambda e, h=h, j=j, half=half: e.matmul(
                            PS[2][half * 64:half * 64 + 64, j * 128:(j + 1) * 128],
                            Aa["Sb"][:, h * 64:(h + 1) * 64],
                            Aa["bct"][:, SUP + ci * L:SUP + ci * L + L],
                            start=True, stop=True, tile_position=(0, half * 64)),
                         reads=["Sb", "bct"], writes=psq(2, j), tag="mm_yo")
                p.op("tensor", lambda e, h=h, j=j, half=half: e.matmul(
                        PS[4][half * 64:half * 64 + 64, j * 128:(j + 1) * 128],
                        Aa["xd"][:, h * 64:(h + 1) * 64],
                        Aa["Mt"][:, j * 256 + half * 128: j * 256 + half * 128 + 128],
                        start=True, stop=True, tile_position=(0, half * 64)),
                     reads=["xd", "Mt"], writes=psq(4, j), tag="mm_yd")
        ysl = Aa["yw"][:, qh * 512:(qh + 1) * 512]
        if c > 0:
            p.op("vector", lambda e, ysl=ysl: e.tensor_tensor(
                    ysl, PS[2][:, :], Aa["erep2"][:, :], ALU.mult),
                 reads=psq(2, 0, 4) + ["erep2"], writes=["yw"], tag="as_yo")
            p.op("vector", lambda e, ysl=ysl: e.tensor_tensor(
                    ysl, ysl, PS[4][:, :], ALU.add),
                 reads=psq(4, 0, 4) + ["yw"], writes=["yw"], tag="as_yd")
        else:
            p.op("vector", lambda e, ysl=ysl: e.tensor_copy(ysl, PS[4][:, :]),
                 reads=psq(4, 0, 4), writes=["yw"], tag="as_yd0")
        p.op("vector", lambda e, ysl=ysl: e.tensor_tensor(
                ysl, ysl, Aa["xsD"][:, :], ALU.add),
             reads=["xsD", "yw"], writes=["yw"], tag="as_xsD")
        p.op("vector", lambda e, ysl=ysl, qh=qh: e.tensor_tensor(
                ysl, ysl,
                apx(Aa["ztil"], (qh * 4) * SUP + ci * L, [(SUP, 4), (1, 128)]),
                ALU.mult),
             reads=["ztil", "yw"], writes=["yw"], tag="as_z")
    p.op("scalar", lambda e: e.activation(Aa["hT"][:, :], Aa["yw"][:, :], ACT.Square,
                                          bias=0.0, scale=1.0),
         reads=["yw"], writes=["hT"], tag="ysq")
    for i in range(16):
        p.op("tensor", lambda e, i=i: e.matmul(
                PS[6][0:1, 0:128], Pm["onescol"][:, :],
                Aa["hT"][:, i * 128:(i + 1) * 128],
                start=(i == 0), stop=(i == 15)),
             reads=["hT", "c_onescol"], writes=psq(6, 0, 4), tag="mm_ysq")
    p.op("scalar", lambda e: e.activation(Pm["srow32"][0:1, 1152:1280], PS[6][0:1, 0:128],
                                          ACT.Sqrt, bias=Pm["kconst"][0:1, 0:1],
                                          scale=Pm["kconst"][0:1, 2:3]),
         reads=psq(6, 0) + ["kconst"], writes=["yrms"], tag="yrms_sd")
    p.op("vector", lambda e: e.reciprocal(Pm["srow32"][0:1, 1280:1408],
                                          Pm["srow32"][0:1, 1152:1280]),
         reads=["yrms"], writes=["yrmsr"], tag="yrms_r")
    p.op("tensor", lambda e: e.matmul(PS[6][:, 128:256], Pm["onesrow32"][:, :],
                                      Pm["srow32"][0:1, 1280:1408], start=True, stop=True),
         reads=["yrmsr", "c_onesrow32"], writes=psq(6, 1), tag="mm_yrrep")
    p.op("vector", lambda e: e.tensor_copy(Aa["rstd_rep"][:, :], PS[6][:, 128:256]),
         reads=psq(6, 1), writes=["rstd_rep"], tag="ev_yrrep")
    p.op("vector", lambda e: e.tensor_tensor(
            Aa["ycore"][:, :], Aa["yw"][:, :],
            apx(Aa["rstd_rep"], 0, [(0, 16), (1, 128)]), ALU.mult),
         reads=["yw", "rstd_rep"], writes=["ycore"], tag="ycore")
    for w in range(4):
        bk = 2 + w
        p.op("tensor", lambda e, w=w, bk=bk: e.matmul(
                PS[bk][:, :], Aa["btok"][:, :], Aa["xdp"][:, w * 512:(w + 1) * 512],
                start=True, stop=True),
             reads=["btok", "xdp"], writes=psq(bk, 0, 4), tag="mm_st")
        p.op("vector", lambda e, w=w: e.tensor_tensor(
                Pm["S"][:, w * 512:(w + 1) * 512], Pm["S"][:, w * 512:(w + 1) * 512],
                apx(Pm["dcrep"], 8 * w, [(1, 8), (0, 64)]), ALU.mult),
             reads=["dcrep", "S"], writes=["S"], tag="S_dec")
        p.op("vector", lambda e, w=w, bk=bk: e.tensor_tensor(
                Pm["S"][:, w * 512:(w + 1) * 512], Pm["S"][:, w * 512:(w + 1) * 512],
                PS[bk][:, :], ALU.add),
             reads=psq(bk, 0, 4) + ["S"], writes=["S"], tag="S_add")
        p.op("scalar", lambda e, w=w: e.copy(
                Aa["Sb"][:, w * 512:(w + 1) * 512], Pm["S"][:, w * 512:(w + 1) * 512]),
             reads=["S"], writes=["Sb"], tag="S_cast")
    if c == 0:
        dump(ctx, "ycore0", Aa["ycore"][:, :], ["ycore"])
        dump(ctx, "yw0", Aa["yw"][:, :], ["yw"])
        dump(ctx, "xd0", Aa["xd"][:, :], ["xd"])
        dump(ctx, "Mt0", Aa["Mt"][:, :], ["Mt"])
    if c == 1:
        dump(ctx, "S1", Pm["S"][:, :], ["S"])
        dump(ctx, "ycore1", Aa["ycore"][:, :], ["ycore"])


def chunk_epilogue(ctx, c):
    p, g, Pm, Aa, PS = ctx["p"], ctx["g"], ctx["Pm"], ctx["Aa"], ctx["PS"]
    T = ctx["T"]
    s, ci = c // 2, c % 2
    for kd in range(8):
        bk = 6 + kd % 2
        for kh in range(16):
            p.op("tensor", lambda e, kd=kd, kh=kh, bk=bk: e.matmul(
                    PS[bk][:, 0:128],
                    Aa["w_outproj"][:, kh * D + kd * 128: kh * D + kd * 128 + 128],
                    Aa["ycore"][:, kh * 128:(kh + 1) * 128],
                    start=(kh == 0), stop=(kh == 15)),
                 reads=["ycore", "woutproj"], writes=psq(bk, 0, 4), tag="mm_op")
        p.op("scalar", lambda e, kd=kd, bk=bk: e.copy(
                Aa["yTb"][:, kd * 128:(kd + 1) * 128], PS[bk][:, 0:128]),
             reads=psq(bk, 0), writes=["yTb"], tag="ev_y")
        p.op("vector", lambda e, kd=kd, ci=ci: e.scalar_tensor_tensor(
                Aa["x2pT"][:, kd * SUP + ci * L: kd * SUP + ci * L + L],
                Aa["yTb"][:, kd * 128:(kd + 1) * 128], Pm["modc"][:, 16 + kd:17 + kd],
                Aa["x2pT"][:, kd * SUP + ci * L: kd * SUP + ci * L + L],
                ALU.mult, ALU.add),
             reads=["yTb", "modc", "x2pT"], writes=["x2pT"], tag="x2p2")
    for hc in range(2):
        for kd in range(8):
            p.op("tensor", lambda e, hc=hc, kd=kd: e.matmul(
                    PS[hc][:, 0:128],
                    Aa["w_sc1"][:, kd * SC_HID + hc * 128: kd * SC_HID + hc * 128 + 128],
                    Aa["yTb"][:, kd * 128:(kd + 1) * 128],
                    start=(kd == 0), stop=(kd == 7)),
                 reads=["yTb", "wsc1"], writes=psq(hc, 0, 4), tag="mm_s1")
        p.op("scalar", lambda e, hc=hc: e.activation(
                Aa["r1"][:, hc * 128:(hc + 1) * 128], PS[hc][:, 0:128],
                ACT.Relu, bias=0.0, scale=1.0),
             reads=psq(hc, 0), writes=["r1"], tag="ev_s1")
    for hc in range(2):
        p.op("tensor", lambda e, hc=hc: e.matmul(
                PS[0][0:1, 256:384], Aa["w_s2"][:, hc:hc + 1],
                Aa["r1"][:, hc * 128:(hc + 1) * 128],
                start=(hc == 0), stop=(hc == 1)),
             reads=["r1", "ws2"], writes=psq(0, 0, 4), tag="mm_s2")
    p.op("scalar", lambda e: e.activation(
            Pm["srow32"][0:1, 1440:1568], PS[0][0:1, 256:384],
            ACT.Sigmoid, bias=0.0, scale=1.0),
         reads=psq(0, 2), writes=["scrow"], tag="ev_s2")
    p.dma("scalar", mkap(g["cc_sc_in"].ap(), c * 128, [(128, 1), (1, 128)]),
          Pm["srow32"][0:1, 1440:1568], reads=["scrow"], writes=["scin"], tag="st_sc")
    for kd in range(8):
        p.op("tensor", lambda e, kd=kd: e.matmul(
                PS[1][0:64, 0:128], Aa["w_summ"][:, kd * SUMD:(kd + 1) * SUMD],
                Aa["yTb"][:, kd * 128:(kd + 1) * 128],
                start=(kd == 0), stop=(kd == 7)),
             reads=["yTb", "wsumm"], writes=psq(1, 0, 4), tag="mm_sm")
    p.op("vector", lambda e: e.tensor_copy(Pm["sb6464"][:, :], PS[1][0:64, 0:128]),
         reads=psq(1, 0), writes=["sb6464"], tag="ev_sm")
    p.op("tensor", lambda e: e.transpose(PS[1].bitcast(BF16)[:, 512:576],
                                         Pm["sb6464"][:, :], Pm["ident"][0:64, 0:64]),
         reads=["sb6464", "c_ident"], writes=psq(1, 2), tag="tr_sm")
    p.op("vector", lambda e: e.tensor_copy(Aa["r1"][:, 0:64],
                                           PS[1].bitcast(BF16)[:, 512:576]),
         reads=psq(1, 2), writes=["r1"], tag="ev_sm2")
    p.dma("scalar", g["summ_spill"][ts(c), :], Aa["r1"][:, 0:64],
          reads=["r1"], writes=["smspill"], tag="st_sm")
    for kd in range(8):
        p.op("tensor", lambda e, kd=kd: e.matmul(
                PS[0][0:64, 0:128], Aa["w_q"][:, kd * SUMD:(kd + 1) * SUMD],
                Aa["yTb"][:, kd * 128:(kd + 1) * 128],
                start=(kd == 0), stop=(kd == 7)),
             reads=["yTb", "wqw"], writes=psq(0, 0, 4), tag="mm_q")
    p.op("scalar", lambda e: e.copy(Pm["sb6464"][:, :], PS[0][0:64, 0:128]),
         reads=psq(0, 0), writes=["sb6464"], tag="ev_q")
    p.dma("scalar", mkap(g["q_spill"].ap(), c * 128, [(T, 64), (1, 128)]),
          Pm["sb6464"][:, :], reads=["sb6464"], writes=["qspill"], tag="st_q")
    p.dma("sync", mkap(g["y_spill"].ap(), c * 128, [(T, 128), (128 * T, 8), (1, 128)]),
          Aa["yTb"][:, :], reads=["yTb"], writes=["yspill"], tag="st_y")
    if ci == 1:
        p.dma("sync", mkap(g["x2p_spill"].ap(), s * SUP,
                             [(T, 128), (128 * T, 8), (1, SUP)]),
              Aa["x2pT"][:, :], reads=["x2pT"], writes=["x2spill"], tag="st_x2p")
    if c == 0:
        dump(ctx, "yTb0", Aa["yTb"][:, :], ["yTb"])


def pool_exchange(ctx):
    p, g, Pm, Aa, Cc, PS = ctx["p"], ctx["g"], ctx["Pm"], ctx["Aa"], ctx["Cc"], ctx["PS"]
    T, NTILE, SCPAD = ctx["T"], ctx["NTILE"], ctx["SCPAD"]
    ALLT = NCORES * T
    fenceA = p.fence()
    _orig_op = p.op

    def _serial_op(eng, fn, reads=(), writes=(), after=(), **kw):
        prev = p.ops[-1] if p.ops else None
        o = _orig_op(eng, fn, reads=reads, writes=writes,
                     after=list(after) + ([prev] if prev is not None else []), **kw)
        return o
    p.op = _serial_op
    try:
        _pool_exchange_body(ctx, fenceA)
    finally:
        p.op = _orig_op


def _pool_exchange_body(ctx, fenceA):
    p, g, Pm, Aa, Cc, PS = ctx["p"], ctx["g"], ctx["Pm"], ctx["Aa"], ctx["Cc"], ctx["PS"]
    T, NTILE, SCPAD = ctx["T"], ctx["NTILE"], ctx["SCPAD"]
    ALLT = NCORES * T

    p.coll(lambda e: e.collective_compute(
            "AllGather", ALU.bypass, replica_groups=[list(range(NCORES))],
            ins=[g["cc_sc_in"].ap().opt()], outs=[g["cc_sc_out"].ap().opt()]),
           reads=["scin", "scpad"], writes=["scout"], tag="ag_sc")
    ALLP = min(128, ALLT // 128)
    ALLF = ALLT // ALLP
    PPC = T // ALLF
    p.dma("sync", Cc["allsc"][0:ALLP, 0:ALLF],
          mkap(g["cc_sc_out"].ap(), 0, [(SCPAD, NCORES), (ALLF, PPC), (1, ALLF)]),
          reads=["scout"], writes=["allsc"], after=fenceA, tag="ld_allsc")
    p.op("vector", lambda e: e.reduce_sum(Pm["cols"][0:ALLP, 8:9],
                                          Cc["allsc"][0:ALLP, 0:ALLF], AX.X),
         reads=["allsc"], writes=["gsum1"], tag="gsum1")
    p.op("tensor", lambda e: e.transpose(PS[0][0:1, 384:384 + ALLP],
                                         Pm["cols"][0:ALLP, 8:9],
                                         Pm["ident32"][0:ALLP, 0:ALLP]),
         reads=["gsum1", "c_ident32"], writes=psq(0, 3), tag="tr_sums")
    p.op("vector", lambda e: e.reduce_sum(Pm["srow32"][0:1, 1568:1569],
                                          PS[0][0:1, 384:384 + ALLP], AX.X),
         reads=psq(0, 3), writes=["gsum"], tag="gsum")

    if PXL < 2:
        return
    def level(pfx, np_, width, cur0, pong0, col0, niter=8):
        cur, pong = cur0, pong0
        for i in range(niter):
            p.op("vector", lambda e, cur=cur, np_=np_, width=width: e.max(
                    out=Pm["m8"][0:np_, :], in_=Cc["allsc"][0:np_, cur:cur + width]),
                 reads=[pfx + f"c{cur}"], writes=["m8v"], tag=pfx + "mx")
            p.op("vector", lambda e, i=i, np_=np_: e.tensor_copy(
                    Cc["allsc"][0:np_, col0 + 8 * i: col0 + 8 * i + 8], Pm["m8"][0:np_, :]),
                 reads=["m8v"], writes=[pfx + "C"], tag=pfx + "cp")
            if i < niter - 1:
                p.op("vector", lambda e, cur=cur, pong=pong, np_=np_, width=width:
                     e.match_replace(out=Cc["allsc"][0:np_, pong:pong + width],
                                     in_to_replace=Pm["m8"][0:np_, :],
                                     in_values=Cc["allsc"][0:np_, cur:cur + width],
                                     imm_value=-1e20),
                     reads=["m8v", pfx + f"c{cur}"], writes=[pfx + f"c{pong}"],
                     tag=pfx + "mr")
                cur, pong = pong, cur

    assert ALLF == 128
    p.buf("L1c0").writer = p.bufs["allsc"].writer
    level("L1", ALLP, 128, 0, 128, 256)
    p.dma("scalar", mkap(g["lred"].ap(), 0, [(64, ALLP), (1, 64)]),
          Cc["allsc"][0:ALLP, 256:320], reads=["L1C"], writes=["lr1"], tag="st_l1")
    n = ALLP * 64
    src_off, dst_off, prev_store = 0, n, "lr1"
    li = 2
    lastC = "L1C"
    pyr0 = ["L1c0"]
    while n > 64:
        np_ = max(1, n // 256)
        width = n // np_
        pfx = f"L{li}"
        p.dma("sync", Cc["allsc"][0:np_, 0:width],
              mkap(g["lred"].ap(), src_off, [(width, np_), (1, width)]),
              reads=[prev_store], writes=[pfx + "c0"], tag="ld_" + pfx)
        pyr0.append(pfx + "c0")
        level(pfx, np_, width, 0, 256, 512)
        lastC = pfx + "C"
        out_n = np_ * 64
        if out_n > 64:
            p.dma("scalar", mkap(g["lred"].ap(), dst_off, [(64, np_), (1, 64)]),
                  Cc["allsc"][0:np_, 512:576], reads=[pfx + "C"], writes=["lr" + pfx],
                  tag="st_" + pfx)
            prev_store = "lr" + pfx
            src_off, dst_off = dst_off, dst_off + out_n
        n = out_n
        li += 1
    if PXL < 3:
        return
    p.op("tensor", lambda e: e.matmul(PS[0][:, 392:393], Pm["onesrow32"][:, :],
                                      Cc["allsc"][0:1, 512 + 49:512 + 50],
                                      start=True, stop=True),
         reads=[lastC, "c_onesrow32"], writes=psq(0, 3), tag="mm_v50")
    p.op("vector", lambda e: e.tensor_copy(Pm["cols"][:, 9:10], PS[0][:, 392:393]),
         reads=psq(0, 3), writes=["v50"], tag="ev_v50")
    dump(ctx, "v50", Pm["cols"][:, 9:10], ["v50"])
    p.dma("sync", Cc["allsc"][0:ALLP, 0:128],
          mkap(g["cc_sc_out"].ap(), 0, [(SCPAD, NCORES), (ALLF, PPC), (1, ALLF)]),
          reads=["scout"], writes=["allsc2", "allsc"] + pyr0, tag="ld_allsc2")
    p.op("vector", lambda e: e.tensor_scalar(Cc["allsc"][0:ALLP, 128:256],
                                             Cc["allsc"][0:ALLP, 0:128],
                                             Pm["cols"][0:ALLP, 9:10], None, ALU.is_ge),
         reads=["allsc2", "v50"], writes=["mask_"], tag="mask1")
    p.op("vector", lambda e: e.scalar_tensor_tensor(
            Cc["allsc"][0:ALLP, 128:256], Cc["allsc"][0:ALLP, 0:128], TAU1,
            Cc["allsc"][0:ALLP, 128:256], ALU.is_gt, ALU.mult),
         reads=["allsc2", "mask_"], writes=["mask"], tag="mask2")
    p.op("vector", lambda e: e.reduce_sum(Pm["cols"][0:ALLP, 10:11],
                                          Cc["allsc"][0:ALLP, 128:256], AX.X),
         reads=["mask"], writes=["rowsum"], tag="rowsum")
    p.op("vector", lambda e: e.tensor_copy(Pm["colsB"][0:ALLP, 0:1],
                                           Pm["cols"][0:ALLP, 10:11]),
         reads=["rowsum"], writes=["rowsumB"], tag="rowsumB")
    p.op("tensor", lambda e: e.matmul(PS[0][0:ALLP, 393:394],
                                      Pm["strictup"][0:ALLP, 0:ALLP],
                                      Pm["colsB"][0:ALLP, 0:1], start=True, stop=True),
         reads=["rowsumB", "c_strictup"], writes=psq(0, 3), tag="mm_pfx")
    p.op("vector", lambda e: e.tensor_copy(Pm["cols"][0:ALLP, 11:12],
                                           PS[0][0:ALLP, 393:394]),
         reads=psq(0, 3), writes=["prefix"], tag="ev_pfx")
    p.op("tensor", lambda e: e.transpose(PS[0][0:1, 256:256 + ALLP],
                                         Pm["cols"][0:ALLP, 10:11],
                                         Pm["ident32"][0:ALLP, 0:ALLP]),
         reads=["rowsum", "c_ident32"], writes=psq(0, 2), tag="tr_cnt")
    p.op("vector", lambda e: e.reduce_sum(Pm["srow32"][0:1, 1569:1570],
                                          PS[0][0:1, 256:256 + ALLP], AX.X),
         reads=psq(0, 2), writes=["count"], tag="count")
    p.op("vector", lambda e: e.tensor_tensor_scan(
            Cc["allsc"][0:ALLP, 256:384], Cc["allsc"][0:ALLP, 128:256],
            Cc["allsc"][0:ALLP, 128:256], 0.0, ALU.add, ALU.bypass),
         reads=["mask"], writes=["incl", "L1C"], tag="incl")
    p.op("vector", lambda e: e.tensor_scalar(Cc["allsc"][0:ALLP, 256:384],
                                             Cc["allsc"][0:ALLP, 256:384],
                                             Pm["cols"][0:ALLP, 11:12], -1.0,
                                             ALU.add, ALU.add),
         reads=["incl", "prefix"], writes=["slots"], tag="slots")
    p.op("vector", lambda e: e.tensor_tensor(Cc["allsc"][0:ALLP, 256:384],
                                             Cc["allsc"][0:ALLP, 256:384],
                                             Cc["allsc"][0:ALLP, 128:256], ALU.mult),
         reads=["slots", "mask"], writes=["sm1"], tag="sm1")
    p.op("vector", lambda e: e.scalar_tensor_tensor(
            Cc["allsc"][0:ALLP, 256:384], Cc["allsc"][0:ALLP, 128:256], -1.0,
            Cc["allsc"][0:ALLP, 256:384], ALU.add, ALU.add),
         reads=["mask", "sm1"], writes=["stilde"], tag="stilde")
    p.op("vector", lambda e: e.tensor_copy(Cc["yrel"][0:ALLP, 0:128],
                                           Cc["allsc"][0:ALLP, 256:384]),
         reads=["stilde"], writes=["stildeB"], after=fenceA, tag="stildeB")
    dump(ctx, "stildeD", Cc["allsc"][0:ALLP, 256:384], ["stilde"])
    if PXL < 4:
        return
    p.dma("sync", Cc["summ_tok"][:, :],
          mkap(g["summ_spill"].ap(), 0, [(SUMD, 128), (128 * SUMD, NTILE), (1, SUMD)]),
          reads=["smspill"], writes=["summ_tok"], after=fenceA, tag="ld_summ_tok")
    p.dma("gpsimd", Cc["vw"][:, :], g["v_w"][:, :], writes=["wvw"], after=fenceA,
          tag="ld_vw")
    p.dma("sync", Pm["colsB"][:, 0:NTILE], g["rowsel"][:, :],
          writes=["rowsel", "rowsumB"], tag="ld_rowsel")
    NT = ctx["NTILE"]
    for j in range(NT):
        p.op("tensor", lambda e, j=j: e.matmul(
                PS[2][:, 0:1], Cc["yrel"][0:ALLP, 0:128], Pm["colsB"][0:ALLP, j:j + 1],
                start=True, stop=True),
             reads=["stildeB", "rowsel"], writes=psq(2, 0), tag="mm_scol")
        p.op("vector", lambda e: e.tensor_scalar(Cc["prb"][:, 0:64], Pm["iota64"][:, :],
                                                 PS[2][:, 0:1], None, ALU.is_equal),
             reads=psq(2, 0) + ["c_iota64"], writes=["PT"], tag="PT")
        p.op("tensor", lambda e, j=j: e.matmul(
                PS[1][0:64, 0:64], Cc["prb"][:, 0:64],
                Cc["summ_tok"][:, j * SUMD:(j + 1) * SUMD],
                start=(j == 0), stop=(j == NT - 1)),
             reads=["PT", "summ_tok"], writes=psq(1, 0, 4), tag="mm_pool")
    p.op("vector", lambda e: e.tensor_copy(Pm["pool"][:, :], PS[1][0:64, 0:64]),
         reads=psq(1, 0), writes=["pool_part"], tag="ev_pool")
    dump(ctx, "poolpart", Pm["pool"][:, :], ["pool_part"])
    dump(ctx, "summtok", Cc["summ_tok"][:, :], ["summ_tok"])
    p.dma("scalar", mkap(g["cc_pool_in"].ap(), 0, [(64, 64), (1, 64)]),
          Pm["pool"][:, :], reads=["pool_part"], writes=["poolin"], tag="st_pool")
    if PXL < 5:
        return
    p.coll(lambda e: e.collective_compute(
            "AllReduce", ALU.add, replica_groups=[list(range(NCORES))],
            ins=[g["cc_pool_in"].ap().opt()], outs=[g["cc_pool_out"].ap().opt()]),
           reads=["poolin", "poolpad"], writes=["poolout"], tag="ar_pool")
    p.dma("sync", Pm["pool"][:, :], mkap(g["cc_pool_out"].ap(), 0, [(64, 64), (1, 64)]),
          reads=["poolout"], writes=["pool"], tag="ld_pool")
    dump(ctx, "pool", Pm["pool"][:, :], ["pool"])
    p.op("vector", lambda e: e.tensor_scalar(Pm["srow32"][0:1, 1600:1664],
                                             Pm["iota64"][0:1, :],
                                             Pm["srow32"][0:1, 1569:1570], -1e9,
                                             ALU.is_ge, ALU.mult),
         reads=["count", "c_iota64"], writes=["mrow"], tag="mrow")
    p.op("tensor", lambda e: e.matmul(PS[0][:, 256:320], Pm["onesrow32"][:, :],
                                      Pm["srow32"][0:1, 1600:1664], start=True, stop=True),
         reads=["mrow", "c_onesrow32"], writes=psq(0, 2), tag="mm_mrep")
    p.op("vector", lambda e: e.tensor_copy(Pm["maskrep"][:, :], PS[0][:, 256:320]),
         reads=psq(0, 2), writes=["maskrep"], tag="ev_mrep")
    p.op("vector", lambda e: e.tensor_scalar(Pm["srow32"][0:1, 1570:1571],
                                             Pm["srow32"][0:1, 1568:1569],
                                             1.0 / ALLT, TAU2, ALU.mult, ALU.is_gt),
         reads=["gsum"], writes=["c1"], tag="c1")
    p.op("vector", lambda e: e.scalar_tensor_tensor(
            Pm["srow32"][0:1, 1571:1572], Pm["srow32"][0:1, 1569:1570], 0.5,
            Pm["srow32"][0:1, 1570:1571], ALU.is_gt, ALU.mult),
         reads=["count", "c1"], writes=["cond"], tag="cond")
    p.op("tensor", lambda e: e.matmul(PS[0][:, 320:321], Pm["onesrow32"][:, :],
                                      Pm["srow32"][0:1, 1571:1572], start=True, stop=True),
         reads=["cond", "c_onesrow32"], writes=psq(0, 2), tag="mm_cond")
    p.op("vector", lambda e: e.tensor_copy(Pm["cols"][:, 12:13], PS[0][:, 320:321]),
         reads=psq(0, 2), writes=["condcol"], tag="ev_cond")
    p.op("vector", lambda e: e.tensor_tensor(Pm["gacond"][:, :], Pm["modc"][:, 16:24],
                                             Pm["cols"][:, 12:13].to_broadcast([128, 8]),
                                             ALU.mult),
         reads=["condcol", "modc"], writes=["gacond"], tag="gacond")
    p.op("tensor", lambda e: e.transpose(PS[0][0:64, 384:448], Pm["pool"][:, :],
                                         Pm["ident32"][0:64, 0:64]),
         reads=["pool", "c_ident32"], writes=psq(0, 3), tag="tr_pool")
    p.op("vector", lambda e: e.tensor_copy(Pm["poolT"][:, :], PS[0][0:64, 384:448]),
         reads=psq(0, 3), writes=["poolT"], tag="ev_poolT")
    p.op("tensor", lambda e: e.matmul(PS[0][0:64, 0:64], Pm["kw"][:, :], Pm["poolT"][:, :],
                                      start=True, stop=True),
         reads=["poolT", "wkw"], writes=psq(0, 0), tag="mm_kk")
    p.op("scalar", lambda e: e.copy(Pm["kkT"][:, :], PS[0][0:64, 0:64]),
         reads=psq(0, 0), writes=["kkT"], tag="ev_kk")
    for v2 in range(2):
        p.op("tensor", lambda e, v2=v2: e.matmul(
                PS[1][0:64, :], Pm["poolT"][:, :], Cc["vw"][:, v2 * 512:(v2 + 1) * 512],
                start=True, stop=True),
             reads=["poolT", "wvw"], writes=psq(1, 0, 4), tag="mm_vv")
        p.op("scalar", lambda e, v2=v2: e.copy(Cc["vvb"][:, v2 * 512:(v2 + 1) * 512],
                                               PS[1][0:64, :]),
             reads=psq(1, 0, 4), writes=["vvb"], tag="ev_vv")


def phase_c(ctx):
    p, g, Pm, Cc, PS = ctx["p"], ctx["g"], ctx["Pm"], ctx["Cc"], ctx["PS"]
    T = ctx["T"]
    fenceA = p.fence()
    for kc in range(16):
        p.dma("gpsimd", Cc["w_gate"][:, kc * D:(kc + 1) * D], g["gate_w"][ts(kc), :],
              writes=["wgate"], after=fenceA if kc == 0 else (), tag="ld_gate")
    p.dma("sync", Cc["qT"][:, :], g["q_spill"][:, :], reads=["qspill"],
          writes=["qTfull"], after=fenceA, tag="ld_qT")
    for bt in range(T // 512):
        p.dma("sync", Cc["yrel"][:, :],
              mkap(g["y_spill"].ap(), bt * 512, [(T, 128), (128 * T, 8), (1, 512)]),
              reads=["yspill"], writes=["yrel", "stildeB"], tag="ld_yrel")
        p.dma("sync", Cc["x2slab"][:, :],
              mkap(g["x2p_spill"].ap(), bt * 512, [(T, 128), (128 * T, 8), (1, 512)]),
              reads=["x2spill"], writes=["x2slab"], tag="ld_x2")
        for tt in range(4):
            t0 = bt * 512 + tt * 128
            p.op("tensor", lambda e, t0=t0: e.matmul(
                    PS[0][:, 0:64], Cc["qT"][:, t0:t0 + 128], Pm["kkT"][:, :],
                    start=True, stop=True),
                 reads=["qTfull", "kkT"], writes=psq(0, 0, 2), tag="mm_lg")
            p.op("vector", lambda e: e.scalar_tensor_tensor(
                    Cc["logits"][:, :], PS[0][:, 0:64], 0.25, Pm["maskrep"][:, :],
                    ALU.mult, ALU.add),
                 reads=psq(0, 0) + ["maskrep"], writes=["lg"], tag="lg")
            if bt == 0 and tt == 0:
                dump(ctx, "lgD", Cc["logits"][:, :], ["lg"])
            p.op("vector", lambda e: e.reduce_max(Pm["cols"][:, 13:14], Cc["logits"][:, :],
                                                  AX.X, negate=True),
                 reads=["lg"], writes=["negm"], tag="negm")
            p.op("scalar", lambda e: e.activation(Cc["logits"][:, :], Cc["logits"][:, :],
                                                  ACT.Exp, bias=Pm["cols"][:, 13:14],
                                                  scale=1.0, accum_out=Pm["cols"][:, 14:15]),
                 reads=["lg", "negm"], writes=["lg", "den"], tag="lexp")
            p.op("vector", lambda e: e.reciprocal(Pm["cols"][:, 15:16], Pm["cols"][:, 14:15]),
                 reads=["den"], writes=["rden"], tag="rden")
            p.op("vector", lambda e: e.tensor_scalar(Cc["prb"][:, 0:64], Cc["logits"][:, :],
                                                     Pm["cols"][:, 15:16], None, ALU.mult),
                 reads=["lg", "rden"], writes=["prbC"], tag="prb")
            if bt == 0 and tt == 0:
                dump(ctx, "prbD", Cc["prb"][:, 0:64], ["prbC"])
            p.op("tensor", lambda e: e.transpose(PS[0].bitcast(BF16)[0:64, 256:384],
                                                 Cc["prb"][:, 0:64], Pm["ident"][:, :]),
                 reads=["prbC", "c_ident"], writes=psq(0, 0, 2), tag="tr_prb")
            p.op("vector", lambda e, t0=t0: e.tensor_copy(
                    Cc["probsT"][:, t0:t0 + 128], PS[0].bitcast(BF16)[0:64, 256:384]),
                 reads=psq(0, 1), writes=["probsT"], tag="ev_prb")
        for kd in range(8):
            bk = 2 + kd % 2
            p.op("tensor", lambda e, kd=kd, bt=bt, bk=bk: e.matmul(
                    PS[bk][:, :], Cc["vvb"][:, kd * 128:(kd + 1) * 128],
                    Cc["probsT"][:, bt * 512:(bt + 1) * 512], start=True, stop=True),
                 reads=["probsT", "vvb"], writes=psq(bk, 0, 4), tag="mm_rt"),
            p.op("scalar", lambda e, kd=kd, bk=bk: e.copy(
                    Cc["retrT"][:, kd * 512:(kd + 1) * 512], PS[bk][:, :]),
                 reads=psq(bk, 0, 4), writes=["retrT"], tag="ev_rt")
            if bt == 0 and kd == 0:
                dump(ctx, "retrD", Cc["retrT"][:, 0:512], ["retrT"])
        for kd in range(8):
            bk = 4 + kd % 2
            for kc in range(16):
                rhs = (Cc["yrel"][:, kc * 512:(kc + 1) * 512] if kc < 8 else
                       Cc["retrT"][:, (kc - 8) * 512:(kc - 7) * 512])
                rb = "yrel" if kc < 8 else "retrT"
                p.op("tensor", lambda e, kd=kd, kc=kc, bk=bk, rhs=rhs: e.matmul(
                        PS[bk][:, :], Cc["w_gate"][:, kc * D + kd * 128:
                                                   kc * D + kd * 128 + 128],
                        rhs, start=(kc == 0), stop=(kc == 15)),
                     reads=[rb, "wgate"], writes=psq(bk, 0, 4), tag="mm_g")
            p.op("scalar", lambda e, bk=bk: e.activation(Cc["gterm"][:, :], PS[bk][:, :],
                                                         ACT.Sigmoid, bias=0.0, scale=1.0),
                 reads=psq(bk, 0, 4), writes=["gterm"], tag="ev_g")
            p.op("vector", lambda e, kd=kd: e.tensor_tensor(
                    Cc["gterm"][:, :], Cc["gterm"][:, :],
                    Cc["retrT"][:, kd * 512:(kd + 1) * 512], ALU.mult),
                 reads=["gterm", "retrT"], writes=["gterm"], tag="gr"),
            if bt == 0 and kd == 0:
                dump(ctx, "gtD", Cc["gterm"][:, :], ["gterm"])
            p.op("vector", lambda e, kd=kd: e.scalar_tensor_tensor(
                    Cc["x2slab"][:, kd * 512:(kd + 1) * 512], Cc["gterm"][:, :],
                    Pm["gacond"][:, kd:kd + 1], Cc["x2slab"][:, kd * 512:(kd + 1) * 512],
                    ALU.mult, ALU.add),
                 reads=["gterm", "gacond", "x2slab"], writes=["x2slab"], tag="x2fin")
        p.dma("scalar", mkap(g["x2p_spill"].ap(), bt * 512,
                             [(T, 128), (128 * T, 8), (1, 512)]),
              Cc["x2slab"][:, :], reads=["x2slab"], writes=["x2spill"], tag="st_x2")
    dump(ctx, "probsT", Cc["probsT"][:, :], ["probsT"])


def phase_b(ctx):
    p, g, Pm, Bb, PS = ctx["p"], ctx["g"], ctx["Pm"], ctx["Bb"], ctx["PS"]
    T = ctx["T"]
    NT512 = T // 512
    fenceC = p.fence()
    p.dma("sync", Bb["cb1rep"][:, :], mkap(g["mod_hbm"].ap(), 4 * D, [(0, 128), (1, D)]),
          reads=["modhbm2"], writes=["cb1rep"], after=fenceC, tag="bc_cb")
    p.op("vector", lambda e: e.tensor_scalar(Bb["cb1rep"][:, :], Bb["cb1rep"][:, :],
                                             1.0, None, ALU.add),
         reads=["cb1rep"], writes=["cb1rep"], tag="cb1")
    p.dma("sync", Bb["sbrep"][:, :], mkap(g["mod_hbm"].ap(), 3 * D, [(0, 128), (1, D)]),
          reads=["modhbm1"], writes=["sbrep"], after=fenceC, tag="bc_sb")

    WQ = WQ_SHARD_ELEMS
    for tg in range(max(1, T // 1024)):
        TGW = min(1024, T)
        for sub in range(TGW // 512):
            bts = tg * 2 + sub
            if BBL == 10:
                continue
            p.dma("sync", Bb["x2f"][:, :],
                  mkap(g["x2p_spill"].ap(), bts * 512, [(T, 128), (128 * T, 8), (1, 512)]),
                  reads=["x2spill"], writes=["x2f"], tag="ld_x2f")
            if BBL == 105:
                continue
            for kd in range(8):
                bk = 6 + kd % 2
                for q4 in range(4):
                    p.op("tensor", lambda e, kd=kd, q4=q4, bk=bk: e.transpose(
                            PS[bk][:, q4 * 128:q4 * 128 + 128],
                            Bb["x2f"][:, kd * 512 + q4 * 128: kd * 512 + q4 * 128 + 128],
                            Pm["ident32"][:, :]),
                         reads=["x2f", "c_ident32"], writes=psq(bk, 0, 4), tag="tr_x2")
                p.op("scalar", lambda e, kd=kd, bk=bk: e.copy(
                        apx(Bb["x2tok"], kd * 128, [(D, 4), (1, 128)]),
                        PS[bk][:, 0:512]),
                     reads=psq(bk, 0, 4), writes=["x2tok"], tag="ev_x2")
            if BBL == 11:
                continue
            Cx = Pm["cols"]
            for tt in range(4):
                xt = Bb["x2tok"][:, tt * D:(tt + 1) * D]
                p.op("scalar", lambda e, xt=xt: e.activation(
                        xt, xt, ACT.Identity, bias=0.0, scale=1.0, accum_out=Cx[:, 0:1]),
                     reads=["x2tok"], writes=["x2tok", "l2sum"], tag="l2_s1")
                p.op("scalar", lambda e, xt=xt, tt=tt: e.activation(
                        Bb["h2q"][:, tt * D:(tt + 1) * D], xt, ACT.Square,
                        bias=0.0, scale=1.0, accum_out=Cx[:, 1:2]),
                     reads=["x2tok"], writes=["l2sq", "h2qdump"], tag="l2_s2")
                p.op("vector", lambda e: e.tensor_scalar(Cx[:, 2:3], Cx[:, 0:1],
                                                         1.0 / D, None, ALU.mult),
                     reads=["l2sum"], writes=["l2m"], tag="l2_m")
                p.op("vector", lambda e: e.tensor_tensor(Cx[:, 3:4], Cx[:, 2:3],
                                                         Cx[:, 2:3], ALU.mult),
                     reads=["l2m"], writes=["l2m2"], tag="l2_m2")
                p.op("vector", lambda e: e.scalar_tensor_tensor(
                        Cx[:, 4:5], Cx[:, 1:2], 1.0 / D, Cx[:, 3:4],
                        ALU.mult, ALU.subtract),
                     reads=["l2sq", "l2m2"], writes=["l2var"], tag="l2_var")
                p.op("scalar", lambda e: e.activation(Cx[:, 5:6], Cx[:, 4:5], ACT.Sqrt,
                                                      bias=Pm["kconst"][:, 0:1], scale=1.0),
                     reads=["l2var", "kconst"], writes=["l2sd"], tag="l2_sd")
                p.op("vector", lambda e: e.reciprocal(Cx[:, 6:7], Cx[:, 5:6]),
                     reads=["l2sd"], writes=["l2rstd"], tag="l2_rstd")
                p.op("vector", lambda e, xt=xt: e.tensor_scalar(
                        Bb["h2"][:, 0:D], xt, Cx[:, 2:3], Cx[:, 6:7],
                        ALU.subtract, ALU.mult),
                     reads=["x2tok", "l2m", "l2rstd"], writes=["h2u"], tag="l2_u")
                p.op("vector", lambda e: e.tensor_tensor(Bb["h2"][:, 0:D], Bb["h2"][:, 0:D],
                                                         Bb["cb1rep"][:, :], ALU.mult),
                     reads=["h2u", "cb1rep"], writes=["h2u"], tag="l2_cb")
                p.op("vector", lambda e: e.tensor_tensor(Bb["h2"][:, 0:D], Bb["h2"][:, 0:D],
                                                         Bb["sbrep"][:, :], ALU.add),
                     reads=["h2u", "sbrep"], writes=["h2m"], tag="l2_sb")
                if BBL == 12:
                    continue
                scr = dict(xn=Bb["q1"][:, :], a=Bb["q2"][:, :], acc=Bb["q3"][:, :],
                           hi=Bb["h2"].bitcast(F32)[:, 0:D] if False else None,
                           sc=(Bb["qsc"], 0, 64, 0), m=Bb["qm"][:, :])
                scr["hi"] = Bb["hffn"][:, 2048:3072]
                scr["sc"] = (Bb["qsc"], 0, 64, 0)
                quant_chain(ctx, "vector", Bb["h2"][:, 0:D],
                            Bb["h2q"][:, tt * D:(tt + 1) * D], D, scr, ["h2m"], "qh2")
            if BBL == 13:
                continue
            for kd in range(8):
                bk = 6 + kd % 2
                for tt4 in range(4):
                    p.op("tensor", lambda e, kd=kd, tt4=tt4, bk=bk: e.transpose(
                            PS[bk].bitcast(BF16)[:, tt4 * 128:tt4 * 128 + 128],
                            Bb["h2q"][:, tt4 * D + kd * 128: tt4 * D + kd * 128 + 128],
                            Pm["ident"][:, :]),
                         reads=["qh2", "c_ident"], writes=psq(bk, 0, 4), tag="tr_h2q")
                p.op("scalar", lambda e, kd=kd, sub=sub, bk=bk: e.copy(
                        Bb["h2qT"][:, kd * 1024 + sub * 512:
                                   kd * 1024 + sub * 512 + 512],
                        PS[bk].bitcast(BF16)[:, 0:512]),
                     reads=psq(bk, 0, 4), writes=["h2qT"], tag="ev_h2q")
            if tg == 0 and sub == 0:
                dump(ctx, "h2qD", Bb["h2q"][:, :], ["qh2"])
        if BBL < 2:
            continue
        for sh in range(NCORES):
            ns = FFN_SH[sh]
            nch = ns // 128
            base_icc = sum(FFN_SH[:sh]) // 128
            wgu_t = Bb["wgu"] if sh % 2 == 0 else Bb["wguB"]
            wgu_b = f"wgu{sh % 2}"
            hffn_t = Bb["hffn"] if sh % 2 == 0 else Bb["hffnB"]
            hffn_b = f"hffn{sh % 2}"
            for kd in range(8):
                p.dma("sync", wgu_t[:, kd * 384:kd * 384 + ns],
                      mkap(g["cc_wq_out"].ap(), sh * WQ + kd * 128 * 384,
                           [(384, 128), (1, ns)]),
                      reads=["wqout"], writes=[wgu_b], tag="ld_wg")
                p.dma("sync", wgu_t[:, 8 * 384 + kd * 384: 8 * 384 + kd * 384 + ns],
                      mkap(g["cc_wq_out"].ap(), sh * WQ + 384 * D + kd * 128 * 384,
                           [(384, 128), (1, ns)]),
                      reads=["wqout"], writes=[wgu_b], tag="ld_wu")
            for nl in range(nch):
                for t5 in range(TGW // 512):
                    for (mi, mb) in ((0, 0), (1, 8 * 384)):
                        bk = mi
                        for kd in range(8):
                            p.op("tensor", lambda e, nl=nl, t5=t5, kd=kd, mb=mb, bk=bk,
                                 wgu_t=wgu_t: e.matmul(
                                    PS[bk][:, :],
                                    wgu_t[:, mb + kd * 384 + nl * 128:
                                          mb + kd * 384 + nl * 128 + 128],
                                    Bb["h2qT"][:, kd * 1024 + t5 * 512:
                                               kd * 1024 + t5 * 512 + 512],
                                    start=(kd == 0), stop=(kd == 7)),
                                 reads=["h2qT", wgu_b], writes=psq(bk, 0, 4), tag="mm_gu")
                    p.op("scalar", lambda e: e.activation(Bb["ftmp"][:, :], PS[0][:, :],
                                                          ACT.Silu, bias=0.0, scale=1.0),
                         reads=psq(0, 0, 4), writes=["gsilu", "fout"], tag="ev_gs")
                    p.op("vector", lambda e, nl=nl, t5=t5, hffn_t=hffn_t: e.tensor_tensor(
                            hffn_t[:, nl * 1024 + t5 * 512: nl * 1024 + t5 * 512 + 512],
                            Bb["ftmp"][:, :], PS[1][:, :], ALU.mult),
                         reads=["gsilu"] + psq(1, 0, 4), writes=[hffn_b], tag="hffn")
                    if tg == 0 and sh == 0 and nl == 0 and t5 == 0:
                        dump(ctx, "hffnD", Bb["hffn"][:, 0:512], ["hffn0"])
            if BBL < 3:
                continue
            for tt8 in range(TGW // 128):
                for nl in range(nch):
                    p.op("tensor", lambda e, nl=nl, tt8=tt8, hffn_t=hffn_t: e.transpose(
                            PS[2][:, nl * 128:(nl + 1) * 128],
                            hffn_t[:, nl * 1024 + tt8 * 128: nl * 1024 + tt8 * 128 + 128],
                            Pm["ident32"][:, :]),
                         reads=[hffn_b, "c_ident32"], writes=psq(2, nl), tag="tr_hf")
                scr = dict(xn=Bb["q1"][:, 0:ns], a=Bb["q2"][:, 0:ns], acc=Bb["q3"][:, 0:ns],
                           hi=Bb["outtok"][:, 0:ns], sc=(Bb["qsc"], 0, 64, 0),
                           m=Bb["qm"][:, 0:ns])
                quant_chain(ctx, "vector", PS[2][:, 0:ns], Bb["hfq"][:, 0:ns], ns, scr,
                            psq(2, 0, nch), "qhf")
                for nl in range(nch):
                    p.op("tensor", lambda e, nl=nl: e.transpose(
                            PS[3].bitcast(BF16)[:, nl * 128:nl * 128 + 128],
                            Bb["hfq"][:, nl * 128:(nl + 1) * 128], Pm["ident"][:, :]),
                         reads=["qhf", "c_ident"], writes=psq(3, 0, 4), tag="tr_hq")
                p.op("scalar", lambda e, nch=nch, tt8=tt8, base_icc=base_icc: e.copy(
                        apx(Bb["hq"], base_icc * 1024 + tt8 * 128, [(1024, nch), (1, 128)]),
                        PS[3].bitcast(BF16)[:, 0:nch * 128]),
                     reads=psq(3, 0, 4), writes=[f"hq{sh}"], tag="ev_hq")
        if BBL < 4:
            continue
        for kd in range(8):
            p.dma("sync", Bb["wd"][:, :],
                  mkap(g["cc_wq_out"].ap(), kd * WQ + 2 * 384 * D,
                       [(128, 128), (16384, 22), (1, 128)]),
                  reads=["wqout"], writes=["wdb"], tag="ld_wd")
            for t5 in range(TGW // 512):
                bk = 4 + t5 % 2
                for icc in range(22):
                    p.op("tensor", lambda e, icc=icc, t5=t5, bk=bk: e.matmul(
                            PS[bk][:, :], Bb["wd"][:, icc * 128:(icc + 1) * 128],
                            Bb["hq"][:, icc * 1024 + t5 * 512: icc * 1024 + t5 * 512 + 512],
                            start=(icc == 0), stop=(icc == 21)),
                         reads=[f"hq{s_}" for s_ in range(NCORES)] + ["wdb"],
                         writes=psq(bk, 0, 4), tag="mm_dn")
                bts = tg * 2 + t5
                p.dma("sync", Bb["ftmp"][:, :],
                      mkap(g["x2p_spill"].ap(), kd * 128 * T + bts * 512,
                           [(T, 128), (1, 512)]),
                      reads=["x2spill"], writes=["x2fd", "gsilu", "fout"],
                      tag="ld_x2d")
                p.op("vector", lambda e, kd=kd, bk=bk: e.scalar_tensor_tensor(
                        Bb["ftmp"][:, :], PS[bk][:, :], Pm["modc"][:, 40 + kd:41 + kd],
                        Bb["ftmp"][:, :], ALU.mult, ALU.add),
                     reads=psq(bk, 0, 4) + ["modc", "x2fd"], writes=["fout"], tag="fout")
                if tg == 0 and kd == 0 and t5 == 0:
                    dump(ctx, "foutD", Bb["ftmp"][:, :], ["fout"])
                    dump(ctx, "hqD", Bb["hq"][:, 0:512], ["hq"])
                obk = 6 + kd % 2
                for q4 in range(4):
                    p.op("tensor", lambda e, q4=q4, obk=obk: e.transpose(
                            PS[obk][:, q4 * 128:(q4 + 1) * 128],
                            Bb["ftmp"][:, q4 * 128:(q4 + 1) * 128], Pm["ident32"][:, :]),
                         reads=["fout", "c_ident32"], writes=psq(obk, 0, 4), tag="tr_out")
                p.op("scalar", lambda e, obk=obk: e.copy(
                        Bb["outtok"][:, 0:512], PS[obk][:, 0:512]),
                     reads=psq(obk, 0, 4), writes=["outtok"], tag="ev_out")
                p.dma("scalar",
                      mkap(g["out"].ap(), bts * 512 * D + kd * 128,
                           [(D, 128), (128 * D, 4), (1, 128)]),
                      Bb["outtok"][:, 0:512], reads=["outtok"], writes=["outhbm"],
                      tag="st_out")


_CACHE = {}


def _prep(inputs, b, T):
    f32 = lambda a: np.ascontiguousarray(np.asarray(a, dtype=np.float32))
    wg = f32(inputs["L0_ffn_w_gate"])
    wu = f32(inputs["L0_ffn_w_up"])
    wd = f32(inputs["L0_ffn_w_down"])
    off = [0]
    for sgl in FFN_SH:
        off.append(off[-1] + sgl)
    def pad384(m):
        out = np.zeros((384, D), np.float32)
        out[:m.shape[0]] = m
        return out
    rowsel = np.zeros((128, T // 128), ml_dtypes.bfloat16)
    for j in range(T // 128):
        rowsel[(b * T + j * 128) // 128, j] = 1
    return {
        "x": f32(inputs["L0_attn_x"][b]),
        "norm_w": f32(inputs["L0_attn_norm_w"]),
        "in_proj_w": f32(inputs["L0_attn_in_proj_w"]),
        "conv_w": f32(inputs["L0_attn_conv_w"]),
        "conv_b": f32(inputs["L0_attn_conv_b"]),
        "dt_bias": f32(inputs["L0_attn_dt_bias"]),
        "A_log": f32(inputs["L0_attn_A_log"]),
        "Dvec": f32(inputs["L0_attn_D"]),
        "ssm_norm_w": f32(inputs["L0_attn_ssm_norm_w"]),
        "out_proj_w": f32(inputs["L0_attn_out_proj_w"]),
        "scorer_w1": f32(inputs["L0_attn_scorer_w1"]),
        "scorer_w2": f32(inputs["L0_attn_scorer_w2"]).reshape(SC_HID),
        "summ_w": f32(inputs["L0_attn_summ_w"]),
        "q_w": f32(inputs["L0_attn_q_w"]),
        "k_w": f32(inputs["L0_attn_k_w"]),
        "v_w": f32(inputs["L0_attn_v_w"]),
        "gate_w": f32(inputs["L0_attn_gate_w"]),
        "cvec": f32(inputs["L0_c"][b]),
        "wmod": f32(inputs["L0_wmod"]),
        "wg_shard": pad384(wg[off[b]:off[b + 1]]),
        "wu_shard": pad384(wu[off[b]:off[b + 1]]),
        "wd_shard": np.ascontiguousarray(wd[b * WD_SH:(b + 1) * WD_SH]),
        "rowsel": rowsel,
    }


def kernel(**inputs):
    T = int(np.asarray(inputs["L0_attn_x"]).shape[1])
    key = (T, tuple(sorted(n for (n, _, _) in DEBUG)))
    if key not in _CACHE:
        _CACHE[key] = build(T=T, dbg=DEBUG)
    nc = _CACHE[key]
    in_maps = [_prep(inputs, b, T) for b in range(NCORES)]
    res = run_bass_kernel_spmd(nc, in_maps, core_ids=list(range(NCORES)))
    out = np.stack([np.asarray(r["out"], dtype=np.float32) for r in res.results])
    if DEBUG:
        kernel._dbg = res.results
    return out


# version 14
# speedup vs baseline: 1.0103x; 1.0103x over previous
import numpy as np
import ml_dtypes
from contextlib import ExitStack

import concourse.bass as bass
import concourse.mybir as mybir
from concourse.bass_utils import run_bass_kernel_spmd
from concourse.bass_types import AP

F32 = mybir.dt.float32
BF16 = mybir.dt.bfloat16
I32 = mybir.dt.int32
ALU = mybir.AluOpType
ACT = mybir.ActivationFunctionType
AX = mybir.AxisListType

D = 1024
NH = 32
HD = 64
DIN = 2048
DSTATE = 128
DCONV = 4
CONV_DIM = DIN + 2 * DSTATE
DINPROJ = 2 * DIN + 2 * DSTATE + NH
SC_HID = 256
SUMD = 64
NSLOT = 64
TAU1, TAU2 = 0.5, 0.3
EPS = 1e-5
L = 128
SUP = 256
NCORES = 8
IC = 2816
FFN_SH = [384, 384, 384, 384, 384, 384, 256, 256]
WD_SH = D // NCORES
WQ_SHARD_ELEMS = 384 * D * 2 + WD_SH * IC
ENGINES = ("sync", "tensor", "vector", "scalar", "gpsimd")

MIDS = [0.25, 0.75, 1.25, 1.75]
DEBUG = []
NQ = 8
PHASES = 8
NO_COLL = False
WQL = 5
FRL = 9
BBL = 9
PXL = 9


class Op:
    __slots__ = ("eng", "fn", "extra", "asyn", "qcls", "signal", "semval", "waits",
                 "tag", "idx")

    def __init__(self, eng, fn, extra, asyn, qcls, tag, idx):
        self.eng, self.fn, self.extra, self.asyn, self.qcls = eng, fn, extra, asyn, qcls
        self.signal, self.semval, self.waits, self.tag, self.idx = False, None, None, tag, idx


class Buf:
    __slots__ = ("name", "writer", "readers")

    def __init__(self, name):
        self.name, self.writer, self.readers = name, None, []


class Prog:

    def __init__(self, nc):
        self.nc = nc
        self.ops = []
        self.bufs = {}

    def buf(self, name):
        if name not in self.bufs:
            self.bufs[name] = Buf(name)
        return self.bufs[name]

    def op(self, eng, fn, reads=(), writes=(), after=(), dma=False, coll=False,
           tag=""):
        reads = list(reads)
        writes = list(writes)
        for b in reads:
            if isinstance(b, str) and len(b) == 5 and b.startswith("ps") \
                    and b[3] == "q" and b not in writes:
                writes.append(b)
        reads = [self.buf(b) if isinstance(b, str) else b for b in reads]
        writes = [self.buf(b) if isinstance(b, str) else b for b in writes]
        deps = list(after)
        for b in reads:
            if b.writer is not None:
                deps.append(b.writer)
        for b in writes:
            if b.writer is not None:
                deps.append(b.writer)
            deps.extend(b.readers)
        qcls = "l" if coll else ("q" if dma else "c")
        o = Op(eng, fn, [], dma or coll, qcls, tag, len(self.ops))
        seen = set()
        for d in deps:
            if d is not None and d is not o and id(d) not in seen:
                seen.add(id(d))
                o.extra.append(d)
        self.ops.append(o)
        for b in reads:
            if o not in b.readers:
                b.readers.append(o)
        for b in writes:
            b.writer = o
            b.readers = []
        return o

    def dma(self, eng, out, in_, reads=(), writes=(), after=(), tag=""):
        return self.op(eng, lambda e, out=out, in_=in_: e.dma_start(out=out, in_=in_),
                       reads=reads, writes=writes, after=after, dma=True, tag=tag)

    def coll(self, fn, reads=(), writes=(), after=(), tag=""):
        if NO_COLL:
            return self.op("gpsimd", lambda e: e.nop(), reads=reads, writes=writes,
                           after=after, tag=tag + "_skip")
        return self.op("gpsimd", fn, reads=reads, writes=writes, after=after,
                       coll=True, tag=tag)

    def fence(self):
        last = {}
        for o in self.ops:
            last[o.eng] = o
        return list(last.values())

    def finalize(self, block, sems):
        for o in self.ops:
            if o.asyn:
                o.signal = True
            for d in o.extra:
                if d.eng == o.eng == "tensor" and not d.asyn:
                    continue
                d.signal = True
        rr = {}
        for o in self.ops:
            if o.qcls == "q":
                i = rr.get(o.eng, 0)
                o.qcls = f"q{i % NQ}"
                rr[o.eng] = i + 1
        qcs = tuple(f"q{i}" for i in range(NQ))
        cnt = {(e, c): 0 for e in ENGINES for c in ("c", "l") + qcs}
        for o in self.ops:
            if o.signal:
                k = (o.eng, o.qcls)
                cnt[k] += 16 if o.qcls.startswith("q") else 1
                o.semval = cnt[k]
        water = {e: {} for e in ENGINES}
        prev_async = {}
        for o in self.ops:
            need = {}
            if o.asyn:
                qk = (o.eng, o.qcls)
                if prev_async.get(qk):
                    need[qk] = prev_async[qk]
                prev_async[qk] = o.semval
            for d in o.extra:
                if d.semval is None:
                    continue
                if d.eng == o.eng == "tensor" and not d.asyn:
                    continue
                k = (d.eng, d.qcls)
                if d.semval > need.get(k, 0):
                    need[k] = d.semval
            w = []
            for k, v in sorted(need.items()):
                if v > water[o.eng].get(k, 0):
                    w.append((k, v))
                    water[o.eng][k] = v
            o.waits = w
        streams = {e: [o for o in self.ops if o.eng == e] for e in ENGINES}

        def emitter(ename):
            def run(eng):
                for o in streams[ename]:
                    for (k, v) in o.waits:
                        eng.wait_ge(sems[k], v)
                    inst = o.fn(eng)
                    if o.signal:
                        inst.then_inc(sems[(o.eng, o.qcls)],
                                      16 if o.qcls.startswith("q") else 1)
                for qc in qcs + ("l",):
                    if cnt[(ename, qc)]:
                        eng.wait_ge(sems[(ename, qc)], cnt[(ename, qc)])
            return run

        for e in ENGINES:
            if streams[e]:
                getattr(block, e)(emitter(e))


def mkap(base, offset_elems, dims):
    if isinstance(base, AP):
        t, off = base.tensor, base.offset
    else:
        t, off = base, 0
    return AP(t, off + int(offset_elems), [[int(s), int(n)] for (s, n) in dims])


def ts(i, n=128):
    return bass.ts(i, n)


def build(T=2048, dbg=()):
    nc = bass.Bass("TRN2", target_bir_lowering=False)
    NCH = T // L
    NSUPS = T // SUP
    NTILE = T // 128
    ALLT = NCORES * T
    ALLP = ALLT // 128
    ALLF = ALLT // ALLP if ALLP == 128 else 128
    if ALLP > 128:
        raise ValueError("T too big")
    SCPAD = T + 8

    g = {}

    def dram(name, shape, dt=F32, kind="Internal", **kw):
        g[name] = nc.dram_tensor(name, shape, dt, kind=kind, **kw)
        return g[name]

    dram("x", [T, D], kind="ExternalInput")
    dram("norm_w", [D], kind="ExternalInput")
    dram("in_proj_w", [D, DINPROJ], kind="ExternalInput")
    dram("conv_w", [CONV_DIM, DCONV], kind="ExternalInput")
    dram("conv_b", [CONV_DIM], kind="ExternalInput")
    dram("dt_bias", [NH], kind="ExternalInput")
    dram("A_log", [NH], kind="ExternalInput")
    dram("Dvec", [NH], kind="ExternalInput")
    dram("ssm_norm_w", [DIN], kind="ExternalInput")
    dram("out_proj_w", [DIN, D], kind="ExternalInput")
    dram("scorer_w1", [D, SC_HID], kind="ExternalInput")
    dram("scorer_w2", [SC_HID], kind="ExternalInput")
    dram("summ_w", [D, SUMD], kind="ExternalInput")
    dram("q_w", [D, SUMD], kind="ExternalInput")
    dram("k_w", [SUMD, SUMD], kind="ExternalInput")
    dram("v_w", [SUMD, D], kind="ExternalInput")
    dram("gate_w", [2 * D, D], kind="ExternalInput")
    dram("cvec", [D], kind="ExternalInput")
    dram("wmod", [D, 6 * D], kind="ExternalInput")
    dram("wg_shard", [384, D], kind="ExternalInput")
    dram("wu_shard", [384, D], kind="ExternalInput")
    dram("wd_shard", [WD_SH, IC], kind="ExternalInput")
    dram("rowsel", [128, NTILE], BF16, kind="ExternalInput")
    dram("out", [T, D], kind="ExternalOutput")

    dram("hbm_rows", [64 * T])
    dram("mod_hbm", [6 * D])
    dram("x2p_spill", [D, T])
    dram("y_spill", [D, T], BF16)
    dram("q_spill", [SUMD, T], BF16)
    dram("summ_spill", [T, SUMD], BF16)
    dram("lred", [3 * 16384])
    dram("cc_sc_in", [SCPAD])
    dram("cc_sc_out", [NCORES * SCPAD], addr_space="Shared")
    dram("cc_pool_in", [NSLOT * SUMD + 16])
    dram("cc_pool_out", [NSLOT * SUMD + 16])
    g["cc_wq_in"] = nc.dram_tensor("cc_wq_in", [WQ_SHARD_ELEMS], BF16)
    g["cc_wq_out"] = nc.dram_tensor("cc_wq_out", [NCORES * WQ_SHARD_ELEMS], BF16,
                                    addr_space="Shared")

    dbgh = {}
    for (name, shape, dt_) in dbg:
        dbgh[name] = dram("dbg_" + name, list(shape), dt_, kind="ExternalOutput")

    g["c_ident"] = nc.inline_tensor(np.eye(128, dtype=ml_dtypes.bfloat16), "c_ident")
    g["c_ident32"] = nc.inline_tensor(np.eye(128, dtype=np.float32), "c_ident32")
    g["c_onescol"] = nc.inline_tensor(np.ones((128, 1), dtype=ml_dtypes.bfloat16), "c_onescol")
    g["c_onesrow32"] = nc.inline_tensor(np.ones((1, 128), dtype=np.float32), "c_onesrow32")
    tril = (np.arange(128)[:, None] <= np.arange(128)[None, :]).astype(np.float32)
    g["c_tril"] = nc.inline_tensor(tril, "c_tril")
    g["c_iota64"] = nc.inline_tensor(
        np.repeat(np.arange(64, dtype=np.float32)[None, :], 128, 0), "c_iota64")
    sup_m = (np.arange(128)[:, None] < np.arange(128)[None, :]).astype(ml_dtypes.bfloat16)
    g["c_strictup"] = nc.inline_tensor(sup_m, "c_strictup")

    p = Prog(nc)
    es = ExitStack()
    sems = {}
    for e in ENGINES:
        clss = ["c"]
        if e in ("sync", "scalar", "gpsimd"):
            clss += [f"q{i}" for i in range(NQ)]
        if e == "gpsimd":
            clss.append("l")
        for cls_ in clss:
            sems[(e, cls_)] = es.enter_context(nc.semaphore(f"sem_{e}_{cls_}"))

    MAP0 = (int(nc.sbuf_base) + 31) & ~31
    SB_LIMIT = int(nc.sbuf_top)
    cur = [MAP0]
    tensors = {}
    _off = {}

    def alloc(name, shape, dt=F32, at=None):
        nbytes = int(np.prod(shape[1:])) * mybir.dt.size(dt)
        nbytes = (nbytes + 31) & ~31
        o = cur[0] if at is None else at
        _off[name] = o
        t = nc.alloc_sbuf_tensor_at(name, list(shape), dt, offset=o)
        if at is None:
            cur[0] += nbytes
            assert cur[0] <= SB_LIMIT, f"sbuf overflow at {name}: {cur[0]}"
        else:
            assert o + nbytes <= SB_LIMIT, f"sbuf overlay overflow {name}"
        tensors[name] = t
        return t

    Pm = {}
    for (n, sh, dt_) in [
        ("ident", [128, 128], BF16), ("ident32", [128, 128], F32),
        ("onescol", [128, 1], BF16), ("onesrow32", [1, 128], F32),
        ("tril", [128, 128], F32), ("iota64", [128, 64], F32),
        ("strictup", [128, 128], BF16),
        ("modc", [128, 64], F32), ("gacond", [128, 8], F32),
        ("normw", [128, 8], F32), ("ssmw", [128, 16], F32),
        ("convw", [128, 18 * DCONV], F32), ("convb", [128, 18], F32),
        ("dtb", [32, 1], F32), ("negA", [32, 1], F32), ("Dcol", [128, 32], F32),
        ("kw", [64, 64], BF16),
        ("scores16", [16, 128], F32), ("S", [128, DIN], F32),
        ("srow32", [32, 1792], F32), ("dcrep", [128, NH], F32),
        ("cols", [128, 16], F32), ("colsB", [128, 16], BF16),
        ("maskrep", [128, 64], F32), ("pool", [64, 64], F32),
        ("m8", [128, 8], F32), ("halo", [128, 18 * 3], F32),
        ("kconst", [128, 4], F32),
        ("sb6464", [64, 128], BF16),
        ("kkT", [64, 64], BF16), ("poolT", [64, 64], BF16),
    ]:
        Pm[n] = alloc(n, sh, dt_)
    PERM_TOP = cur[0]

    Aa = {}
    cur[0] = PERM_TOP
    for (n, sh, dt_) in [
        ("w_inproj", [128, 8 * DINPROJ], BF16),
        ("w_outproj", [128, 16 * D], BF16),
        ("w_sc1", [128, 8 * SC_HID], BF16),
        ("w_summ", [128, 8 * SUMD], BF16),
        ("w_q", [128, 8 * SUMD], BF16),
        ("w_s2", [128, 2], BF16),
        ("x_tok", [128, D], F32),
        ("hT", [128, 8 * SUP], BF16),
        ("ztil", [128, 16 * SUP], BF16),
        ("acc0", [128, SUP + 8], F32),
        ("xst", [128, 16 * SUP], BF16),
        ("bct", [128, 2 * SUP], BF16),
        ("xsD", [128, 4 * L], F32),
        ("Mt", [128, 8 * L], BF16),
        ("csrep", [128, 8 * L], F32),
        ("erep2", [128, 4 * L], F32),
        ("yw", [128, 16 * L], F32),
        ("ycore", [128, 16 * L], BF16),
        ("x2pT", [128, 8 * SUP], F32),
        ("xd", [128, DIN], BF16),
        ("xdp", [128, DIN], BF16),
        ("btok", [128, DSTATE], BF16),
        ("Gt", [128, L], BF16),
        ("cstok", [128, NH], F32),
        ("dttok", [128, NH], F32),
        ("ddtok", [128, NH], F32),
        ("dstok", [128, NH], F32),
        ("dsup", [32, SUP], F32),
        ("cssup", [32, SUP], F32),
        ("essup", [32, SUP], F32),
        ("dssup", [32, SUP], F32),
        ("yTb", [128, 8 * L], BF16),
        ("r1", [128, 2 * L], BF16),
        ("rstd_rep", [128, L], F32),
        ("Sb", [128, DIN], BF16),
    ]:
        Aa[n] = alloc(n, sh, dt_)
    Aa["wmodbuf"] = alloc("wmodbuf", [128, 2048], BF16, at=_off["xd"])
    A_TOP = cur[0]
    assert A_TOP <= SB_LIMIT, f"phase A overflow: {A_TOP}"

    Cc = {}
    cur[0] = PERM_TOP
    for (n, sh, dt_) in [
        ("w_gate", [128, 16 * D], BF16),
        ("probsT", [64, T], BF16),
        ("logits", [128, 64], F32),
        ("prb", [128, 64], BF16),
        ("yrel", [128, 8 * 512], BF16),
        ("retrT", [128, 8 * 512], BF16),
        ("gterm", [128, 512], BF16),
        ("x2slab", [128, 8 * 512], F32),
        ("allsc", [128, 576], F32),
        ("vw", [64, D], BF16),
        ("vvb", [64, D], BF16),
        ("qT", [64, T], BF16),
        ("summ_tok", [128, (T // 128) * SUMD], BF16),
    ]:
        Cc[n] = alloc(n, sh, dt_)
    C_TOP = cur[0]
    assert C_TOP <= SB_LIMIT

    Bb = {}
    cur[0] = PERM_TOP
    TG = 1024
    for (n, sh, dt_) in [
        ("x2f", [128, 8 * 512], F32),
        ("x2tok", [128, 4 * D], F32),
        ("h2", [128, D], F32),
        ("h2q", [128, 4 * D], BF16),
        ("h2qT", [128, 8 * TG], BF16),
        ("cb1rep", [128, D], F32),
        ("sbrep", [128, D], F32),
        ("q1", [128, D], F32),
        ("q2", [128, D], F32),
        ("q3", [128, D], F32),
        ("qm", [128, D], I32),
        ("qsc", [128, 128], F32),
        ("wgu", [128, 2 * 8 * 384], BF16),
        ("wguB", [128, 2 * 8 * 384], BF16),
        ("hq", [128, 22 * TG], BF16),
        ("hffn", [128, 3 * TG], F32),
        ("hffnB", [128, 3 * TG], F32),
        ("hfq", [128, 384], BF16),
        ("wd", [128, 22 * 128], BF16),
        ("outtok", [128, 512], F32),
        ("ftmp", [128, 512], F32),
    ]:
        Bb[n] = alloc(n, sh, dt_)
    B_TOP = cur[0]
    assert B_TOP <= SB_LIMIT, f"phase B overflow: {B_TOP}"

    PS = [es.enter_context(nc.psum_tensor(f"pb{i}", [128, 512], F32)) for i in range(8)]

    ctx = dict(nc=nc, p=p, g=g, Pm=Pm, Aa=Aa, Cc=Cc, Bb=Bb, PS=PS, T=T, NCH=NCH,
               NSUPS=NSUPS, NTILE=NTILE, dbgh=dbgh, SCPAD=SCPAD)

    ph = PHASES
    if ph >= 1:
        phase0(ctx)
    if ph >= 2:
        wquant(ctx)
    if ph >= 3:
        for s in range(NSUPS):
            frontend_sup(ctx, s)
            if ph >= 4:
                for ci in (0, 1):
                    ssd_chunk(ctx, 2 * s + ci)
                    if ph >= 5:
                        chunk_epilogue(ctx, 2 * s + ci)
    if ph >= 6:
        pool_exchange(ctx)
    if ph >= 7:
        phase_c(ctx)
    if ph >= 8:
        phase_b(ctx)

    with nc.Block() as block:
        p.finalize(block, sems)
    es.close()
    assert int(nc.sbuf_base) <= MAP0, \
        f"lazy sbuf alloc moved base into manual map: {nc.sbuf_base} > {MAP0}"
    return nc


def phase0(ctx):
    p, g, Pm, Aa, PS = ctx["p"], ctx["g"], ctx["Pm"], ctx["Aa"], ctx["PS"]

    for (ci_, v_) in ((0, EPS), (1, 1.0 / D), (2, 1.0 / DIN), (3, -1.0)):
        p.op("vector", lambda e, ci_=ci_, v_=v_: e.memset(Pm["kconst"][:, ci_:ci_ + 1], v_),
             writes=["kconst"], tag="kconst")
    p.op("vector", lambda e: e.memset(Pm["srow32"][0:1, 1572:1588], 0.0),
         writes=["scpad0"], tag="scpad0")
    p.dma("scalar", mkap(g["cc_sc_in"].ap(), ctx["T"], [(8, 1), (1, 8)]),
          Pm["srow32"][0:1, 1572:1580], reads=["scpad0"], writes=["scpad"],
          tag="st_scpad")
    p.dma("scalar", mkap(g["cc_pool_in"].ap(), NSLOT * SUMD, [(16, 1), (1, 16)]),
          Pm["srow32"][0:1, 1572:1588], reads=["scpad0"], writes=["poolpad"],
          tag="st_poolpad")
    for dst, src in [("ident", "c_ident"), ("ident32", "c_ident32"),
                     ("onescol", "c_onescol"), ("onesrow32", "c_onesrow32"),
                     ("tril", "c_tril"), ("iota64", "c_iota64"),
                     ("strictup", "c_strictup")]:
        p.dma("sync", Pm[dst][:, :], g[src].ap(), writes=["c_" + dst], tag="ldc")

    for k in range(8):
        p.dma("gpsimd", Aa["w_inproj"][:, k * DINPROJ:(k + 1) * DINPROJ],
              g["in_proj_w"][ts(k), :], writes=["winproj"], tag="ld_inp")
    for k in range(16):
        p.dma("gpsimd", Aa["w_outproj"][:, k * D:(k + 1) * D],
              g["out_proj_w"][ts(k), :], writes=["woutproj_raw"], tag="ld_outp")
    for k in range(8):
        p.dma("gpsimd", Aa["w_sc1"][:, k * SC_HID:(k + 1) * SC_HID],
              g["scorer_w1"][ts(k), :], writes=["wsc1"], tag="ld_sc1")
        p.dma("gpsimd", Aa["w_summ"][:, k * SUMD:(k + 1) * SUMD],
              g["summ_w"][ts(k), :], writes=["wsumm"], tag="ld_summ")
        p.dma("gpsimd", Aa["w_q"][:, k * SUMD:(k + 1) * SUMD],
              g["q_w"][ts(k), :], writes=["wqw"], tag="ld_q")
    load_cols(ctx, Aa["w_s2"], g["scorer_w2"].ap(), SC_HID, "ws2")
    p.dma("gpsimd", Pm["kw"][:, :], g["k_w"][:, :], writes=["wkw"], tag="ld_kw")
    for k in range(18):
        p.dma("sync", Pm["convw"][:, k * DCONV:(k + 1) * DCONV],
              g["conv_w"][ts(k), :], writes=["convw"], tag="ld_convw")
    load_cols(ctx, Pm["convb"], g["conv_b"].ap(), CONV_DIM, "convb")
    load_cols(ctx, Pm["normw"], g["norm_w"].ap(), D, "normw")
    load_cols(ctx, Pm["ssmw"], g["ssm_norm_w"].ap(), DIN, "ssmw")
    p.dma("sync", Pm["dtb"][:, :], mkap(g["dt_bias"].ap(), 0, [(1, 32), (1, 1)]),
          writes=["dtb"], tag="ld_dtb")
    p.dma("sync", Pm["cols"][:32, 0:1], mkap(g["A_log"].ap(), 0, [(1, 32), (1, 1)]),
          writes=["alog"], tag="ld_alog")
    p.dma("sync", Pm["cols"][:32, 1:2], mkap(g["Dvec"].ap(), 0, [(1, 32), (1, 1)]),
          writes=["Dv"], tag="ld_D")
    p.op("tensor", lambda e: e.transpose(PS[7][0:1, 0:32], Pm["cols"][:32, 1:2],
                                         Pm["ident32"][0:32, 0:32]),
         reads=["Dv", "c_ident32"], writes=psq(7, 0), tag="tr_D")
    p.op("vector", lambda e: e.tensor_copy(Pm["srow32"][0:1, 1664:1696], PS[7][0:1, 0:32]),
         reads=psq(7, 0), writes=["Drow"], tag="ev_D")
    p.op("tensor", lambda e: e.matmul(PS[7][:, 0:32], Pm["onesrow32"][:, :],
                                      Pm["srow32"][0:1, 1664:1696], start=True, stop=True),
         reads=["Drow", "c_onesrow32"], writes=psq(7, 0), tag="mm_Drep")
    p.op("vector", lambda e: e.tensor_copy(Pm["Dcol"][:, :], PS[7][:, 0:32]),
         reads=psq(7, 0), writes=["Dcol"], tag="ev_Drep")

    p.op("scalar", lambda e: e.activation(Pm["negA"][:, :], Pm["cols"][:32, 0:1], ACT.Exp,
                                          bias=0.0, scale=1.0),
         reads=["alog"], writes=["negA_"], tag="negA_exp")
    p.op("vector", lambda e: e.tensor_scalar(Pm["negA"][:, :], Pm["negA"][:, :], -1.0,
                                             None, ALU.mult),
         reads=["negA_"], writes=["negA"], tag="negA_neg")

    for k in range(16):
        p.op("vector", lambda e, k=k: e.tensor_scalar(
                Aa["w_outproj"][:, k * D:(k + 1) * D], Aa["w_outproj"][:, k * D:(k + 1) * D],
                Pm["ssmw"][:, k:k + 1], None, ALU.mult),
             reads=["woutproj_raw", "ssmw"], writes=["woutproj"], tag="fold_ssm")

    load_cols(ctx, _ColsView(Pm["cols"], 8), g["cvec"].ap(), D, "ccol")
    p.op("scalar", lambda e: e.activation(Pm["cols"][:, 8:16], Pm["cols"][:, 8:16],
                                          ACT.Silu, bias=0.0, scale=1.0),
         reads=["ccol"], writes=["siluc"], tag="siluc")
    p.op("vector", lambda e: e.tensor_copy(Pm["colsB"][:, 8:16], Pm["cols"][:, 8:16]),
         reads=["siluc"], writes=["silucB"], tag="silucB")

    modrow = Aa["yw"]
    for blk in range(3):
        for k in range(8):
            p.dma("gpsimd", Aa["wmodbuf"][:, :],
                  g["wmod"][ts(k), blk * 2048:(blk + 1) * 2048],
                  writes=["wmodbuf"], tag="ld_wmod")
            for j in range(4):
                p.op("tensor", lambda e, k=k, j=j: e.matmul(
                        PS[4 + j][0:1, :], Pm["colsB"][:, 8 + k:9 + k],
                        Aa["wmodbuf"][:, j * 512:(j + 1) * 512],
                        start=(k == 0), stop=(k == 7)),
                     reads=["wmodbuf", "silucB"], writes=[f"modps{j}"] + psq(4 + j, 0, 4), tag="mm_mod")
        for j in range(4):
            p.op("vector", lambda e, j=j: e.tensor_copy(
                    modrow[0:1, j * 512:(j + 1) * 512], PS[4 + j][0:1, :]),
                 reads=[f"modps{j}"] + psq(4 + j, 0, 4), writes=["modrow"],
                 tag="ev_mod")
        p.dma("scalar", mkap(g["mod_hbm"].ap(), blk * 2048, [(2048, 1), (1, 2048)]),
              modrow[0:1, :], reads=["modrow"], writes=[f"modhbm{blk}"], tag="st_mod")
    load_cols(ctx, _ColsView(Pm["modc"], 0), g["mod_hbm"].ap(), 6 * D, "modc",
              extra_reads=["modhbm0", "modhbm1", "modhbm2"])
    p.op("vector", lambda e: e.tensor_scalar(Pm["modc"][:, 8:16], Pm["modc"][:, 8:16],
                                             1.0, None, ALU.add),
         reads=["modc"], writes=["modc"], tag="ca1")
    p.op("vector", lambda e: e.tensor_scalar(Pm["modc"][:, 32:40], Pm["modc"][:, 32:40],
                                             1.0, None, ALU.add),
         reads=["modc"], writes=["modc"], tag="cb1")
    for c0 in (16, 40):
        cc = Pm["cols"]
        p.op("vector", lambda e, c0=c0: e.tensor_scalar(cc[:, 0:8], Pm["modc"][:, c0:c0 + 8],
                                                        0.25, None, ALU.max),
             reads=["modc"], writes=["gcols"], tag="gc1")
        p.op("vector", lambda e, c0=c0: e.tensor_scalar(cc[:, 8:16], Pm["modc"][:, c0:c0 + 8],
                                                        0.0, None, ALU.is_ge),
             reads=["modc", "gcols"], writes=["gcols"], tag="gc2")
        p.op("vector", lambda e: e.tensor_tensor(cc[:, 0:8], cc[:, 0:8], cc[:, 8:16],
                                                 ALU.mult),
             reads=["gcols"], writes=["gcols"], tag="gc3")
        p.op("vector", lambda e, c0=c0: e.tensor_scalar(cc[:, 8:16], Pm["modc"][:, c0:c0 + 8],
                                                        -0.25, None, ALU.min),
             reads=["modc", "gcols"], writes=["gcols"], tag="gc4")
        p.op("vector", lambda e, c0=c0: e.scalar_tensor_tensor(
                cc[:, 8:16], Pm["modc"][:, c0:c0 + 8], 0.0, cc[:, 8:16],
                ALU.is_lt, ALU.mult),
             reads=["modc", "gcols"], writes=["gcols"], tag="gc5")
        p.op("vector", lambda e, c0=c0: e.tensor_tensor(
                Pm["modc"][:, c0:c0 + 8], cc[:, 0:8], cc[:, 8:16], ALU.add),
             reads=["gcols"], writes=["modc"], tag="gc6")
    p.op("vector", lambda e: e.tensor_tensor(Pm["modc"][:, 48:56], Pm["modc"][:, 16:24],
                                             Pm["modc"][:, 8:16], ALU.mult),
         reads=["modc"], writes=["modc"], tag="gc1col")
    p.op("vector", lambda e: e.tensor_tensor(Pm["modc"][:, 56:64], Pm["modc"][:, 16:24],
                                             Pm["modc"][:, 0:8], ALU.mult),
         reads=["modc"], writes=["modc"], tag="gscol")




def apx(t, off, dims):
    pitch = t.shape[1] if hasattr(t, "shape") else None
    return AP(t, int(off), [[pitch, 128]] + [[int(s), int(n)] for (s, n) in dims])


def apxp(t, off, nparts, dims):
    pitch = t.shape[1]
    return AP(t, int(off), [[pitch, nparts]] + [[int(s), int(n)] for (s, n) in dims])


def rep16(t, col_off, nb):
    return apx(t, col_off, [(1, nb), (0, 16)])


def psq(i, q0, nq=1):
    return [f"ps{i}q{q}" for q in range(q0, q0 + nq)]


def dump(ctx, name, src_ap, reads):
    if name in ctx["dbgh"]:
        ctx["p"].dma("scalar", ctx["dbgh"][name].ap(), src_ap, reads=reads,
                     tag="dbg_" + name)


def quant_chain(ctx, eng, src, dst, ncols, scr, reads, wtag, dst_extra=()):
    p = ctx["p"]
    nb = ncols // 16
    (sct, bmo, sco, rco) = scr["sc"]
    u = wtag
    p.op(eng, lambda e: e.tensor_reduce(sct[:, bmo:bmo + nb],
                                        src.rearrange("p (b k) -> p b k", k=16),
                                        AX.X, ALU.max, apply_absolute_value=True),
         reads=reads, writes=[u + "bm"], tag=u + "bm")
    p.op(eng, lambda e: e.tensor_scalar(sct[:, sco:sco + nb], sct[:, bmo:bmo + nb],
                                        1.0 / 6.0, 1e-12, ALU.mult, ALU.max),
         reads=[u + "bm"], writes=[u + "sc"], tag=u + "sc")
    p.op("vector", lambda e: e.reciprocal(sct[:, rco:rco + nb], sct[:, sco:sco + nb]),
         reads=[u + "sc"], writes=[u + "rc"], tag=u + "rc")
    p.op(eng, lambda e: e.tensor_tensor(scr["xn"], src, rep16(sct, rco, nb), ALU.mult),
         reads=reads + [u + "rc"], writes=[u + "xn"], tag=u + "xn")
    p.op(eng, lambda e: e.scalar_tensor_tensor(scr["a"], scr["xn"], -1.0, scr["xn"],
                                               ALU.mult, ALU.max),
         reads=[u + "xn"], writes=[u + "a"], tag=u + "a")
    p.op(eng, lambda e: e.tensor_scalar(scr["acc"], scr["a"], MIDS[0], None, ALU.is_gt),
         reads=[u + "a"], writes=[u + "acc"], tag=u + "s0")
    for m in MIDS[1:]:
        p.op(eng, lambda e, m=m: e.scalar_tensor_tensor(scr["acc"], scr["a"], m,
                                                        scr["acc"], ALU.is_gt, ALU.add),
             reads=[u + "a"], writes=[u + "acc"], tag=u + "s")
    for (m, w) in ((2.5, 2.0), (3.5, 2.0), (5.0, 4.0)):
        p.op(eng, lambda e, m=m, w=w: e.tensor_scalar(scr["hi"], scr["a"], m, w,
                                                      ALU.is_gt, ALU.mult),
             reads=[u + "a"], writes=[u + "hi"], tag=u + "h")
        p.op(eng, lambda e: e.tensor_tensor(scr["acc"], scr["acc"], scr["hi"], ALU.add),
             reads=[u + "acc", u + "hi"], writes=[u + "acc"], tag=u + "ha")
    p.op(eng, lambda e: e.tensor_scalar(scr["m"], scr["xn"], 0.0, None, ALU.is_lt),
         reads=[u + "xn"], writes=[u + "m"], tag=u + "m")
    p.op(eng, lambda e: e.scalar_tensor_tensor(scr["xn"], scr["acc"], 0.5,
                                               rep16(sct, sco, nb), ALU.mult, ALU.mult),
         reads=[u + "acc", u + "sc"], writes=[u + "qv", u + "xn"], tag=u + "qv")
    p.op(eng, lambda e: e.tensor_scalar(scr["hi"], scr["xn"], -1.0, None, ALU.mult),
         reads=[u + "qv"], writes=[u + "nq", u + "hi"], tag=u + "nq")
    p.op("scalar", lambda e: e.copy(dst, scr["xn"]),
         reads=[u + "qv"], writes=[wtag] + list(dst_extra), tag=u + "sel0")
    p.op(eng, lambda e: e.copy_predicated(dst, scr["m"], scr["hi"]),
         reads=[u + "nq", u + "m", wtag], writes=[wtag] + list(dst_extra),
         tag=u + "sel")



class _ColsView:
    def __init__(self, t, c0):
        self.t, self.c0 = t, c0

    def __getitem__(self, key):
        rows, cols = key
        c = slice(cols.start + self.c0, cols.stop + self.c0)
        return self.t[rows, c]


def load_cols(ctx, dst, src_flat_ap, n, bufw, after=(), extra_reads=()):
    p, Aa, Pm, PS = ctx["p"], ctx["Aa"], ctx["Pm"], ctx["PS"]
    nrow = (n + 1023) // 1024
    stage = Aa["csrep"]
    for r in range(nrow):
        w = min(1024, n - r * 1024)
        p.dma("sync", stage[0:1, 0:w],
              mkap(src_flat_ap, r * 1024, [(w, 1), (1, w)]),
              reads=list(extra_reads), writes=[bufw + "stg", "csrep"], after=after,
              tag=bufw + "stg")
        for cj in range(w // 128):
            j = r * 8 + cj
            q = j % 4
            p.op("tensor", lambda e, cj=cj, q=q: e.transpose(
                    PS[7][:, 256 + q * 32:256 + q * 32 + 1],
                    stage[0:1, cj * 128:(cj + 1) * 128], Pm["ident32"][0:1, 0:1]),
                 reads=[bufw + "stg", "csrep", "c_ident32"], writes=psq(7, 0, 4),
                 tag=bufw + "tr")
            p.op("vector", lambda e, j=j, q=q: e.tensor_copy(
                    dst[:, j:j + 1], PS[7][:, 256 + q * 32:256 + q * 32 + 1]),
                 reads=psq(7, 2 + q // 2), writes=[bufw], tag=bufw + "ev")


def wquant(ctx):
    p, g, Pm, Aa, PS = ctx["p"], ctx["g"], ctx["Pm"], ctx["Aa"], ctx["PS"]
    WQ = WQ_SHARD_ELEMS

    def one_tile(u, src_hbm_ap, ncols, dst_dma_fn, reads_extra):
        src = Aa["x2pT"][:, 0:ncols]
        p.dma("sync", src, src_hbm_ap, writes=[u + "src", "wq_src"], tag=u + "ld")
        if WQL < 2:
            return
        scr = dict(xn=Aa["csrep"][:, 0:ncols], a=Aa["yw"][:, 0:ncols],
                   acc=Aa["yw"][:, 1024:1024 + ncols],
                   hi=Aa["x2pT"][:, 1024:1024 + ncols],
                   sc=(Aa["xsD"], 0, 64, 128),
                   m=Aa["xst"].bitcast(I32)[:, 0:ncols])
        quant_chain(ctx, "vector", src, Aa["ycore"][:, 0:ncols], ncols, scr,
                    [u + "src", "wq_src"] + reads_extra, u, dst_extra=("wq_dst",))
        if WQL < 3:
            return
        ntile = ncols // 128
        for b0 in range(0, ntile, 4):
            nb = min(4, ntile - b0)
            for j in range(nb):
                i = b0 + j
                p.op("tensor", lambda e, i=i, j=j: e.transpose(
                        PS[0].bitcast(BF16)[:, j * 128:j * 128 + 128],
                        Aa["ycore"][:, i * 128:(i + 1) * 128], Pm["ident"][:, :]),
                     reads=[u, "wq_dst", "c_ident"], writes=[f"ps0q{j}", "wqbank"],
                     tag=u + "tr")
            if WQL >= 4:
                p.op("scalar", lambda e, b0=b0, nb=nb: e.copy(
                        Aa["ztil"][:, b0 * 128:(b0 + nb) * 128],
                        PS[0].bitcast(BF16)[:, 0:nb * 128]),
                     reads=[f"ps0q{j}" for j in range(nb)] + ["wqbank"],
                     writes=[u + "tT"] + ["wq_ztil" + str(b0 + j) for j in range(nb)],
                     tag=u + "ev")
            if WQL >= 5:
                for j in range(nb):
                    dst_dma_fn(b0 + j)

    for (mat, base) in (("wg", 0), ("wu", 384 * D)):
        for r in range(3):
            u = f"q{mat}{r}"

            def dst_fn(i, r=r, base=base, u=u):
                p.dma("scalar",
                      mkap(g["cc_wq_in"].ap(), base + i * 128 * 384 + r * 128,
                           [(384, 128), (1, 128)]),
                      Aa["ztil"][:, i * 128:(i + 1) * 128],
                      reads=[u + "tT", "wq_ztil" + str(i)], writes=["wqin"],
                      tag=u + "st")

            src_hbm = g[mat[0:2] + "_shard" if False else ("wg_shard" if mat == "wg" else "wu_shard")][ts(r), :]
            one_tile(u, src_hbm, D, dst_fn, [])
    wdbase = 2 * 384 * D
    for (si, (c0, w)) in enumerate(((0, 1024), (1024, 1024), (2048, 768))):
        u = f"qwd{si}"

        def dst_fn(i, c0=c0, u=u):
            icc = (c0 // 128) + i
            p.dma("scalar",
                  mkap(g["cc_wq_in"].ap(), wdbase + icc * 128 * 128, [(128, 128), (1, 128)]),
                  Aa["ztil"][:, i * 128:(i + 1) * 128],
                  reads=[u + "tT", "wq_ztil" + str(i)], writes=["wqin"],
                  tag=u + "st")

        one_tile(u, g["wd_shard"][:, c0:c0 + w], w, dst_fn, [])

    _last = p.ops[-1]
    for r in ("csrep", "yw", "x2pT", "xsD", "xst", "ycore", "ztil"):
        p.buf(r).readers.append(_last)
    p.coll(lambda e: e.collective_compute(
            "AllGather", ALU.bypass, replica_groups=[list(range(NCORES))],
            ins=[g["cc_wq_in"].ap().opt()], outs=[g["cc_wq_out"].ap().opt()]),
           reads=["wqin"], writes=["wqout"], tag="ag_wq")


def frontend_sup(ctx, s):
    p, g, Pm, Aa, PS = ctx["p"], ctx["g"], ctx["Pm"], ctx["Aa"], ctx["PS"]
    T = ctx["T"]
    if s == 0:
        p.op("vector", lambda e: e.memset(Pm["halo"][:, :], 0.0), writes=["halo_all"],
             tag="halo0")
        p.op("vector", lambda e: e.memset(Pm["S"][:, :], 0.0), writes=["S"], tag="S0")
        for cc in range(18):
            p.buf(f"halo{cc}").writer = p.bufs["halo_all"].writer

    C = Pm["cols"]
    for tt in range(2):
        j = 2 * s + tt
        xb = "xtok"
        p.dma("sync", Aa["x_tok"][:, :], g["x"][ts(j), :], writes=[xb, "u" + xb],
              tag="ld_x")
        p.op("scalar", lambda e: e.activation(Aa["x_tok"][:, :], Aa["x_tok"][:, :],
                                              ACT.Identity, bias=0.0, scale=1.0,
                                              accum_out=C[:, 0:1]),
             reads=[xb], writes=[xb, "lnsum"], tag="ln_s1")
        p.op("scalar", lambda e, tt=tt: e.activation(
                Aa["ztil"][:, tt * 1024:(tt + 1) * 1024], Aa["x_tok"][:, :],
                ACT.Square, bias=0.0, scale=1.0, accum_out=C[:, 1:2]),
             reads=[xb], writes=["lnsq", "ztil"], tag="ln_s2")
        p.op("vector", lambda e: e.tensor_scalar(C[:, 2:3], C[:, 0:1], 1.0 / D, None,
                                                 ALU.mult),
             reads=["lnsum"], writes=["lnm"], tag="ln_m")
        p.op("vector", lambda e: e.tensor_tensor(C[:, 3:4], C[:, 2:3], C[:, 2:3],
                                                 ALU.mult),
             reads=["lnm"], writes=["lnm2"], tag="ln_m2")
        p.op("vector", lambda e: e.scalar_tensor_tensor(C[:, 4:5], C[:, 1:2], 1.0 / D,
                                                        C[:, 3:4], ALU.mult, ALU.subtract),
             reads=["lnsq", "lnm2"], writes=["lnvar"], tag="ln_var")
        p.op("scalar", lambda e: e.activation(C[:, 5:6], C[:, 4:5], ACT.Sqrt,
                                              bias=Pm["kconst"][:, 0:1], scale=1.0),
             reads=["lnvar", "kconst"], writes=["lnsd"], tag="ln_sd")
        p.op("vector", lambda e: e.reciprocal(C[:, 6:7], C[:, 5:6]),
             reads=["lnsd"], writes=["lnrstd"], tag="ln_rstd")
        if FRL == 11:
            continue
        for k in range(8):
            bk, qt = 4 + k // 4, k % 4
            p.op("tensor", lambda e, k=k, bk=bk, qt=qt: e.transpose(
                    PS[bk][:, qt * 128:(qt + 1) * 128],
                    Aa["x_tok"][:, k * 128:(k + 1) * 128], Pm["ident32"][:, :]),
                 reads=[xb, "c_ident32"], writes=psq(bk, 0, 4), tag="tr_x")
        if FRL == 12:
            continue
        p.op("vector", lambda e: e.tensor_scalar(Aa["x_tok"][:, :], Aa["x_tok"][:, :],
                                                 C[:, 2:3], C[:, 6:7],
                                                 ALU.subtract, ALU.mult),
             reads=[xb, "lnm", "lnrstd"], writes=["u" + xb, xb], tag="ln_u")
        for k in range(8):
            bk, qt = 6 + k // 4, k % 4
            p.op("tensor", lambda e, k=k, bk=bk, qt=qt: e.transpose(
                    PS[bk][:, qt * 128:(qt + 1) * 128],
                    Aa["x_tok"][:, k * 128:(k + 1) * 128], Pm["ident32"][:, :]),
                 reads=["u" + xb, "c_ident32"], writes=psq(bk, 0, 4), tag="tr_u")
        if FRL == 13:
            continue
        for k in range(8):
            bku, qtu = 6 + k // 4, k % 4
            bkx, qtx = 4 + k // 4, k % 4
            hsl = Aa["hT"][:, k * SUP + tt * 128: k * SUP + tt * 128 + 128]
            x2sl = Aa["x2pT"][:, k * SUP + tt * 128: k * SUP + tt * 128 + 128]
            p.op("scalar", lambda e, k=k, bku=bku, qtu=qtu, hsl=hsl: e.activation(
                    hsl, PS[bku][:, qtu * 128:(qtu + 1) * 128], ACT.Identity,
                    bias=Pm["modc"][:, k:k + 1], scale=Pm["modc"][:, 8 + k:9 + k]),
                 reads=psq(bku, qtu) + ["modc"], writes=["hT"], tag="h_ev")
            if FRL == 14:
                continue
            p.op("scalar", lambda e, k=k, tt=tt, bku=bku, qtu=qtu: e.activation(
                    Aa["ztil"][:, 2048 + k * SUP + tt * 128: 2048 + k * SUP + tt * 128 + 128],
                    PS[bku][:, qtu * 128:(qtu + 1) * 128], ACT.Square,
                    bias=Pm["modc"][:, k:k + 1], scale=Pm["modc"][:, 8 + k:9 + k]),
                 reads=psq(bku, qtu) + ["modc"], writes=["ztil"], tag="h_sq")
            if FRL == 15:
                continue
            p.op("scalar", lambda e, k=k, bkx=bkx, qtx=qtx, x2sl=x2sl: e.copy(
                    x2sl, PS[bkx][:, qtx * 128:(qtx + 1) * 128]),
                 reads=psq(bkx, qtx), writes=["x2pT"], tag="x2p0")
            if FRL == 16:
                continue
            p.op("vector", lambda e, k=k, hsl=hsl, x2sl=x2sl:
                 e.scalar_tensor_tensor(
                    x2sl, hsl, Pm["modc"][:, 16 + k:17 + k], x2sl,
                    ALU.mult, ALU.add),
                 reads=["hT", "modc", "x2pT"], writes=["x2pT"], tag="x2p1")
    if FRL < 2:
        return
    for k in range(8):
        p.op("tensor", lambda e, k=k: e.matmul(
                PS[0][0:1, 0:256], Pm["onescol"][:, :],
                Aa["ztil"][:, 2048 + k * SUP: 2048 + (k + 1) * SUP],
                start=(k == 0), stop=(k == 7)),
             reads=["ztil", "c_onescol"], writes=psq(0, 0, 4), tag="mm_rms")
    p.op("scalar", lambda e: e.activation(Pm["srow32"][0:1, 640:896], PS[0][0:1, 0:256],
                                          ACT.Sqrt, bias=Pm["kconst"][0:1, 0:1],
                                          scale=Pm["kconst"][0:1, 1:2]),
         reads=psq(0, 0, 2) + ["kconst"], writes=["rmsd"], tag="rms_sd")
    p.op("vector", lambda e: e.reciprocal(Pm["srow32"][0:1, 896:1152],
                                          Pm["srow32"][0:1, 640:896]),
         reads=["rmsd"], writes=["rmsr"], tag="rms_r")
    p.op("tensor", lambda e: e.matmul(PS[1][:, 0:256], Pm["onesrow32"][:, :],
                                      Pm["srow32"][0:1, 896:1152], start=True, stop=True),
         reads=["rmsr", "c_onesrow32"], writes=psq(1, 0, 2), tag="mm_rrep")
    for k in range(8):
        p.op("vector", lambda e, k=k: e.tensor_tensor(
                Aa["hT"][:, k * SUP:(k + 1) * SUP], Aa["hT"][:, k * SUP:(k + 1) * SUP],
                PS[1][:, 0:256], ALU.mult),
             reads=["hT"] + psq(1, 0, 2), writes=["hT"], tag="xn1")
        p.op("vector", lambda e, k=k: e.tensor_scalar(
                Aa["hT"][:, k * SUP:(k + 1) * SUP], Aa["hT"][:, k * SUP:(k + 1) * SUP],
                Pm["normw"][:, k:k + 1], None, ALU.mult),
             reads=["hT", "normw"], writes=["hT"], tag="xn2")
    if s == 0:
        dump(ctx, "hT", Aa["hT"][:, :], ["hT"])
        dump(ctx, "xnT", Aa["hT"][:, :], ["hT"])

    if FRL < 3:
        return
    for oc in range(35):
        bk = 2 + oc % 6
        wdt = 128 if oc < 34 else 32
        pslice = PS[bk][0:wdt, 0:256]
        pb = psq(bk, 0, 4)
        for k in range(8):
            p.op("tensor", lambda e, oc=oc, k=k, pslice=pslice, wdt=wdt: e.matmul(
                    pslice, Aa["w_inproj"][:, k * DINPROJ + oc * 128:
                                           k * DINPROJ + oc * 128 + wdt],
                    Aa["hT"][:, k * SUP:(k + 1) * SUP],
                    start=(k == 0), stop=(k == 7)),
                 reads=["hT", "winproj"], writes=pb, tag=f"mm_ip{oc}")
        if FRL < 4:
            continue
        if oc < 16:
            p.op("scalar", lambda e, oc=oc, pslice=pslice: e.activation(
                    Aa["ztil"][:, oc * SUP:(oc + 1) * SUP], pslice, ACT.Silu,
                    bias=0.0, scale=1.0),
                 reads=pb, writes=["ztil"], tag="z_silu")
        elif oc < 34:
            cc = oc - 16
            acc = Aa["acc0"]
            accb = "acc0"
            wv = Pm["convw"]
            p.op("vector", lambda e, cc=cc, acc=acc, pslice=pslice: e.tensor_scalar(
                    acc[:, 0:SUP], pslice, wv[:, cc * 4 + 3:cc * 4 + 4],
                    Pm["convb"][:, cc:cc + 1], ALU.mult, ALU.add),
                 reads=pb + ["convw", "convb"], writes=[accb], tag="cv3")
            for kk in range(3):
                dsh = 3 - kk
                p.op("vector", lambda e, cc=cc, kk=kk, dsh=dsh, acc=acc, pslice=pslice:
                     e.scalar_tensor_tensor(
                        acc[:, dsh:SUP], pslice[:, 0:SUP - dsh],
                        wv[:, cc * 4 + kk:cc * 4 + kk + 1], acc[:, dsh:SUP],
                        ALU.mult, ALU.add),
                     reads=pb + ["convw", accb], writes=[accb], tag=f"cv{kk}")
                p.op("vector", lambda e, cc=cc, kk=kk, dsh=dsh, acc=acc:
                     e.scalar_tensor_tensor(
                        acc[:, 0:dsh], Pm["halo"][:, cc * 3 + kk:cc * 3 + kk + dsh],
                        wv[:, cc * 4 + kk:cc * 4 + kk + 1], acc[:, 0:dsh],
                        ALU.mult, ALU.add),
                     reads=["convw", f"halo{cc}", accb], writes=[accb], tag=f"cvh{kk}")
            p.op("vector", lambda e, cc=cc, pslice=pslice: e.tensor_copy(
                    Pm["halo"][:, cc * 3:cc * 3 + 3], pslice[:, SUP - 3:SUP]),
                 reads=pb, writes=[f"halo{cc}"], tag="cv_halo")
            if cc < 16:
                dst = Aa["xst"][:, cc * SUP:(cc + 1) * SUP]
                wb = "xst"
            else:
                dst = Aa["bct"][:, (cc - 16) * SUP:(cc - 15) * SUP]
                wb = "bct"
            p.op("scalar", lambda e, acc=acc, dst=dst: e.activation(
                    dst, acc[:, 0:SUP], ACT.Silu, bias=0.0, scale=1.0),
                 reads=[accb], writes=[wb], tag="cv_silu")
        else:
            p.op("scalar", lambda e, pslice=pslice: e.activation(
                    Pm["srow32"][:, 0:SUP], pslice, ACT.Exp,
                    bias=Pm["dtb"][:, :], scale=1.0),
                 reads=pb + ["dtb"], writes=["dtexp"], tag="dt_exp")
            p.op("scalar", lambda e: e.activation(Aa["dsup"][:, :], Pm["srow32"][:, 0:SUP],
                                                  ACT.Ln, bias=1.0, scale=1.0),
                 reads=["dtexp"], writes=["dsup"], tag="dt_ln")
    if FRL < 5:
        return
    for ci in range(2):
        c = 2 * s + ci
        b0 = ci * L
        p.op("vector", lambda e, b0=b0: e.tensor_scalar(
                Pm["srow32"][:, 256:256 + L], Aa["dsup"][:, b0:b0 + L],
                Pm["negA"][:, :], None, ALU.mult),
             reads=["dsup", "negA"], writes=["dA"], tag="dA")
        p.op("vector", lambda e, b0=b0: e.tensor_tensor_scan(
                Aa["cssup"][:, b0:b0 + L], Pm["srow32"][:, 256:256 + L],
                Pm["srow32"][:, 256:256 + L], 0.0, ALU.add, ALU.bypass),
             reads=["dA"], writes=["cssup"], tag="cs")
        p.op("scalar", lambda e, b0=b0: e.activation(
                Aa["essup"][:, b0:b0 + L], Aa["cssup"][:, b0:b0 + L], ACT.Exp,
                bias=0.0, scale=1.0),
             reads=["cssup"], writes=["essup"], tag="ecs")
        p.op("vector", lambda e, b0=b0: e.tensor_copy(
                Pm["srow32"][:, 384:385], Aa["cssup"][:, b0 + L - 1:b0 + L]),
             reads=["cssup"], writes=["ncl"], tag="ncl")
        p.op("scalar", lambda e, b0=b0: e.activation(
                Aa["dssup"][:, b0:b0 + L], Aa["cssup"][:, b0:b0 + L], ACT.Exp,
                bias=Pm["srow32"][:, 384:385], scale=Pm["kconst"][0:32, 3:4]),
             reads=["cssup", "ncl", "kconst"], writes=["dssup"], tag="ds")
        p.op("scalar", lambda e, b0=b0: e.activation(
                Pm["srow32"][:, 385:386], Aa["cssup"][:, b0 + L - 1:b0 + L], ACT.Exp,
                bias=0.0, scale=1.0),
             reads=["cssup"], writes=["dcc"], tag="dc")
        p.op("vector", lambda e, b0=b0: e.tensor_tensor(
                Pm["srow32"][:, 512:512 + L], Aa["dsup"][:, b0:b0 + L],
                Aa["dssup"][:, b0:b0 + L], ALU.mult),
             reads=["dsup", "dssup"], writes=["ddt"], tag="ddt")
        p.dma("scalar", mkap(g["hbm_rows"].ap(), c * 8192, [(128, 32), (1, 128)]),
              Aa["cssup"][:, b0:b0 + L], reads=["cssup"], writes=[f"cshbm{c}"], tag="st_cs")
        p.dma("scalar", mkap(g["hbm_rows"].ap(), c * 8192 + 4096, [(128, 32), (1, 128)]),
              Aa["essup"][:, b0:b0 + L], reads=["essup"], writes=[f"eshbm{c}"], tag="st_es")
    if s == 0:
        dump(ctx, "xst", Aa["xst"][:, :], ["xst"])
        dump(ctx, "bct", Aa["bct"][:, :], ["bct"])
        dump(ctx, "dsup", Aa["dsup"][:, :], ["dsup"])
        dump(ctx, "cssup", Aa["cssup"][:, :], ["cssup"])


def ssd_chunk(ctx, c):
    p, g, Pm, Aa, PS = ctx["p"], ctx["g"], ctx["Pm"], ctx["Aa"], ctx["PS"]
    s, ci = c // 2, c % 2
    b0 = ci * L

    srcs = [(Aa["cssup"][:, b0:b0 + L], "cssup", Aa["cstok"], "cstok"),
            (Aa["dsup"][:, b0:b0 + L], "dsup", Aa["dttok"], "dttok"),
            (Aa["dssup"][:, b0:b0 + L], "dssup", Aa["dstok"], "dstok"),
            (Pm["srow32"][:, 512 + ci * L:512 + ci * L + L] if False else
             Pm["srow32"][:, 512:512 + L], "ddt", Aa["ddtok"], "ddtok")]
    p.op("vector", lambda e: e.tensor_tensor(
            Pm["srow32"][:, 512:512 + L], Aa["dsup"][:, b0:b0 + L],
            Aa["dssup"][:, b0:b0 + L], ALU.mult),
         reads=["dsup", "dssup"], writes=["ddt"], tag="ddt")
    for (i, (srcap, rb, dstt, db)) in enumerate(srcs):
        p.op("tensor", lambda e, srcap=srcap, i=i: e.transpose(
                PS[7][:, i * 128:i * 128 + 32], srcap, Pm["ident32"][0:32, 0:32]),
             reads=[rb, "c_ident32"], writes=psq(7, 0, 4), tag="tr_row")
        p.op("vector", lambda e, dstt=dstt, i=i: e.tensor_copy(
                dstt[:, :], PS[7][:, i * 128:i * 128 + 32]),
             reads=psq(7, i), writes=[db], tag="ev_row")
    p.op("scalar", lambda e: e.activation(
            Pm["srow32"][:, 385:386], Aa["cssup"][:, b0 + L - 1:b0 + L], ACT.Exp,
            bias=0.0, scale=1.0),
         reads=["cssup"], writes=["dcc"], tag="dc")
    p.op("tensor", lambda e: e.transpose(PS[7][0:1, 448:480], Pm["srow32"][:, 385:386],
                                         Pm["ident32"][0:32, 0:32]),
         reads=["dcc", "c_ident32"], writes=psq(7, 0, 4), tag="tr_dc")
    p.op("vector", lambda e: e.tensor_copy(Pm["srow32"][0:1, 1408:1440],
                                           PS[7][0:1, 448:480]),
         reads=psq(7, 3), writes=["dcrow"], tag="ev_dc")
    p.op("tensor", lambda e: e.matmul(PS[7][:, 448:480], Pm["onesrow32"][:, :],
                                      Pm["srow32"][0:1, 1408:1440], start=True, stop=True),
         reads=["dcrow", "c_onesrow32"], writes=psq(7, 0, 4), tag="mm_dcrep")
    p.op("vector", lambda e: e.tensor_copy(Pm["dcrep"][:, :], PS[7][:, 448:480]),
         reads=psq(7, 3), writes=["dcrep"], tag="ev_dcrep")

    p.op("tensor", lambda e: e.matmul(PS[0][:, 0:128],
                                      Aa["bct"][:, ci * L:ci * L + L],
                                      Aa["bct"][:, SUP + ci * L:SUP + ci * L + L],
                                      start=True, stop=True),
         reads=["bct"], writes=psq(0, 0, 2), tag="mm_G")
    p.op("vector", lambda e: e.tensor_tensor(Aa["Gt"][:, :], PS[0][:, 0:128],
                                             Pm["tril"][:, :], ALU.mult),
         reads=psq(0, 0) + ["c_tril"], writes=["Gt"], tag="ev_G")
    p.op("tensor", lambda e: e.transpose(PS[0].bitcast(BF16)[:, 256:384],
                                         Aa["bct"][:, ci * L:ci * L + L],
                                         Pm["ident"][:, :]),
         reads=["bct", "c_ident"], writes=psq(0, 0, 2), tag="tr_btok")
    p.op("vector", lambda e: e.tensor_copy(Aa["btok"][:, :],
                                           PS[0].bitcast(BF16)[:, 256:384]),
         reads=psq(0, 1), writes=["btok"], tag="ev_btok")
    for w in range(4):
        for i in range(4):
            kk = 4 * w + i
            p.op("tensor", lambda e, kk=kk, i=i: e.transpose(
                    PS[1].bitcast(BF16)[:, i * 128:i * 128 + 128],
                    Aa["xst"][:, kk * SUP + ci * L: kk * SUP + ci * L + L],
                    Pm["ident"][:, :]),
                 reads=["xst", "c_ident"], writes=psq(1, i // 2), tag="tr_xs")
        p.op("vector", lambda e, w=w: e.tensor_tensor(
                Aa["xd"][:, w * 512:(w + 1) * 512],
                PS[1].bitcast(BF16)[:, 0:512],
                apx(Aa["dttok"], 8 * w, [(1, 8), (0, 64)]), ALU.mult),
             reads=psq(1, 0, 2) + ["dttok"], writes=["xd"], tag="ev_xd")
        p.op("vector", lambda e, w=w: e.tensor_tensor(
                Aa["xdp"][:, w * 512:(w + 1) * 512], Aa["xd"][:, w * 512:(w + 1) * 512],
                apx(Aa["dstok"], 8 * w, [(1, 8), (0, 64)]), ALU.mult),
             reads=["xd", "dstok"], writes=["xdp"], tag="ev_xdp")

    for qh in range(4):
        p.dma("sync", Aa["csrep"][:, :],
              mkap(g["hbm_rows"].ap(), c * 8192 + qh * 1024,
                   [(0, 128), (128, 8), (1, 128)]),
              reads=[f"cshbm{c}"], writes=["csrep"], tag="bc_cs")
        for eh in range(2):
            p.dma("sync", Aa["erep2"][eh * 64:(eh + 1) * 64, :],
                  mkap(g["hbm_rows"].ap(), c * 8192 + 4096 + qh * 1024 + eh * 128,
                       [(0, 64), (256, 4), (1, 128)]),
                  reads=[f"eshbm{c}"], writes=["erep2"], tag="bc_e")
        p.op("vector", lambda e, qh=qh: e.tensor_tensor(
                Aa["csrep"][:, :], Aa["csrep"][:, :],
                apx(Aa["cstok"], qh * 8, [(1, 8), (0, 128)]), ALU.subtract),
             reads=["csrep", "cstok"], writes=["csrep"], tag="seg")
        p.op("vector", lambda e: e.tensor_scalar(Aa["csrep"][:, :], Aa["csrep"][:, :],
                                                 0.0, None, ALU.min),
             reads=["csrep"], writes=["csrep"], tag="seg_cl")
        p.op("scalar", lambda e: e.activation(Aa["csrep"][:, :], Aa["csrep"][:, :],
                                              ACT.Exp, bias=0.0, scale=1.0),
             reads=["csrep"], writes=["csrep"], tag="seg_exp")
        p.op("vector", lambda e: e.tensor_tensor(
                Aa["Mt"][:, :], Aa["csrep"][:, :],
                apx(Aa["Gt"], 0, [(0, 8), (1, 128)]), ALU.mult),
             reads=["csrep", "Gt"], writes=["Mt"], tag="M")
        for i in range(4):
            cc = qh * 4 + i
            for half in range(2):
                pr = slice(half * 64, half * 64 + 64)
                p.op("vector", lambda e, cc=cc, i=i, pr=pr, half=half: e.tensor_scalar(
                        Aa["xsD"][pr, i * 128:(i + 1) * 128],
                        Aa["xst"][pr, cc * SUP + ci * L: cc * SUP + ci * L + L],
                        Pm["Dcol"][pr, 2 * cc + half:2 * cc + half + 1], None, ALU.mult),
                     reads=["xst", "Dcol"], writes=["xsD"], tag="xsD")
        for j in range(4):
            jj = qh * 4 + j
            for half in range(2):
                h = 2 * jj + half
                if c > 0:
                    p.op("tensor", lambda e, h=h, j=j, half=half: e.matmul(
                            PS[2][half * 64:half * 64 + 64, j * 128:(j + 1) * 128],
                            Aa["Sb"][:, h * 64:(h + 1) * 64],
                            Aa["bct"][:, SUP + ci * L:SUP + ci * L + L],
                            start=True, stop=True, tile_position=(0, half * 64)),
                         reads=["Sb", "bct"], writes=psq(2, j), tag="mm_yo")
                p.op("tensor", lambda e, h=h, j=j, half=half: e.matmul(
                        PS[4][half * 64:half * 64 + 64, j * 128:(j + 1) * 128],
                        Aa["xd"][:, h * 64:(h + 1) * 64],
                        Aa["Mt"][:, j * 256 + half * 128: j * 256 + half * 128 + 128],
                        start=True, stop=True, tile_position=(0, half * 64)),
                     reads=["xd", "Mt"], writes=psq(4, j), tag="mm_yd")
        ysl = Aa["yw"][:, qh * 512:(qh + 1) * 512]
        if c > 0:
            p.op("vector", lambda e, ysl=ysl: e.tensor_tensor(
                    ysl, PS[2][:, :], Aa["erep2"][:, :], ALU.mult),
                 reads=psq(2, 0, 4) + ["erep2"], writes=["yw"], tag="as_yo")
            p.op("vector", lambda e, ysl=ysl: e.tensor_tensor(
                    ysl, ysl, PS[4][:, :], ALU.add),
                 reads=psq(4, 0, 4) + ["yw"], writes=["yw"], tag="as_yd")
        else:
            p.op("vector", lambda e, ysl=ysl: e.tensor_copy(ysl, PS[4][:, :]),
                 reads=psq(4, 0, 4), writes=["yw"], tag="as_yd0")
        p.op("vector", lambda e, ysl=ysl: e.tensor_tensor(
                ysl, ysl, Aa["xsD"][:, :], ALU.add),
             reads=["xsD", "yw"], writes=["yw"], tag="as_xsD")
        p.op("vector", lambda e, ysl=ysl, qh=qh: e.tensor_tensor(
                ysl, ysl,
                apx(Aa["ztil"], (qh * 4) * SUP + ci * L, [(SUP, 4), (1, 128)]),
                ALU.mult),
             reads=["ztil", "yw"], writes=["yw"], tag="as_z")
    p.op("scalar", lambda e: e.activation(Aa["hT"][:, :], Aa["yw"][:, :], ACT.Square,
                                          bias=0.0, scale=1.0),
         reads=["yw"], writes=["hT"], tag="ysq")
    for i in range(16):
        p.op("tensor", lambda e, i=i: e.matmul(
                PS[6][0:1, 0:128], Pm["onescol"][:, :],
                Aa["hT"][:, i * 128:(i + 1) * 128],
                start=(i == 0), stop=(i == 15)),
             reads=["hT", "c_onescol"], writes=psq(6, 0, 4), tag="mm_ysq")
    p.op("scalar", lambda e: e.activation(Pm["srow32"][0:1, 1152:1280], PS[6][0:1, 0:128],
                                          ACT.Sqrt, bias=Pm["kconst"][0:1, 0:1],
                                          scale=Pm["kconst"][0:1, 2:3]),
         reads=psq(6, 0) + ["kconst"], writes=["yrms"], tag="yrms_sd")
    p.op("vector", lambda e: e.reciprocal(Pm["srow32"][0:1, 1280:1408],
                                          Pm["srow32"][0:1, 1152:1280]),
         reads=["yrms"], writes=["yrmsr"], tag="yrms_r")
    p.op("tensor", lambda e: e.matmul(PS[6][:, 128:256], Pm["onesrow32"][:, :],
                                      Pm["srow32"][0:1, 1280:1408], start=True, stop=True),
         reads=["yrmsr", "c_onesrow32"], writes=psq(6, 1), tag="mm_yrrep")
    p.op("vector", lambda e: e.tensor_copy(Aa["rstd_rep"][:, :], PS[6][:, 128:256]),
         reads=psq(6, 1), writes=["rstd_rep"], tag="ev_yrrep")
    p.op("vector", lambda e: e.tensor_tensor(
            Aa["ycore"][:, :], Aa["yw"][:, :],
            apx(Aa["rstd_rep"], 0, [(0, 16), (1, 128)]), ALU.mult),
         reads=["yw", "rstd_rep"], writes=["ycore"], tag="ycore")
    for w in range(4):
        bk = 2 + w
        p.op("tensor", lambda e, w=w, bk=bk: e.matmul(
                PS[bk][:, :], Aa["btok"][:, :], Aa["xdp"][:, w * 512:(w + 1) * 512],
                start=True, stop=True),
             reads=["btok", "xdp"], writes=psq(bk, 0, 4), tag="mm_st")
        p.op("vector", lambda e, w=w: e.tensor_tensor(
                Pm["S"][:, w * 512:(w + 1) * 512], Pm["S"][:, w * 512:(w + 1) * 512],
                apx(Pm["dcrep"], 8 * w, [(1, 8), (0, 64)]), ALU.mult),
             reads=["dcrep", "S"], writes=["S"], tag="S_dec")
        p.op("vector", lambda e, w=w, bk=bk: e.tensor_tensor(
                Pm["S"][:, w * 512:(w + 1) * 512], Pm["S"][:, w * 512:(w + 1) * 512],
                PS[bk][:, :], ALU.add),
             reads=psq(bk, 0, 4) + ["S"], writes=["S"], tag="S_add")
        p.op("scalar", lambda e, w=w: e.copy(
                Aa["Sb"][:, w * 512:(w + 1) * 512], Pm["S"][:, w * 512:(w + 1) * 512]),
             reads=["S"], writes=["Sb"], tag="S_cast")
    if c == 0:
        dump(ctx, "ycore0", Aa["ycore"][:, :], ["ycore"])
        dump(ctx, "yw0", Aa["yw"][:, :], ["yw"])
        dump(ctx, "xd0", Aa["xd"][:, :], ["xd"])
        dump(ctx, "Mt0", Aa["Mt"][:, :], ["Mt"])
    if c == 1:
        dump(ctx, "S1", Pm["S"][:, :], ["S"])
        dump(ctx, "ycore1", Aa["ycore"][:, :], ["ycore"])


def chunk_epilogue(ctx, c):
    p, g, Pm, Aa, PS = ctx["p"], ctx["g"], ctx["Pm"], ctx["Aa"], ctx["PS"]
    T = ctx["T"]
    s, ci = c // 2, c % 2
    for kd in range(8):
        bk = 6 + kd % 2
        for kh in range(16):
            p.op("tensor", lambda e, kd=kd, kh=kh, bk=bk: e.matmul(
                    PS[bk][:, 0:128],
                    Aa["w_outproj"][:, kh * D + kd * 128: kh * D + kd * 128 + 128],
                    Aa["ycore"][:, kh * 128:(kh + 1) * 128],
                    start=(kh == 0), stop=(kh == 15)),
                 reads=["ycore", "woutproj"], writes=psq(bk, 0, 4), tag="mm_op")
        p.op("scalar", lambda e, kd=kd, bk=bk: e.copy(
                Aa["yTb"][:, kd * 128:(kd + 1) * 128], PS[bk][:, 0:128]),
             reads=psq(bk, 0), writes=["yTb"], tag="ev_y")
        p.op("vector", lambda e, kd=kd, ci=ci: e.scalar_tensor_tensor(
                Aa["x2pT"][:, kd * SUP + ci * L: kd * SUP + ci * L + L],
                Aa["yTb"][:, kd * 128:(kd + 1) * 128], Pm["modc"][:, 16 + kd:17 + kd],
                Aa["x2pT"][:, kd * SUP + ci * L: kd * SUP + ci * L + L],
                ALU.mult, ALU.add),
             reads=["yTb", "modc", "x2pT"], writes=["x2pT"], tag="x2p2")
    for hc in range(2):
        for kd in range(8):
            p.op("tensor", lambda e, hc=hc, kd=kd: e.matmul(
                    PS[hc][:, 0:128],
                    Aa["w_sc1"][:, kd * SC_HID + hc * 128: kd * SC_HID + hc * 128 + 128],
                    Aa["yTb"][:, kd * 128:(kd + 1) * 128],
                    start=(kd == 0), stop=(kd == 7)),
                 reads=["yTb", "wsc1"], writes=psq(hc, 0, 4), tag="mm_s1")
        p.op("scalar", lambda e, hc=hc: e.activation(
                Aa["r1"][:, hc * 128:(hc + 1) * 128], PS[hc][:, 0:128],
                ACT.Relu, bias=0.0, scale=1.0),
             reads=psq(hc, 0), writes=["r1"], tag="ev_s1")
    for hc in range(2):
        p.op("tensor", lambda e, hc=hc: e.matmul(
                PS[0][0:1, 256:384], Aa["w_s2"][:, hc:hc + 1],
                Aa["r1"][:, hc * 128:(hc + 1) * 128],
                start=(hc == 0), stop=(hc == 1)),
             reads=["r1", "ws2"], writes=psq(0, 0, 4), tag="mm_s2")
    p.op("scalar", lambda e: e.activation(
            Pm["srow32"][0:1, 1440:1568], PS[0][0:1, 256:384],
            ACT.Sigmoid, bias=0.0, scale=1.0),
         reads=psq(0, 2), writes=["scrow"], tag="ev_s2")
    p.dma("scalar", mkap(g["cc_sc_in"].ap(), c * 128, [(128, 1), (1, 128)]),
          Pm["srow32"][0:1, 1440:1568], reads=["scrow"], writes=["scin"], tag="st_sc")
    for kd in range(8):
        p.op("tensor", lambda e, kd=kd: e.matmul(
                PS[1][0:64, 0:128], Aa["w_summ"][:, kd * SUMD:(kd + 1) * SUMD],
                Aa["yTb"][:, kd * 128:(kd + 1) * 128],
                start=(kd == 0), stop=(kd == 7)),
             reads=["yTb", "wsumm"], writes=psq(1, 0, 4), tag="mm_sm")
    p.op("vector", lambda e: e.tensor_copy(Pm["sb6464"][:, :], PS[1][0:64, 0:128]),
         reads=psq(1, 0), writes=["sb6464"], tag="ev_sm")
    p.op("tensor", lambda e: e.transpose(PS[1].bitcast(BF16)[:, 512:576],
                                         Pm["sb6464"][:, :], Pm["ident"][0:64, 0:64]),
         reads=["sb6464", "c_ident"], writes=psq(1, 2), tag="tr_sm")
    p.op("vector", lambda e: e.tensor_copy(Aa["r1"][:, 0:64],
                                           PS[1].bitcast(BF16)[:, 512:576]),
         reads=psq(1, 2), writes=["r1"], tag="ev_sm2")
    p.dma("scalar", g["summ_spill"][ts(c), :], Aa["r1"][:, 0:64],
          reads=["r1"], writes=["smspill"], tag="st_sm")
    for kd in range(8):
        p.op("tensor", lambda e, kd=kd: e.matmul(
                PS[0][0:64, 0:128], Aa["w_q"][:, kd * SUMD:(kd + 1) * SUMD],
                Aa["yTb"][:, kd * 128:(kd + 1) * 128],
                start=(kd == 0), stop=(kd == 7)),
             reads=["yTb", "wqw"], writes=psq(0, 0, 4), tag="mm_q")
    p.op("scalar", lambda e: e.copy(Pm["sb6464"][:, :], PS[0][0:64, 0:128]),
         reads=psq(0, 0), writes=["sb6464"], tag="ev_q")
    p.dma("scalar", mkap(g["q_spill"].ap(), c * 128, [(T, 64), (1, 128)]),
          Pm["sb6464"][:, :], reads=["sb6464"], writes=["qspill"], tag="st_q")
    p.dma("sync", mkap(g["y_spill"].ap(), c * 128, [(T, 128), (128 * T, 8), (1, 128)]),
          Aa["yTb"][:, :], reads=["yTb"], writes=["yspill"], tag="st_y")
    if ci == 1:
        p.dma("sync", mkap(g["x2p_spill"].ap(), s * SUP,
                             [(T, 128), (128 * T, 8), (1, SUP)]),
              Aa["x2pT"][:, :], reads=["x2pT"], writes=["x2spill"], tag="st_x2p")
    if c == 0:
        dump(ctx, "yTb0", Aa["yTb"][:, :], ["yTb"])


def pool_exchange(ctx):
    p, g, Pm, Aa, Cc, PS = ctx["p"], ctx["g"], ctx["Pm"], ctx["Aa"], ctx["Cc"], ctx["PS"]
    T, NTILE, SCPAD = ctx["T"], ctx["NTILE"], ctx["SCPAD"]
    ALLT = NCORES * T
    fenceA = p.fence()
    _orig_op = p.op

    def _serial_op(eng, fn, reads=(), writes=(), after=(), **kw):
        prev = p.ops[-1] if p.ops else None
        o = _orig_op(eng, fn, reads=reads, writes=writes,
                     after=list(after) + ([prev] if prev is not None else []), **kw)
        return o
    p.op = _serial_op
    try:
        _pool_exchange_body(ctx, fenceA)
    finally:
        p.op = _orig_op


def _pool_exchange_body(ctx, fenceA):
    p, g, Pm, Aa, Cc, PS = ctx["p"], ctx["g"], ctx["Pm"], ctx["Aa"], ctx["Cc"], ctx["PS"]
    T, NTILE, SCPAD = ctx["T"], ctx["NTILE"], ctx["SCPAD"]
    ALLT = NCORES * T

    p.coll(lambda e: e.collective_compute(
            "AllGather", ALU.bypass, replica_groups=[list(range(NCORES))],
            ins=[g["cc_sc_in"].ap().opt()], outs=[g["cc_sc_out"].ap().opt()]),
           reads=["scin", "scpad"], writes=["scout"], tag="ag_sc")
    ALLP = min(128, ALLT // 128)
    ALLF = ALLT // ALLP
    PPC = T // ALLF
    p.dma("sync", Cc["allsc"][0:ALLP, 0:ALLF],
          mkap(g["cc_sc_out"].ap(), 0, [(SCPAD, NCORES), (ALLF, PPC), (1, ALLF)]),
          reads=["scout"], writes=["allsc"], after=fenceA, tag="ld_allsc")
    p.op("vector", lambda e: e.reduce_sum(Pm["cols"][0:ALLP, 8:9],
                                          Cc["allsc"][0:ALLP, 0:ALLF], AX.X),
         reads=["allsc"], writes=["gsum1"], tag="gsum1")
    p.op("tensor", lambda e: e.transpose(PS[0][0:1, 384:384 + ALLP],
                                         Pm["cols"][0:ALLP, 8:9],
                                         Pm["ident32"][0:ALLP, 0:ALLP]),
         reads=["gsum1", "c_ident32"], writes=psq(0, 3), tag="tr_sums")
    p.op("vector", lambda e: e.reduce_sum(Pm["srow32"][0:1, 1568:1569],
                                          PS[0][0:1, 384:384 + ALLP], AX.X),
         reads=psq(0, 3), writes=["gsum"], tag="gsum")

    if PXL < 2:
        return
    def level(pfx, np_, width, cur0, pong0, col0, niter=8):
        cur, pong = cur0, pong0
        for i in range(niter):
            p.op("vector", lambda e, cur=cur, np_=np_, width=width: e.max(
                    out=Pm["m8"][0:np_, :], in_=Cc["allsc"][0:np_, cur:cur + width]),
                 reads=[pfx + f"c{cur}"], writes=["m8v"], tag=pfx + "mx")
            p.op("vector", lambda e, i=i, np_=np_: e.tensor_copy(
                    Cc["allsc"][0:np_, col0 + 8 * i: col0 + 8 * i + 8], Pm["m8"][0:np_, :]),
                 reads=["m8v"], writes=[pfx + "C"], tag=pfx + "cp")
            if i < niter - 1:
                p.op("vector", lambda e, cur=cur, pong=pong, np_=np_, width=width:
                     e.match_replace(out=Cc["allsc"][0:np_, pong:pong + width],
                                     in_to_replace=Pm["m8"][0:np_, :],
                                     in_values=Cc["allsc"][0:np_, cur:cur + width],
                                     imm_value=-1e20),
                     reads=["m8v", pfx + f"c{cur}"], writes=[pfx + f"c{pong}"],
                     tag=pfx + "mr")
                cur, pong = pong, cur

    assert ALLF == 128
    p.buf("L1c0").writer = p.bufs["allsc"].writer
    level("L1", ALLP, 128, 0, 128, 256)
    p.dma("scalar", mkap(g["lred"].ap(), 0, [(64, ALLP), (1, 64)]),
          Cc["allsc"][0:ALLP, 256:320], reads=["L1C"], writes=["lr1"], tag="st_l1")
    n = ALLP * 64
    src_off, dst_off, prev_store = 0, n, "lr1"
    li = 2
    lastC = "L1C"
    pyr0 = ["L1c0"]
    while n > 64:
        np_ = max(1, n // 256)
        width = n // np_
        pfx = f"L{li}"
        p.dma("sync", Cc["allsc"][0:np_, 0:width],
              mkap(g["lred"].ap(), src_off, [(width, np_), (1, width)]),
              reads=[prev_store], writes=[pfx + "c0"], tag="ld_" + pfx)
        pyr0.append(pfx + "c0")
        level(pfx, np_, width, 0, 256, 512)
        lastC = pfx + "C"
        out_n = np_ * 64
        if out_n > 64:
            p.dma("scalar", mkap(g["lred"].ap(), dst_off, [(64, np_), (1, 64)]),
                  Cc["allsc"][0:np_, 512:576], reads=[pfx + "C"], writes=["lr" + pfx],
                  tag="st_" + pfx)
            prev_store = "lr" + pfx
            src_off, dst_off = dst_off, dst_off + out_n
        n = out_n
        li += 1
    if PXL < 3:
        return
    p.op("tensor", lambda e: e.matmul(PS[0][:, 392:393], Pm["onesrow32"][:, :],
                                      Cc["allsc"][0:1, 512 + 49:512 + 50],
                                      start=True, stop=True),
         reads=[lastC, "c_onesrow32"], writes=psq(0, 3), tag="mm_v50")
    p.op("vector", lambda e: e.tensor_copy(Pm["cols"][:, 9:10], PS[0][:, 392:393]),
         reads=psq(0, 3), writes=["v50"], tag="ev_v50")
    dump(ctx, "v50", Pm["cols"][:, 9:10], ["v50"])
    p.dma("sync", Cc["allsc"][0:ALLP, 0:128],
          mkap(g["cc_sc_out"].ap(), 0, [(SCPAD, NCORES), (ALLF, PPC), (1, ALLF)]),
          reads=["scout"], writes=["allsc2", "allsc"] + pyr0, tag="ld_allsc2")
    p.op("vector", lambda e: e.tensor_scalar(Cc["allsc"][0:ALLP, 128:256],
                                             Cc["allsc"][0:ALLP, 0:128],
                                             Pm["cols"][0:ALLP, 9:10], None, ALU.is_ge),
         reads=["allsc2", "v50"], writes=["mask_"], tag="mask1")
    p.op("vector", lambda e: e.scalar_tensor_tensor(
            Cc["allsc"][0:ALLP, 128:256], Cc["allsc"][0:ALLP, 0:128], TAU1,
            Cc["allsc"][0:ALLP, 128:256], ALU.is_gt, ALU.mult),
         reads=["allsc2", "mask_"], writes=["mask"], tag="mask2")
    p.op("vector", lambda e: e.reduce_sum(Pm["cols"][0:ALLP, 10:11],
                                          Cc["allsc"][0:ALLP, 128:256], AX.X),
         reads=["mask"], writes=["rowsum"], tag="rowsum")
    p.op("vector", lambda e: e.tensor_copy(Pm["colsB"][0:ALLP, 0:1],
                                           Pm["cols"][0:ALLP, 10:11]),
         reads=["rowsum"], writes=["rowsumB"], tag="rowsumB")
    p.op("tensor", lambda e: e.matmul(PS[0][0:ALLP, 393:394],
                                      Pm["strictup"][0:ALLP, 0:ALLP],
                                      Pm["colsB"][0:ALLP, 0:1], start=True, stop=True),
         reads=["rowsumB", "c_strictup"], writes=psq(0, 3), tag="mm_pfx")
    p.op("vector", lambda e: e.tensor_copy(Pm["cols"][0:ALLP, 11:12],
                                           PS[0][0:ALLP, 393:394]),
         reads=psq(0, 3), writes=["prefix"], tag="ev_pfx")
    p.op("tensor", lambda e: e.transpose(PS[0][0:1, 256:256 + ALLP],
                                         Pm["cols"][0:ALLP, 10:11],
                                         Pm["ident32"][0:ALLP, 0:ALLP]),
         reads=["rowsum", "c_ident32"], writes=psq(0, 2), tag="tr_cnt")
    p.op("vector", lambda e: e.reduce_sum(Pm["srow32"][0:1, 1569:1570],
                                          PS[0][0:1, 256:256 + ALLP], AX.X),
         reads=psq(0, 2), writes=["count"], tag="count")
    p.op("vector", lambda e: e.tensor_tensor_scan(
            Cc["allsc"][0:ALLP, 256:384], Cc["allsc"][0:ALLP, 128:256],
            Cc["allsc"][0:ALLP, 128:256], 0.0, ALU.add, ALU.bypass),
         reads=["mask"], writes=["incl", "L1C"], tag="incl")
    p.op("vector", lambda e: e.tensor_scalar(Cc["allsc"][0:ALLP, 256:384],
                                             Cc["allsc"][0:ALLP, 256:384],
                                             Pm["cols"][0:ALLP, 11:12], -1.0,
                                             ALU.add, ALU.add),
         reads=["incl", "prefix"], writes=["slots"], tag="slots")
    p.op("vector", lambda e: e.tensor_tensor(Cc["allsc"][0:ALLP, 256:384],
                                             Cc["allsc"][0:ALLP, 256:384],
                                             Cc["allsc"][0:ALLP, 128:256], ALU.mult),
         reads=["slots", "mask"], writes=["sm1"], tag="sm1")
    p.op("vector", lambda e: e.scalar_tensor_tensor(
            Cc["allsc"][0:ALLP, 256:384], Cc["allsc"][0:ALLP, 128:256], -1.0,
            Cc["allsc"][0:ALLP, 256:384], ALU.add, ALU.add),
         reads=["mask", "sm1"], writes=["stilde"], tag="stilde")
    p.op("vector", lambda e: e.tensor_copy(Cc["yrel"][0:ALLP, 0:128],
                                           Cc["allsc"][0:ALLP, 256:384]),
         reads=["stilde"], writes=["stildeB"], after=fenceA, tag="stildeB")
    dump(ctx, "stildeD", Cc["allsc"][0:ALLP, 256:384], ["stilde"])
    if PXL < 4:
        return
    p.dma("sync", Cc["summ_tok"][:, :],
          mkap(g["summ_spill"].ap(), 0, [(SUMD, 128), (128 * SUMD, NTILE), (1, SUMD)]),
          reads=["smspill"], writes=["summ_tok"], after=fenceA, tag="ld_summ_tok")
    p.dma("gpsimd", Cc["vw"][:, :], g["v_w"][:, :], writes=["wvw"], after=fenceA,
          tag="ld_vw")
    p.dma("sync", Pm["colsB"][:, 0:NTILE], g["rowsel"][:, :],
          writes=["rowsel", "rowsumB"], tag="ld_rowsel")
    NT = ctx["NTILE"]
    for j in range(NT):
        p.op("tensor", lambda e, j=j: e.matmul(
                PS[2][:, 0:1], Cc["yrel"][0:ALLP, 0:128], Pm["colsB"][0:ALLP, j:j + 1],
                start=True, stop=True),
             reads=["stildeB", "rowsel"], writes=psq(2, 0), tag="mm_scol")
        p.op("vector", lambda e: e.tensor_scalar(Cc["prb"][:, 0:64], Pm["iota64"][:, :],
                                                 PS[2][:, 0:1], None, ALU.is_equal),
             reads=psq(2, 0) + ["c_iota64"], writes=["PT"], tag="PT")
        p.op("tensor", lambda e, j=j: e.matmul(
                PS[1][0:64, 0:64], Cc["prb"][:, 0:64],
                Cc["summ_tok"][:, j * SUMD:(j + 1) * SUMD],
                start=(j == 0), stop=(j == NT - 1)),
             reads=["PT", "summ_tok"], writes=psq(1, 0, 4), tag="mm_pool")
    p.op("vector", lambda e: e.tensor_copy(Pm["pool"][:, :], PS[1][0:64, 0:64]),
         reads=psq(1, 0), writes=["pool_part"], tag="ev_pool")
    dump(ctx, "poolpart", Pm["pool"][:, :], ["pool_part"])
    dump(ctx, "summtok", Cc["summ_tok"][:, :], ["summ_tok"])
    p.dma("scalar", mkap(g["cc_pool_in"].ap(), 0, [(64, 64), (1, 64)]),
          Pm["pool"][:, :], reads=["pool_part"], writes=["poolin"], tag="st_pool")
    if PXL < 5:
        return
    p.coll(lambda e: e.collective_compute(
            "AllReduce", ALU.add, replica_groups=[list(range(NCORES))],
            ins=[g["cc_pool_in"].ap().opt()], outs=[g["cc_pool_out"].ap().opt()]),
           reads=["poolin", "poolpad"], writes=["poolout"], tag="ar_pool")
    p.dma("sync", Pm["pool"][:, :], mkap(g["cc_pool_out"].ap(), 0, [(64, 64), (1, 64)]),
          reads=["poolout"], writes=["pool"], tag="ld_pool")
    dump(ctx, "pool", Pm["pool"][:, :], ["pool"])
    p.op("vector", lambda e: e.tensor_scalar(Pm["srow32"][0:1, 1600:1664],
                                             Pm["iota64"][0:1, :],
                                             Pm["srow32"][0:1, 1569:1570], -1e9,
                                             ALU.is_ge, ALU.mult),
         reads=["count", "c_iota64"], writes=["mrow"], tag="mrow")
    p.op("tensor", lambda e: e.matmul(PS[0][:, 256:320], Pm["onesrow32"][:, :],
                                      Pm["srow32"][0:1, 1600:1664], start=True, stop=True),
         reads=["mrow", "c_onesrow32"], writes=psq(0, 2), tag="mm_mrep")
    p.op("vector", lambda e: e.tensor_copy(Pm["maskrep"][:, :], PS[0][:, 256:320]),
         reads=psq(0, 2), writes=["maskrep"], tag="ev_mrep")
    p.op("vector", lambda e: e.tensor_scalar(Pm["srow32"][0:1, 1570:1571],
                                             Pm["srow32"][0:1, 1568:1569],
                                             1.0 / ALLT, TAU2, ALU.mult, ALU.is_gt),
         reads=["gsum"], writes=["c1"], tag="c1")
    p.op("vector", lambda e: e.scalar_tensor_tensor(
            Pm["srow32"][0:1, 1571:1572], Pm["srow32"][0:1, 1569:1570], 0.5,
            Pm["srow32"][0:1, 1570:1571], ALU.is_gt, ALU.mult),
         reads=["count", "c1"], writes=["cond"], tag="cond")
    p.op("tensor", lambda e: e.matmul(PS[0][:, 320:321], Pm["onesrow32"][:, :],
                                      Pm["srow32"][0:1, 1571:1572], start=True, stop=True),
         reads=["cond", "c_onesrow32"], writes=psq(0, 2), tag="mm_cond")
    p.op("vector", lambda e: e.tensor_copy(Pm["cols"][:, 12:13], PS[0][:, 320:321]),
         reads=psq(0, 2), writes=["condcol"], tag="ev_cond")
    p.op("vector", lambda e: e.tensor_tensor(Pm["gacond"][:, :], Pm["modc"][:, 16:24],
                                             Pm["cols"][:, 12:13].to_broadcast([128, 8]),
                                             ALU.mult),
         reads=["condcol", "modc"], writes=["gacond"], tag="gacond")
    p.op("tensor", lambda e: e.transpose(PS[0][0:64, 384:448], Pm["pool"][:, :],
                                         Pm["ident32"][0:64, 0:64]),
         reads=["pool", "c_ident32"], writes=psq(0, 3), tag="tr_pool")
    p.op("vector", lambda e: e.tensor_copy(Pm["poolT"][:, :], PS[0][0:64, 384:448]),
         reads=psq(0, 3), writes=["poolT"], tag="ev_poolT")
    p.op("tensor", lambda e: e.matmul(PS[0][0:64, 0:64], Pm["kw"][:, :], Pm["poolT"][:, :],
                                      start=True, stop=True),
         reads=["poolT", "wkw"], writes=psq(0, 0), tag="mm_kk")
    p.op("scalar", lambda e: e.copy(Pm["kkT"][:, :], PS[0][0:64, 0:64]),
         reads=psq(0, 0), writes=["kkT"], tag="ev_kk")
    for v2 in range(2):
        p.op("tensor", lambda e, v2=v2: e.matmul(
                PS[1][0:64, :], Pm["poolT"][:, :], Cc["vw"][:, v2 * 512:(v2 + 1) * 512],
                start=True, stop=True),
             reads=["poolT", "wvw"], writes=psq(1, 0, 4), tag="mm_vv")
        p.op("scalar", lambda e, v2=v2: e.copy(Cc["vvb"][:, v2 * 512:(v2 + 1) * 512],
                                               PS[1][0:64, :]),
             reads=psq(1, 0, 4), writes=["vvb"], tag="ev_vv")


def phase_c(ctx):
    p, g, Pm, Cc, PS = ctx["p"], ctx["g"], ctx["Pm"], ctx["Cc"], ctx["PS"]
    T = ctx["T"]
    fenceA = p.fence()
    for kc in range(16):
        p.dma("gpsimd", Cc["w_gate"][:, kc * D:(kc + 1) * D], g["gate_w"][ts(kc), :],
              writes=["wgate"], after=fenceA if kc == 0 else (), tag="ld_gate")
    p.dma("sync", Cc["qT"][:, :], g["q_spill"][:, :], reads=["qspill"],
          writes=["qTfull"], after=fenceA, tag="ld_qT")
    for bt in range(T // 512):
        p.dma("sync", Cc["yrel"][:, :],
              mkap(g["y_spill"].ap(), bt * 512, [(T, 128), (128 * T, 8), (1, 512)]),
              reads=["yspill"], writes=["yrel", "stildeB"], tag="ld_yrel")
        p.dma("sync", Cc["x2slab"][:, :],
              mkap(g["x2p_spill"].ap(), bt * 512, [(T, 128), (128 * T, 8), (1, 512)]),
              reads=["x2spill"], writes=["x2slab"], tag="ld_x2")
        for tt in range(4):
            t0 = bt * 512 + tt * 128
            p.op("tensor", lambda e, t0=t0: e.matmul(
                    PS[0][:, 0:64], Cc["qT"][:, t0:t0 + 128], Pm["kkT"][:, :],
                    start=True, stop=True),
                 reads=["qTfull", "kkT"], writes=psq(0, 0, 2), tag="mm_lg")
            p.op("vector", lambda e: e.scalar_tensor_tensor(
                    Cc["logits"][:, :], PS[0][:, 0:64], 0.25, Pm["maskrep"][:, :],
                    ALU.mult, ALU.add),
                 reads=psq(0, 0) + ["maskrep"], writes=["lg"], tag="lg")
            if bt == 0 and tt == 0:
                dump(ctx, "lgD", Cc["logits"][:, :], ["lg"])
            p.op("vector", lambda e: e.reduce_max(Pm["cols"][:, 13:14], Cc["logits"][:, :],
                                                  AX.X, negate=True),
                 reads=["lg"], writes=["negm"], tag="negm")
            p.op("scalar", lambda e: e.activation(Cc["logits"][:, :], Cc["logits"][:, :],
                                                  ACT.Exp, bias=Pm["cols"][:, 13:14],
                                                  scale=1.0, accum_out=Pm["cols"][:, 14:15]),
                 reads=["lg", "negm"], writes=["lg", "den"], tag="lexp")
            p.op("vector", lambda e: e.reciprocal(Pm["cols"][:, 15:16], Pm["cols"][:, 14:15]),
                 reads=["den"], writes=["rden"], tag="rden")
            p.op("vector", lambda e: e.tensor_scalar(Cc["prb"][:, 0:64], Cc["logits"][:, :],
                                                     Pm["cols"][:, 15:16], None, ALU.mult),
                 reads=["lg", "rden"], writes=["prbC"], tag="prb")
            if bt == 0 and tt == 0:
                dump(ctx, "prbD", Cc["prb"][:, 0:64], ["prbC"])
            p.op("tensor", lambda e: e.transpose(PS[0].bitcast(BF16)[0:64, 256:384],
                                                 Cc["prb"][:, 0:64], Pm["ident"][:, :]),
                 reads=["prbC", "c_ident"], writes=psq(0, 0, 2), tag="tr_prb")
            p.op("vector", lambda e, t0=t0: e.tensor_copy(
                    Cc["probsT"][:, t0:t0 + 128], PS[0].bitcast(BF16)[0:64, 256:384]),
                 reads=psq(0, 1), writes=["probsT"], tag="ev_prb")
        for kd in range(8):
            bk = 2 + kd % 2
            p.op("tensor", lambda e, kd=kd, bt=bt, bk=bk: e.matmul(
                    PS[bk][:, :], Cc["vvb"][:, kd * 128:(kd + 1) * 128],
                    Cc["probsT"][:, bt * 512:(bt + 1) * 512], start=True, stop=True),
                 reads=["probsT", "vvb"], writes=psq(bk, 0, 4), tag="mm_rt"),
            p.op("scalar", lambda e, kd=kd, bk=bk: e.copy(
                    Cc["retrT"][:, kd * 512:(kd + 1) * 512], PS[bk][:, :]),
                 reads=psq(bk, 0, 4), writes=["retrT"], tag="ev_rt")
            if bt == 0 and kd == 0:
                dump(ctx, "retrD", Cc["retrT"][:, 0:512], ["retrT"])
        for kd in range(8):
            bk = 4 + kd % 2
            for kc in range(16):
                rhs = (Cc["yrel"][:, kc * 512:(kc + 1) * 512] if kc < 8 else
                       Cc["retrT"][:, (kc - 8) * 512:(kc - 7) * 512])
                rb = "yrel" if kc < 8 else "retrT"
                p.op("tensor", lambda e, kd=kd, kc=kc, bk=bk, rhs=rhs: e.matmul(
                        PS[bk][:, :], Cc["w_gate"][:, kc * D + kd * 128:
                                                   kc * D + kd * 128 + 128],
                        rhs, start=(kc == 0), stop=(kc == 15)),
                     reads=[rb, "wgate"], writes=psq(bk, 0, 4), tag="mm_g")
            p.op("scalar", lambda e, bk=bk: e.activation(Cc["gterm"][:, :], PS[bk][:, :],
                                                         ACT.Sigmoid, bias=0.0, scale=1.0),
                 reads=psq(bk, 0, 4), writes=["gterm"], tag="ev_g")
            p.op("vector", lambda e, kd=kd: e.tensor_tensor(
                    Cc["gterm"][:, :], Cc["gterm"][:, :],
                    Cc["retrT"][:, kd * 512:(kd + 1) * 512], ALU.mult),
                 reads=["gterm", "retrT"], writes=["gterm"], tag="gr"),
            if bt == 0 and kd == 0:
                dump(ctx, "gtD", Cc["gterm"][:, :], ["gterm"])
            p.op("vector", lambda e, kd=kd: e.scalar_tensor_tensor(
                    Cc["x2slab"][:, kd * 512:(kd + 1) * 512], Cc["gterm"][:, :],
                    Pm["gacond"][:, kd:kd + 1], Cc["x2slab"][:, kd * 512:(kd + 1) * 512],
                    ALU.mult, ALU.add),
                 reads=["gterm", "gacond", "x2slab"], writes=["x2slab"], tag="x2fin")
        p.dma("scalar", mkap(g["x2p_spill"].ap(), bt * 512,
                             [(T, 128), (128 * T, 8), (1, 512)]),
              Cc["x2slab"][:, :], reads=["x2slab"], writes=["x2spill"], tag="st_x2")
    dump(ctx, "probsT", Cc["probsT"][:, :], ["probsT"])


def phase_b(ctx):
    p, g, Pm, Bb, PS = ctx["p"], ctx["g"], ctx["Pm"], ctx["Bb"], ctx["PS"]
    T = ctx["T"]
    NT512 = T // 512
    fenceC = p.fence()
    p.dma("sync", Bb["cb1rep"][:, :], mkap(g["mod_hbm"].ap(), 4 * D, [(0, 128), (1, D)]),
          reads=["modhbm2"], writes=["cb1rep"], after=fenceC, tag="bc_cb")
    p.op("vector", lambda e: e.tensor_scalar(Bb["cb1rep"][:, :], Bb["cb1rep"][:, :],
                                             1.0, None, ALU.add),
         reads=["cb1rep"], writes=["cb1rep"], tag="cb1")
    p.dma("sync", Bb["sbrep"][:, :], mkap(g["mod_hbm"].ap(), 3 * D, [(0, 128), (1, D)]),
          reads=["modhbm1"], writes=["sbrep"], after=fenceC, tag="bc_sb")

    WQ = WQ_SHARD_ELEMS
    for tg in range(max(1, T // 1024)):
        TGW = min(1024, T)
        for sub in range(TGW // 512):
            bts = tg * 2 + sub
            if BBL == 10:
                continue
            p.dma("sync", Bb["x2f"][:, :],
                  mkap(g["x2p_spill"].ap(), bts * 512, [(T, 128), (128 * T, 8), (1, 512)]),
                  reads=["x2spill"], writes=["x2f"], tag="ld_x2f")
            if BBL == 105:
                continue
            for kd in range(8):
                bk = 6 + kd % 2
                for q4 in range(4):
                    p.op("tensor", lambda e, kd=kd, q4=q4, bk=bk: e.transpose(
                            PS[bk][:, q4 * 128:q4 * 128 + 128],
                            Bb["x2f"][:, kd * 512 + q4 * 128: kd * 512 + q4 * 128 + 128],
                            Pm["ident32"][:, :]),
                         reads=["x2f", "c_ident32"], writes=psq(bk, 0, 4), tag="tr_x2")
                p.op("scalar", lambda e, kd=kd, bk=bk: e.copy(
                        apx(Bb["x2tok"], kd * 128, [(D, 4), (1, 128)]),
                        PS[bk][:, 0:512]),
                     reads=psq(bk, 0, 4), writes=["x2tok"], tag="ev_x2")
            if BBL == 11:
                continue
            Cx = Pm["cols"]
            for tt in range(4):
                xt = Bb["x2tok"][:, tt * D:(tt + 1) * D]
                p.op("scalar", lambda e, xt=xt: e.activation(
                        xt, xt, ACT.Identity, bias=0.0, scale=1.0, accum_out=Cx[:, 0:1]),
                     reads=["x2tok"], writes=["x2tok", "l2sum"], tag="l2_s1")
                p.op("scalar", lambda e, xt=xt, tt=tt: e.activation(
                        Bb["h2q"][:, tt * D:(tt + 1) * D], xt, ACT.Square,
                        bias=0.0, scale=1.0, accum_out=Cx[:, 1:2]),
                     reads=["x2tok"], writes=["l2sq", "h2qdump"], tag="l2_s2")
                p.op("vector", lambda e: e.tensor_scalar(Cx[:, 2:3], Cx[:, 0:1],
                                                         1.0 / D, None, ALU.mult),
                     reads=["l2sum"], writes=["l2m"], tag="l2_m")
                p.op("vector", lambda e: e.tensor_tensor(Cx[:, 3:4], Cx[:, 2:3],
                                                         Cx[:, 2:3], ALU.mult),
                     reads=["l2m"], writes=["l2m2"], tag="l2_m2")
                p.op("vector", lambda e: e.scalar_tensor_tensor(
                        Cx[:, 4:5], Cx[:, 1:2], 1.0 / D, Cx[:, 3:4],
                        ALU.mult, ALU.subtract),
                     reads=["l2sq", "l2m2"], writes=["l2var"], tag="l2_var")
                p.op("scalar", lambda e: e.activation(Cx[:, 5:6], Cx[:, 4:5], ACT.Sqrt,
                                                      bias=Pm["kconst"][:, 0:1], scale=1.0),
                     reads=["l2var", "kconst"], writes=["l2sd"], tag="l2_sd")
                p.op("vector", lambda e: e.reciprocal(Cx[:, 6:7], Cx[:, 5:6]),
                     reads=["l2sd"], writes=["l2rstd"], tag="l2_rstd")
                p.op("vector", lambda e, xt=xt: e.tensor_scalar(
                        Bb["h2"][:, 0:D], xt, Cx[:, 2:3], Cx[:, 6:7],
                        ALU.subtract, ALU.mult),
                     reads=["x2tok", "l2m", "l2rstd"], writes=["h2u"], tag="l2_u")
                p.op("vector", lambda e: e.tensor_tensor(Bb["h2"][:, 0:D], Bb["h2"][:, 0:D],
                                                         Bb["cb1rep"][:, :], ALU.mult),
                     reads=["h2u", "cb1rep"], writes=["h2u"], tag="l2_cb")
                p.op("vector", lambda e: e.tensor_tensor(Bb["h2"][:, 0:D], Bb["h2"][:, 0:D],
                                                         Bb["sbrep"][:, :], ALU.add),
                     reads=["h2u", "sbrep"], writes=["h2m"], tag="l2_sb")
                if BBL == 12:
                    continue
                scr = dict(xn=Bb["q1"][:, :], a=Bb["q2"][:, :], acc=Bb["q3"][:, :],
                           hi=Bb["h2"].bitcast(F32)[:, 0:D] if False else None,
                           sc=(Bb["qsc"], 0, 64, 0), m=Bb["qm"][:, :])
                scr["hi"] = Bb["hffn"][:, 2048:3072]
                scr["sc"] = (Bb["qsc"], 0, 64, 0)
                quant_chain(ctx, "vector", Bb["h2"][:, 0:D],
                            Bb["h2q"][:, tt * D:(tt + 1) * D], D, scr, ["h2m"], "qh2")
            if BBL == 13:
                continue
            for kd in range(8):
                bk = 6 + kd % 2
                for tt4 in range(4):
                    p.op("tensor", lambda e, kd=kd, tt4=tt4, bk=bk: e.transpose(
                            PS[bk].bitcast(BF16)[:, tt4 * 128:tt4 * 128 + 128],
                            Bb["h2q"][:, tt4 * D + kd * 128: tt4 * D + kd * 128 + 128],
                            Pm["ident"][:, :]),
                         reads=["qh2", "c_ident"], writes=psq(bk, 0, 4), tag="tr_h2q")
                p.op("scalar", lambda e, kd=kd, sub=sub, bk=bk: e.copy(
                        Bb["h2qT"][:, kd * 1024 + sub * 512:
                                   kd * 1024 + sub * 512 + 512],
                        PS[bk].bitcast(BF16)[:, 0:512]),
                     reads=psq(bk, 0, 4), writes=["h2qT"], tag="ev_h2q")
            if tg == 0 and sub == 0:
                dump(ctx, "h2qD", Bb["h2q"][:, :], ["qh2"])
        if BBL < 2:
            continue
        for sh in range(NCORES):
            ns = FFN_SH[sh]
            nch = ns // 128
            base_icc = sum(FFN_SH[:sh]) // 128
            wgu_t = Bb["wgu"] if sh % 2 == 0 else Bb["wguB"]
            wgu_b = f"wgu{sh % 2}"
            hffn_t = Bb["hffn"] if sh % 2 == 0 else Bb["hffnB"]
            hffn_b = f"hffn{sh % 2}"
            for kd in range(8):
                p.dma("sync", wgu_t[:, kd * 384:kd * 384 + ns],
                      mkap(g["cc_wq_out"].ap(), sh * WQ + kd * 128 * 384,
                           [(384, 128), (1, ns)]),
                      reads=["wqout"], writes=[wgu_b], tag="ld_wg")
                p.dma("sync", wgu_t[:, 8 * 384 + kd * 384: 8 * 384 + kd * 384 + ns],
                      mkap(g["cc_wq_out"].ap(), sh * WQ + 384 * D + kd * 128 * 384,
                           [(384, 128), (1, ns)]),
                      reads=["wqout"], writes=[wgu_b], tag="ld_wu")
            for nl in range(nch):
                for t5 in range(TGW // 512):
                    for (mi, mb) in ((0, 0), (1, 8 * 384)):
                        bk = mi
                        for kd in range(8):
                            p.op("tensor", lambda e, nl=nl, t5=t5, kd=kd, mb=mb, bk=bk,
                                 wgu_t=wgu_t: e.matmul(
                                    PS[bk][:, :],
                                    wgu_t[:, mb + kd * 384 + nl * 128:
                                          mb + kd * 384 + nl * 128 + 128],
                                    Bb["h2qT"][:, kd * 1024 + t5 * 512:
                                               kd * 1024 + t5 * 512 + 512],
                                    start=(kd == 0), stop=(kd == 7)),
                                 reads=["h2qT", wgu_b], writes=psq(bk, 0, 4), tag="mm_gu")
                    p.op("scalar", lambda e: e.activation(Bb["ftmp"][:, :], PS[0][:, :],
                                                          ACT.Silu, bias=0.0, scale=1.0),
                         reads=psq(0, 0, 4), writes=["gsilu", "fout"], tag="ev_gs")
                    p.op("vector", lambda e, nl=nl, t5=t5, hffn_t=hffn_t: e.tensor_tensor(
                            hffn_t[:, nl * 1024 + t5 * 512: nl * 1024 + t5 * 512 + 512],
                            Bb["ftmp"][:, :], PS[1][:, :], ALU.mult),
                         reads=["gsilu"] + psq(1, 0, 4), writes=[hffn_b], tag="hffn")
                    if tg == 0 and sh == 0 and nl == 0 and t5 == 0:
                        dump(ctx, "hffnD", Bb["hffn"][:, 0:512], ["hffn0"])
            if BBL < 3:
                continue
            for tt8 in range(TGW // 128):
                for nl in range(nch):
                    p.op("tensor", lambda e, nl=nl, tt8=tt8, hffn_t=hffn_t: e.transpose(
                            PS[2][:, nl * 128:(nl + 1) * 128],
                            hffn_t[:, nl * 1024 + tt8 * 128: nl * 1024 + tt8 * 128 + 128],
                            Pm["ident32"][:, :]),
                         reads=[hffn_b, "c_ident32"], writes=psq(2, nl), tag="tr_hf")
                scr = dict(xn=Bb["q1"][:, 0:ns], a=Bb["q2"][:, 0:ns], acc=Bb["q3"][:, 0:ns],
                           hi=Bb["outtok"][:, 0:ns], sc=(Bb["qsc"], 0, 64, 0),
                           m=Bb["qm"][:, 0:ns])
                quant_chain(ctx, "vector", PS[2][:, 0:ns], Bb["hfq"][:, 0:ns], ns, scr,
                            psq(2, 0, nch), "qhf")
                for nl in range(nch):
                    p.op("tensor", lambda e, nl=nl: e.transpose(
                            PS[3].bitcast(BF16)[:, nl * 128:nl * 128 + 128],
                            Bb["hfq"][:, nl * 128:(nl + 1) * 128], Pm["ident"][:, :]),
                         reads=["qhf", "c_ident"], writes=psq(3, 0, 4), tag="tr_hq")
                p.op("scalar", lambda e, nch=nch, tt8=tt8, base_icc=base_icc: e.copy(
                        apx(Bb["hq"], base_icc * 1024 + tt8 * 128, [(1024, nch), (1, 128)]),
                        PS[3].bitcast(BF16)[:, 0:nch * 128]),
                     reads=psq(3, 0, 4), writes=[f"hq{sh}"], tag="ev_hq")
        if BBL < 4:
            continue
        for kd in range(8):
            p.dma("sync", Bb["wd"][:, :],
                  mkap(g["cc_wq_out"].ap(), kd * WQ + 2 * 384 * D,
                       [(128, 128), (16384, 22), (1, 128)]),
                  reads=["wqout"], writes=["wdb"], tag="ld_wd")
            for t5 in range(TGW // 512):
                bk = 4 + t5 % 2
                for icc in range(22):
                    p.op("tensor", lambda e, icc=icc, t5=t5, bk=bk: e.matmul(
                            PS[bk][:, :], Bb["wd"][:, icc * 128:(icc + 1) * 128],
                            Bb["hq"][:, icc * 1024 + t5 * 512: icc * 1024 + t5 * 512 + 512],
                            start=(icc == 0), stop=(icc == 21)),
                         reads=[f"hq{s_}" for s_ in range(NCORES)] + ["wdb"],
                         writes=psq(bk, 0, 4), tag="mm_dn")
                bts = tg * 2 + t5
                p.dma("sync", Bb["ftmp"][:, :],
                      mkap(g["x2p_spill"].ap(), kd * 128 * T + bts * 512,
                           [(T, 128), (1, 512)]),
                      reads=["x2spill"], writes=["x2fd", "gsilu", "fout"],
                      tag="ld_x2d")
                p.op("vector", lambda e, kd=kd, bk=bk: e.scalar_tensor_tensor(
                        Bb["ftmp"][:, :], PS[bk][:, :], Pm["modc"][:, 40 + kd:41 + kd],
                        Bb["ftmp"][:, :], ALU.mult, ALU.add),
                     reads=psq(bk, 0, 4) + ["modc", "x2fd"], writes=["fout"], tag="fout")
                if tg == 0 and kd == 0 and t5 == 0:
                    dump(ctx, "foutD", Bb["ftmp"][:, :], ["fout"])
                    dump(ctx, "hqD", Bb["hq"][:, 0:512], ["hq"])
                obk = 6 + kd % 2
                for q4 in range(4):
                    p.op("tensor", lambda e, q4=q4, obk=obk: e.transpose(
                            PS[obk][:, q4 * 128:(q4 + 1) * 128],
                            Bb["ftmp"][:, q4 * 128:(q4 + 1) * 128], Pm["ident32"][:, :]),
                         reads=["fout", "c_ident32"], writes=psq(obk, 0, 4), tag="tr_out")
                p.op("scalar", lambda e, obk=obk: e.copy(
                        Bb["outtok"][:, 0:512], PS[obk][:, 0:512]),
                     reads=psq(obk, 0, 4), writes=["outtok"], tag="ev_out")
                p.dma("scalar",
                      mkap(g["out"].ap(), bts * 512 * D + kd * 128,
                           [(D, 128), (128 * D, 4), (1, 128)]),
                      Bb["outtok"][:, 0:512], reads=["outtok"], writes=["outhbm"],
                      tag="st_out")


_CACHE = {}


def _prep(inputs, b, T):
    f32 = lambda a: np.ascontiguousarray(np.asarray(a, dtype=np.float32))
    wg = f32(inputs["L0_ffn_w_gate"])
    wu = f32(inputs["L0_ffn_w_up"])
    wd = f32(inputs["L0_ffn_w_down"])
    off = [0]
    for sgl in FFN_SH:
        off.append(off[-1] + sgl)
    def pad384(m):
        out = np.zeros((384, D), np.float32)
        out[:m.shape[0]] = m
        return out
    rowsel = np.zeros((128, T // 128), ml_dtypes.bfloat16)
    for j in range(T // 128):
        rowsel[(b * T + j * 128) // 128, j] = 1
    return {
        "x": f32(inputs["L0_attn_x"][b]),
        "norm_w": f32(inputs["L0_attn_norm_w"]),
        "in_proj_w": f32(inputs["L0_attn_in_proj_w"]),
        "conv_w": f32(inputs["L0_attn_conv_w"]),
        "conv_b": f32(inputs["L0_attn_conv_b"]),
        "dt_bias": f32(inputs["L0_attn_dt_bias"]),
        "A_log": f32(inputs["L0_attn_A_log"]),
        "Dvec": f32(inputs["L0_attn_D"]),
        "ssm_norm_w": f32(inputs["L0_attn_ssm_norm_w"]),
        "out_proj_w": f32(inputs["L0_attn_out_proj_w"]),
        "scorer_w1": f32(inputs["L0_attn_scorer_w1"]),
        "scorer_w2": f32(inputs["L0_attn_scorer_w2"]).reshape(SC_HID),
        "summ_w": f32(inputs["L0_attn_summ_w"]),
        "q_w": f32(inputs["L0_attn_q_w"]),
        "k_w": f32(inputs["L0_attn_k_w"]),
        "v_w": f32(inputs["L0_attn_v_w"]),
        "gate_w": f32(inputs["L0_attn_gate_w"]),
        "cvec": f32(inputs["L0_c"][b]),
        "wmod": f32(inputs["L0_wmod"]),
        "wg_shard": pad384(wg[off[b]:off[b + 1]]),
        "wu_shard": pad384(wu[off[b]:off[b + 1]]),
        "wd_shard": np.ascontiguousarray(wd[b * WD_SH:(b + 1) * WD_SH]),
        "rowsel": rowsel,
    }


def kernel(**inputs):
    T = int(np.asarray(inputs["L0_attn_x"]).shape[1])
    key = (T, tuple(sorted(n for (n, _, _) in DEBUG)))
    if key not in _CACHE:
        _CACHE[key] = build(T=T, dbg=DEBUG)
    nc = _CACHE[key]
    in_maps = [_prep(inputs, b, T) for b in range(NCORES)]
    res = run_bass_kernel_spmd(nc, in_maps, core_ids=list(range(NCORES)))
    out = np.stack([np.asarray(r["out"], dtype=np.float32) for r in res.results])
    if DEBUG:
        kernel._dbg = res.results
    return out


# version 15
# speedup vs baseline: 1.0191x; 1.0087x over previous
import numpy as np
import ml_dtypes
from contextlib import ExitStack

import concourse.bass as bass
import concourse.mybir as mybir
from concourse.bass_utils import run_bass_kernel_spmd
from concourse.bass_types import AP

F32 = mybir.dt.float32
BF16 = mybir.dt.bfloat16
I32 = mybir.dt.int32
ALU = mybir.AluOpType
ACT = mybir.ActivationFunctionType
AX = mybir.AxisListType

D = 1024
NH = 32
HD = 64
DIN = 2048
DSTATE = 128
DCONV = 4
CONV_DIM = DIN + 2 * DSTATE
DINPROJ = 2 * DIN + 2 * DSTATE + NH
SC_HID = 256
SUMD = 64
NSLOT = 64
TAU1, TAU2 = 0.5, 0.3
EPS = 1e-5
L = 128
SUP = 256
NCORES = 8
IC = 2816
FFN_SH = [384, 384, 384, 384, 384, 384, 256, 256]
WD_SH = D // NCORES
WQ_SHARD_ELEMS = 384 * D * 2 + WD_SH * IC
ENGINES = ("sync", "tensor", "vector", "scalar", "gpsimd")

MIDS = [0.25, 0.75, 1.25, 1.75]
DEBUG = []
NQ = 8
PHASES = 8
NO_COLL = False
WQL = 5
FRL = 9
BBL = 9
PXL = 9


class Op:
    __slots__ = ("eng", "fn", "extra", "asyn", "qcls", "signal", "semval", "waits",
                 "tag", "idx")

    def __init__(self, eng, fn, extra, asyn, qcls, tag, idx):
        self.eng, self.fn, self.extra, self.asyn, self.qcls = eng, fn, extra, asyn, qcls
        self.signal, self.semval, self.waits, self.tag, self.idx = False, None, None, tag, idx


class Buf:
    __slots__ = ("name", "writer", "readers")

    def __init__(self, name):
        self.name, self.writer, self.readers = name, None, []


class Prog:

    def __init__(self, nc):
        self.nc = nc
        self.ops = []
        self.bufs = {}

    def buf(self, name):
        if name not in self.bufs:
            self.bufs[name] = Buf(name)
        return self.bufs[name]

    def op(self, eng, fn, reads=(), writes=(), after=(), dma=False, coll=False,
           tag=""):
        reads = list(reads)
        writes = list(writes)
        for b in reads:
            if isinstance(b, str) and len(b) == 5 and b.startswith("ps") \
                    and b[3] == "q" and b not in writes:
                writes.append(b)
        reads = [self.buf(b) if isinstance(b, str) else b for b in reads]
        writes = [self.buf(b) if isinstance(b, str) else b for b in writes]
        deps = list(after)
        for b in reads:
            if b.writer is not None:
                deps.append(b.writer)
        for b in writes:
            if b.writer is not None:
                deps.append(b.writer)
            deps.extend(b.readers)
        qcls = "l" if coll else ("q" if dma else "c")
        o = Op(eng, fn, [], dma or coll, qcls, tag, len(self.ops))
        seen = set()
        for d in deps:
            if d is not None and d is not o and id(d) not in seen:
                seen.add(id(d))
                o.extra.append(d)
        self.ops.append(o)
        for b in reads:
            if o not in b.readers:
                b.readers.append(o)
        for b in writes:
            b.writer = o
            b.readers = []
        return o

    def dma(self, eng, out, in_, reads=(), writes=(), after=(), tag=""):
        return self.op(eng, lambda e, out=out, in_=in_: e.dma_start(out=out, in_=in_),
                       reads=reads, writes=writes, after=after, dma=True, tag=tag)

    def coll(self, fn, reads=(), writes=(), after=(), tag=""):
        if NO_COLL:
            return self.op("gpsimd", lambda e: e.nop(), reads=reads, writes=writes,
                           after=after, tag=tag + "_skip")
        return self.op("gpsimd", fn, reads=reads, writes=writes, after=after,
                       coll=True, tag=tag)

    def fence(self):
        last = {}
        for o in self.ops:
            last[o.eng] = o
        return list(last.values())

    def finalize(self, block, sems):
        for o in self.ops:
            if o.asyn:
                o.signal = True
            for d in o.extra:
                if d.eng == o.eng == "tensor" and not d.asyn:
                    continue
                d.signal = True
        rr = {}
        for o in self.ops:
            if o.qcls == "q":
                i = rr.get(o.eng, 0)
                o.qcls = f"q{i % NQ}"
                rr[o.eng] = i + 1
        qcs = tuple(f"q{i}" for i in range(NQ))
        cnt = {(e, c): 0 for e in ENGINES for c in ("c", "l") + qcs}
        for o in self.ops:
            if o.signal:
                k = (o.eng, o.qcls)
                cnt[k] += 16 if o.qcls.startswith("q") else 1
                o.semval = cnt[k]
        water = {e: {} for e in ENGINES}
        prev_async = {}
        for o in self.ops:
            need = {}
            if o.asyn:
                qk = (o.eng, o.qcls)
                if prev_async.get(qk):
                    need[qk] = prev_async[qk]
                prev_async[qk] = o.semval
            for d in o.extra:
                if d.semval is None:
                    continue
                if d.eng == o.eng == "tensor" and not d.asyn:
                    continue
                k = (d.eng, d.qcls)
                if d.semval > need.get(k, 0):
                    need[k] = d.semval
            w = []
            for k, v in sorted(need.items()):
                if v > water[o.eng].get(k, 0):
                    w.append((k, v))
                    water[o.eng][k] = v
            o.waits = w
        streams = {e: [o for o in self.ops if o.eng == e] for e in ENGINES}

        def emitter(ename):
            def run(eng):
                for o in streams[ename]:
                    for (k, v) in o.waits:
                        eng.wait_ge(sems[k], v)
                    inst = o.fn(eng)
                    if o.signal:
                        inst.then_inc(sems[(o.eng, o.qcls)],
                                      16 if o.qcls.startswith("q") else 1)
                for qc in qcs + ("l",):
                    if cnt[(ename, qc)]:
                        eng.wait_ge(sems[(ename, qc)], cnt[(ename, qc)])
            return run

        for e in ENGINES:
            if streams[e]:
                getattr(block, e)(emitter(e))


def mkap(base, offset_elems, dims):
    if isinstance(base, AP):
        t, off = base.tensor, base.offset
    else:
        t, off = base, 0
    return AP(t, off + int(offset_elems), [[int(s), int(n)] for (s, n) in dims])


def ts(i, n=128):
    return bass.ts(i, n)


def build(T=2048, dbg=()):
    nc = bass.Bass("TRN2", target_bir_lowering=False)
    NCH = T // L
    NSUPS = T // SUP
    NTILE = T // 128
    ALLT = NCORES * T
    ALLP = ALLT // 128
    ALLF = ALLT // ALLP if ALLP == 128 else 128
    if ALLP > 128:
        raise ValueError("T too big")
    SCPAD = T + 8

    g = {}

    def dram(name, shape, dt=F32, kind="Internal", **kw):
        g[name] = nc.dram_tensor(name, shape, dt, kind=kind, **kw)
        return g[name]

    dram("x", [T, D], kind="ExternalInput")
    dram("norm_w", [D], kind="ExternalInput")
    dram("in_proj_w", [D, DINPROJ], kind="ExternalInput")
    dram("conv_w", [CONV_DIM, DCONV], kind="ExternalInput")
    dram("conv_b", [CONV_DIM], kind="ExternalInput")
    dram("dt_bias", [NH], kind="ExternalInput")
    dram("A_log", [NH], kind="ExternalInput")
    dram("Dvec", [NH], kind="ExternalInput")
    dram("ssm_norm_w", [DIN], kind="ExternalInput")
    dram("out_proj_w", [DIN, D], kind="ExternalInput")
    dram("scorer_w1", [D, SC_HID], kind="ExternalInput")
    dram("scorer_w2", [SC_HID], kind="ExternalInput")
    dram("summ_w", [D, SUMD], kind="ExternalInput")
    dram("q_w", [D, SUMD], kind="ExternalInput")
    dram("k_w", [SUMD, SUMD], kind="ExternalInput")
    dram("v_w", [SUMD, D], kind="ExternalInput")
    dram("gate_w", [2 * D, D], kind="ExternalInput")
    dram("cvec", [D], kind="ExternalInput")
    dram("wmod", [D, 6 * D], kind="ExternalInput")
    dram("wg_shard", [384, D], kind="ExternalInput")
    dram("wu_shard", [384, D], kind="ExternalInput")
    dram("wd_shard", [WD_SH, IC], kind="ExternalInput")
    dram("rowsel", [128, NTILE], BF16, kind="ExternalInput")
    dram("out", [T, D], kind="ExternalOutput")

    dram("hbm_rows", [64 * T])
    dram("mod_hbm", [6 * D])
    dram("x2p_spill", [D, T])
    dram("y_spill", [D, T], BF16)
    dram("q_spill", [SUMD, T], BF16)
    dram("summ_spill", [T, SUMD], BF16)
    dram("lred", [3 * 16384])
    dram("cc_sc_in", [SCPAD])
    dram("cc_sc_out", [NCORES * SCPAD], addr_space="Shared")
    dram("cc_pool_in", [NSLOT * SUMD + 16])
    dram("cc_pool_out", [NSLOT * SUMD + 16])
    g["cc_wq_in"] = nc.dram_tensor("cc_wq_in", [WQ_SHARD_ELEMS], BF16)
    g["cc_wq_out"] = nc.dram_tensor("cc_wq_out", [NCORES * WQ_SHARD_ELEMS], BF16,
                                    addr_space="Shared")

    dbgh = {}
    for (name, shape, dt_) in dbg:
        dbgh[name] = dram("dbg_" + name, list(shape), dt_, kind="ExternalOutput")

    g["c_ident"] = nc.inline_tensor(np.eye(128, dtype=ml_dtypes.bfloat16), "c_ident")
    g["c_ident32"] = nc.inline_tensor(np.eye(128, dtype=np.float32), "c_ident32")
    g["c_onescol"] = nc.inline_tensor(np.ones((128, 1), dtype=ml_dtypes.bfloat16), "c_onescol")
    g["c_onesrow32"] = nc.inline_tensor(np.ones((1, 128), dtype=np.float32), "c_onesrow32")
    tril = (np.arange(128)[:, None] <= np.arange(128)[None, :]).astype(np.float32)
    g["c_tril"] = nc.inline_tensor(tril, "c_tril")
    g["c_iota64"] = nc.inline_tensor(
        np.repeat(np.arange(64, dtype=np.float32)[None, :], 128, 0), "c_iota64")
    sup_m = (np.arange(128)[:, None] < np.arange(128)[None, :]).astype(ml_dtypes.bfloat16)
    g["c_strictup"] = nc.inline_tensor(sup_m, "c_strictup")

    p = Prog(nc)
    es = ExitStack()
    sems = {}
    for e in ENGINES:
        clss = ["c"]
        if e in ("sync", "scalar", "gpsimd"):
            clss += [f"q{i}" for i in range(NQ)]
        if e == "gpsimd":
            clss.append("l")
        for cls_ in clss:
            sems[(e, cls_)] = es.enter_context(nc.semaphore(f"sem_{e}_{cls_}"))

    MAP0 = (int(nc.sbuf_base) + 31) & ~31
    SB_LIMIT = int(nc.sbuf_top)
    cur = [MAP0]
    tensors = {}
    _off = {}

    def alloc(name, shape, dt=F32, at=None):
        nbytes = int(np.prod(shape[1:])) * mybir.dt.size(dt)
        nbytes = (nbytes + 31) & ~31
        o = cur[0] if at is None else at
        _off[name] = o
        t = nc.alloc_sbuf_tensor_at(name, list(shape), dt, offset=o)
        if at is None:
            cur[0] += nbytes
            assert cur[0] <= SB_LIMIT, f"sbuf overflow at {name}: {cur[0]}"
        else:
            assert o + nbytes <= SB_LIMIT, f"sbuf overlay overflow {name}"
        tensors[name] = t
        return t

    Pm = {}
    for (n, sh, dt_) in [
        ("ident", [128, 128], BF16), ("ident32", [128, 128], F32),
        ("onescol", [128, 1], BF16), ("onesrow32", [1, 128], F32),
        ("tril", [128, 128], F32), ("iota64", [128, 64], F32),
        ("strictup", [128, 128], BF16),
        ("modc", [128, 64], F32), ("gacond", [128, 8], F32),
        ("normw", [128, 8], F32), ("ssmw", [128, 16], F32),
        ("convw", [128, 18 * DCONV], F32), ("convb", [128, 18], F32),
        ("dtb", [32, 1], F32), ("negA", [32, 1], F32), ("Dcol", [128, 32], F32),
        ("kw", [64, 64], BF16),
        ("scores16", [16, 128], F32), ("S", [128, DIN], F32),
        ("srow32", [32, 1792], F32), ("dcrep", [128, NH], F32),
        ("cols", [128, 16], F32), ("colsB", [128, 16], BF16),
        ("maskrep", [128, 64], F32), ("pool", [64, 64], F32),
        ("m8", [128, 8], F32), ("halo", [128, 18 * 3], F32),
        ("kconst", [128, 4], F32),
        ("sb6464", [64, 128], BF16),
        ("kkT", [64, 64], BF16), ("poolT", [64, 64], BF16),
    ]:
        Pm[n] = alloc(n, sh, dt_)
    PERM_TOP = cur[0]

    Aa = {}
    cur[0] = PERM_TOP
    for (n, sh, dt_) in [
        ("w_inproj", [128, 8 * DINPROJ], BF16),
        ("w_outproj", [128, 16 * D], BF16),
        ("w_sc1", [128, 8 * SC_HID], BF16),
        ("w_summ", [128, 8 * SUMD], BF16),
        ("w_q", [128, 8 * SUMD], BF16),
        ("w_s2", [128, 2], BF16),
        ("x_tok", [128, D], F32),
        ("hT", [128, 8 * SUP], BF16),
        ("ztil", [128, 16 * SUP], BF16),
        ("acc0", [128, SUP + 8], F32),
        ("xst", [128, 16 * SUP], BF16),
        ("bct", [128, 2 * SUP], BF16),
        ("xsD", [128, 4 * L], F32),
        ("Mt", [128, 8 * L], BF16),
        ("csrep", [128, 8 * L], F32),
        ("erep2", [128, 4 * L], F32),
        ("yw", [128, 16 * L], F32),
        ("ycore", [128, 16 * L], BF16),
        ("x2pT", [128, 8 * SUP], F32),
        ("xd", [128, DIN], BF16),
        ("xdp", [128, DIN], BF16),
        ("btok", [128, DSTATE], BF16),
        ("Gt", [128, L], BF16),
        ("cstok", [128, NH], F32),
        ("dttok", [128, NH], F32),
        ("ddtok", [128, NH], F32),
        ("dstok", [128, NH], F32),
        ("dsup", [32, SUP], F32),
        ("cssup", [32, SUP], F32),
        ("essup", [32, SUP], F32),
        ("dssup", [32, SUP], F32),
        ("yTb", [128, 8 * L], BF16),
        ("r1", [128, 2 * L], BF16),
        ("rstd_rep", [128, L], F32),
        ("Sb", [128, DIN], BF16),
    ]:
        Aa[n] = alloc(n, sh, dt_)
    Aa["wmodbuf"] = alloc("wmodbuf", [128, 2048], BF16, at=_off["xd"])
    A_TOP = cur[0]
    assert A_TOP <= SB_LIMIT, f"phase A overflow: {A_TOP}"

    Cc = {}
    cur[0] = PERM_TOP
    for (n, sh, dt_) in [
        ("w_gate", [128, 16 * D], BF16),
        ("probsT", [64, T], BF16),
        ("logits", [128, 64], F32),
        ("prb", [128, 64], BF16),
        ("yrel", [128, 8 * 512], BF16),
        ("retrT", [128, 8 * 512], BF16),
        ("gterm", [128, 512], BF16),
        ("x2slab", [128, 8 * 512], F32),
        ("allsc", [128, 576], F32),
        ("vw", [64, D], BF16),
        ("vvb", [64, D], BF16),
        ("qT", [64, T], BF16),
        ("summ_tok", [128, (T // 128) * SUMD], BF16),
    ]:
        Cc[n] = alloc(n, sh, dt_)
    C_TOP = cur[0]
    assert C_TOP <= SB_LIMIT

    Bb = {}
    cur[0] = PERM_TOP
    TG = 1024
    for (n, sh, dt_) in [
        ("x2f", [128, 8 * 512], F32),
        ("x2tok", [128, 4 * D], F32),
        ("h2", [128, D], F32),
        ("h2q", [128, 4 * D], BF16),
        ("h2qT", [128, 8 * TG], BF16),
        ("cb1rep", [128, D], F32),
        ("sbrep", [128, D], F32),
        ("q1", [128, D], F32),
        ("q2", [128, D], F32),
        ("q3", [128, D], F32),
        ("qm", [128, D], I32),
        ("qsc", [128, 128], F32),
        ("wgu", [128, 2 * 8 * 384], BF16),
        ("wguB", [128, 2 * 8 * 384], BF16),
        ("hq", [128, 22 * TG], BF16),
        ("hffn", [128, 3 * TG], F32),
        ("hffnB", [128, 3 * TG], F32),
        ("hfq", [128, 384], BF16),
        ("wd", [128, 22 * 128], BF16),
        ("outtok", [128, 512], F32),
        ("ftmp", [128, 512], F32),
    ]:
        Bb[n] = alloc(n, sh, dt_)
    B_TOP = cur[0]
    assert B_TOP <= SB_LIMIT, f"phase B overflow: {B_TOP}"

    PS = [es.enter_context(nc.psum_tensor(f"pb{i}", [128, 512], F32)) for i in range(8)]

    ctx = dict(nc=nc, p=p, g=g, Pm=Pm, Aa=Aa, Cc=Cc, Bb=Bb, PS=PS, T=T, NCH=NCH,
               NSUPS=NSUPS, NTILE=NTILE, dbgh=dbgh, SCPAD=SCPAD)

    ph = PHASES
    if ph >= 1:
        phase0(ctx)
    if ph >= 2:
        wquant(ctx)
    if ph >= 3:
        for s in range(NSUPS):
            frontend_sup(ctx, s)
            if ph >= 4:
                for ci in (0, 1):
                    ssd_chunk(ctx, 2 * s + ci)
                    if ph >= 5:
                        chunk_epilogue(ctx, 2 * s + ci)
    if ph >= 6:
        pool_exchange(ctx)
    if ph >= 7:
        phase_c(ctx)
    if ph >= 8:
        phase_b(ctx)

    with nc.Block() as block:
        p.finalize(block, sems)
    es.close()
    assert int(nc.sbuf_base) <= MAP0, \
        f"lazy sbuf alloc moved base into manual map: {nc.sbuf_base} > {MAP0}"
    return nc


def phase0(ctx):
    p, g, Pm, Aa, PS = ctx["p"], ctx["g"], ctx["Pm"], ctx["Aa"], ctx["PS"]

    for (ci_, v_) in ((0, EPS), (1, 1.0 / D), (2, 1.0 / DIN), (3, -1.0)):
        p.op("vector", lambda e, ci_=ci_, v_=v_: e.memset(Pm["kconst"][:, ci_:ci_ + 1], v_),
             writes=["kconst"], tag="kconst")
    p.op("vector", lambda e: e.memset(Pm["srow32"][0:1, 1572:1588], 0.0),
         writes=["scpad0"], tag="scpad0")
    p.dma("scalar", mkap(g["cc_sc_in"].ap(), ctx["T"], [(8, 1), (1, 8)]),
          Pm["srow32"][0:1, 1572:1580], reads=["scpad0"], writes=["scpad"],
          tag="st_scpad")
    p.dma("scalar", mkap(g["cc_pool_in"].ap(), NSLOT * SUMD, [(16, 1), (1, 16)]),
          Pm["srow32"][0:1, 1572:1588], reads=["scpad0"], writes=["poolpad"],
          tag="st_poolpad")
    for dst, src in [("ident", "c_ident"), ("ident32", "c_ident32"),
                     ("onescol", "c_onescol"), ("onesrow32", "c_onesrow32"),
                     ("tril", "c_tril"), ("iota64", "c_iota64"),
                     ("strictup", "c_strictup")]:
        p.dma("sync", Pm[dst][:, :], g[src].ap(), writes=["c_" + dst], tag="ldc")

    for k in range(8):
        p.dma("gpsimd", Aa["w_inproj"][:, k * DINPROJ:(k + 1) * DINPROJ],
              g["in_proj_w"][ts(k), :], writes=["winproj"], tag="ld_inp")
    for k in range(16):
        p.dma("gpsimd", Aa["w_outproj"][:, k * D:(k + 1) * D],
              g["out_proj_w"][ts(k), :], writes=["woutproj_raw"], tag="ld_outp")
    for k in range(8):
        p.dma("gpsimd", Aa["w_sc1"][:, k * SC_HID:(k + 1) * SC_HID],
              g["scorer_w1"][ts(k), :], writes=["wsc1"], tag="ld_sc1")
        p.dma("gpsimd", Aa["w_summ"][:, k * SUMD:(k + 1) * SUMD],
              g["summ_w"][ts(k), :], writes=["wsumm"], tag="ld_summ")
        p.dma("gpsimd", Aa["w_q"][:, k * SUMD:(k + 1) * SUMD],
              g["q_w"][ts(k), :], writes=["wqw"], tag="ld_q")
    load_cols(ctx, Aa["w_s2"], g["scorer_w2"].ap(), SC_HID, "ws2")
    p.dma("gpsimd", Pm["kw"][:, :], g["k_w"][:, :], writes=["wkw"], tag="ld_kw")
    for k in range(18):
        p.dma("sync", Pm["convw"][:, k * DCONV:(k + 1) * DCONV],
              g["conv_w"][ts(k), :], writes=["convw"], tag="ld_convw")
    load_cols(ctx, Pm["convb"], g["conv_b"].ap(), CONV_DIM, "convb")
    load_cols(ctx, Pm["normw"], g["norm_w"].ap(), D, "normw")
    load_cols(ctx, Pm["ssmw"], g["ssm_norm_w"].ap(), DIN, "ssmw")
    p.dma("sync", Pm["dtb"][:, :], mkap(g["dt_bias"].ap(), 0, [(1, 32), (1, 1)]),
          writes=["dtb"], tag="ld_dtb")
    p.dma("sync", Pm["cols"][:32, 0:1], mkap(g["A_log"].ap(), 0, [(1, 32), (1, 1)]),
          writes=["alog"], tag="ld_alog")
    p.dma("sync", Pm["cols"][:32, 1:2], mkap(g["Dvec"].ap(), 0, [(1, 32), (1, 1)]),
          writes=["Dv"], tag="ld_D")
    p.op("tensor", lambda e: e.transpose(PS[7][0:1, 0:32], Pm["cols"][:32, 1:2],
                                         Pm["ident32"][0:32, 0:32]),
         reads=["Dv", "c_ident32"], writes=psq(7, 0), tag="tr_D")
    p.op("vector", lambda e: e.tensor_copy(Pm["srow32"][0:1, 1664:1696], PS[7][0:1, 0:32]),
         reads=psq(7, 0), writes=["Drow"], tag="ev_D")
    p.op("tensor", lambda e: e.matmul(PS[7][:, 0:32], Pm["onesrow32"][:, :],
                                      Pm["srow32"][0:1, 1664:1696], start=True, stop=True),
         reads=["Drow", "c_onesrow32"], writes=psq(7, 0), tag="mm_Drep")
    p.op("vector", lambda e: e.tensor_copy(Pm["Dcol"][:, :], PS[7][:, 0:32]),
         reads=psq(7, 0), writes=["Dcol"], tag="ev_Drep")

    p.op("scalar", lambda e: e.activation(Pm["negA"][:, :], Pm["cols"][:32, 0:1], ACT.Exp,
                                          bias=0.0, scale=1.0),
         reads=["alog"], writes=["negA_"], tag="negA_exp")
    p.op("vector", lambda e: e.tensor_scalar(Pm["negA"][:, :], Pm["negA"][:, :], -1.0,
                                             None, ALU.mult),
         reads=["negA_"], writes=["negA"], tag="negA_neg")

    for k in range(16):
        p.op("vector", lambda e, k=k: e.tensor_scalar(
                Aa["w_outproj"][:, k * D:(k + 1) * D], Aa["w_outproj"][:, k * D:(k + 1) * D],
                Pm["ssmw"][:, k:k + 1], None, ALU.mult),
             reads=["woutproj_raw", "ssmw"], writes=["woutproj"], tag="fold_ssm")

    load_cols(ctx, _ColsView(Pm["cols"], 8), g["cvec"].ap(), D, "ccol")
    p.op("scalar", lambda e: e.activation(Pm["cols"][:, 8:16], Pm["cols"][:, 8:16],
                                          ACT.Silu, bias=0.0, scale=1.0),
         reads=["ccol"], writes=["siluc"], tag="siluc")
    p.op("vector", lambda e: e.tensor_copy(Pm["colsB"][:, 8:16], Pm["cols"][:, 8:16]),
         reads=["siluc"], writes=["silucB"], tag="silucB")

    modrow = Aa["yw"]
    for blk in range(3):
        for k in range(8):
            p.dma("gpsimd", Aa["wmodbuf"][:, :],
                  g["wmod"][ts(k), blk * 2048:(blk + 1) * 2048],
                  writes=["wmodbuf"], tag="ld_wmod")
            for j in range(4):
                p.op("tensor", lambda e, k=k, j=j: e.matmul(
                        PS[4 + j][0:1, :], Pm["colsB"][:, 8 + k:9 + k],
                        Aa["wmodbuf"][:, j * 512:(j + 1) * 512],
                        start=(k == 0), stop=(k == 7)),
                     reads=["wmodbuf", "silucB"], writes=[f"modps{j}"] + psq(4 + j, 0, 4), tag="mm_mod")
        for j in range(4):
            p.op("vector", lambda e, j=j: e.tensor_copy(
                    modrow[0:1, j * 512:(j + 1) * 512], PS[4 + j][0:1, :]),
                 reads=[f"modps{j}"] + psq(4 + j, 0, 4), writes=["modrow"],
                 tag="ev_mod")
        p.dma("scalar", mkap(g["mod_hbm"].ap(), blk * 2048, [(2048, 1), (1, 2048)]),
              modrow[0:1, :], reads=["modrow"], writes=[f"modhbm{blk}"], tag="st_mod")
    load_cols(ctx, _ColsView(Pm["modc"], 0), g["mod_hbm"].ap(), 6 * D, "modc",
              extra_reads=["modhbm0", "modhbm1", "modhbm2"])
    p.op("vector", lambda e: e.tensor_scalar(Pm["modc"][:, 8:16], Pm["modc"][:, 8:16],
                                             1.0, None, ALU.add),
         reads=["modc"], writes=["modc"], tag="ca1")
    p.op("vector", lambda e: e.tensor_scalar(Pm["modc"][:, 32:40], Pm["modc"][:, 32:40],
                                             1.0, None, ALU.add),
         reads=["modc"], writes=["modc"], tag="cb1")
    for c0 in (16, 40):
        cc = Pm["cols"]
        p.op("vector", lambda e, c0=c0: e.tensor_scalar(cc[:, 0:8], Pm["modc"][:, c0:c0 + 8],
                                                        0.25, None, ALU.max),
             reads=["modc"], writes=["gcols"], tag="gc1")
        p.op("vector", lambda e, c0=c0: e.tensor_scalar(cc[:, 8:16], Pm["modc"][:, c0:c0 + 8],
                                                        0.0, None, ALU.is_ge),
             reads=["modc", "gcols"], writes=["gcols"], tag="gc2")
        p.op("vector", lambda e: e.tensor_tensor(cc[:, 0:8], cc[:, 0:8], cc[:, 8:16],
                                                 ALU.mult),
             reads=["gcols"], writes=["gcols"], tag="gc3")
        p.op("vector", lambda e, c0=c0: e.tensor_scalar(cc[:, 8:16], Pm["modc"][:, c0:c0 + 8],
                                                        -0.25, None, ALU.min),
             reads=["modc", "gcols"], writes=["gcols"], tag="gc4")
        p.op("vector", lambda e, c0=c0: e.scalar_tensor_tensor(
                cc[:, 8:16], Pm["modc"][:, c0:c0 + 8], 0.0, cc[:, 8:16],
                ALU.is_lt, ALU.mult),
             reads=["modc", "gcols"], writes=["gcols"], tag="gc5")
        p.op("vector", lambda e, c0=c0: e.tensor_tensor(
                Pm["modc"][:, c0:c0 + 8], cc[:, 0:8], cc[:, 8:16], ALU.add),
             reads=["gcols"], writes=["modc"], tag="gc6")
    p.op("vector", lambda e: e.tensor_tensor(Pm["modc"][:, 48:56], Pm["modc"][:, 16:24],
                                             Pm["modc"][:, 8:16], ALU.mult),
         reads=["modc"], writes=["modc"], tag="gc1col")
    p.op("vector", lambda e: e.tensor_tensor(Pm["modc"][:, 56:64], Pm["modc"][:, 16:24],
                                             Pm["modc"][:, 0:8], ALU.mult),
         reads=["modc"], writes=["modc"], tag="gscol")




def apx(t, off, dims):
    pitch = t.shape[1] if hasattr(t, "shape") else None
    return AP(t, int(off), [[pitch, 128]] + [[int(s), int(n)] for (s, n) in dims])


def apxp(t, off, nparts, dims):
    pitch = t.shape[1]
    return AP(t, int(off), [[pitch, nparts]] + [[int(s), int(n)] for (s, n) in dims])


def rep16(t, col_off, nb):
    return apx(t, col_off, [(1, nb), (0, 16)])


def psq(i, q0, nq=1):
    return [f"ps{i}q{q}" for q in range(q0, q0 + nq)]


def dump(ctx, name, src_ap, reads):
    if name in ctx["dbgh"]:
        ctx["p"].dma("scalar", ctx["dbgh"][name].ap(), src_ap, reads=reads,
                     tag="dbg_" + name)


def quant_chain(ctx, eng, src, dst, ncols, scr, reads, wtag, dst_extra=()):
    p = ctx["p"]
    nb = ncols // 16
    (sct, bmo, sco, rco) = scr["sc"]
    u = wtag
    p.op(eng, lambda e: e.tensor_reduce(sct[:, bmo:bmo + nb],
                                        src.rearrange("p (b k) -> p b k", k=16),
                                        AX.X, ALU.max, apply_absolute_value=True),
         reads=reads, writes=[u + "bm"], tag=u + "bm")
    p.op(eng, lambda e: e.tensor_scalar(sct[:, sco:sco + nb], sct[:, bmo:bmo + nb],
                                        1.0 / 6.0, 1e-12, ALU.mult, ALU.max),
         reads=[u + "bm"], writes=[u + "sc"], tag=u + "sc")
    p.op("vector", lambda e: e.reciprocal(sct[:, rco:rco + nb], sct[:, sco:sco + nb]),
         reads=[u + "sc"], writes=[u + "rc"], tag=u + "rc")
    p.op(eng, lambda e: e.tensor_tensor(scr["xn"], src, rep16(sct, rco, nb), ALU.mult),
         reads=reads + [u + "rc"], writes=[u + "xn"], tag=u + "xn")
    p.op(eng, lambda e: e.scalar_tensor_tensor(scr["a"], scr["xn"], -1.0, scr["xn"],
                                               ALU.mult, ALU.max),
         reads=[u + "xn"], writes=[u + "a"], tag=u + "a")
    p.op(eng, lambda e: e.tensor_scalar(scr["acc"], scr["a"], MIDS[0], None, ALU.is_gt),
         reads=[u + "a"], writes=[u + "acc"], tag=u + "s0")
    for m in MIDS[1:]:
        p.op(eng, lambda e, m=m: e.scalar_tensor_tensor(scr["acc"], scr["a"], m,
                                                        scr["acc"], ALU.is_gt, ALU.add),
             reads=[u + "a"], writes=[u + "acc"], tag=u + "s")
    for (m, w) in ((2.5, 2.0), (3.5, 2.0), (5.0, 4.0)):
        p.op(eng, lambda e, m=m, w=w: e.tensor_scalar(scr["hi"], scr["a"], m, w,
                                                      ALU.is_gt, ALU.mult),
             reads=[u + "a"], writes=[u + "hi"], tag=u + "h")
        p.op(eng, lambda e: e.tensor_tensor(scr["acc"], scr["acc"], scr["hi"], ALU.add),
             reads=[u + "acc", u + "hi"], writes=[u + "acc"], tag=u + "ha")
    p.op(eng, lambda e: e.tensor_scalar(scr["m"], scr["xn"], 0.0, None, ALU.is_lt),
         reads=[u + "xn"], writes=[u + "m"], tag=u + "m")
    p.op(eng, lambda e: e.scalar_tensor_tensor(scr["xn"], scr["acc"], 0.5,
                                               rep16(sct, sco, nb), ALU.mult, ALU.mult),
         reads=[u + "acc", u + "sc"], writes=[u + "qv", u + "xn"], tag=u + "qv")
    p.op(eng, lambda e: e.tensor_scalar(scr["hi"], scr["xn"], -1.0, None, ALU.mult),
         reads=[u + "qv"], writes=[u + "nq", u + "hi"], tag=u + "nq")
    p.op(eng, lambda e: e.tensor_copy(dst, scr["xn"]),
         reads=[u + "qv"], writes=[wtag] + list(dst_extra), tag=u + "sel0")
    p.op(eng, lambda e: e.copy_predicated(dst, scr["m"], scr["hi"]),
         reads=[u + "nq", u + "m", wtag], writes=[wtag] + list(dst_extra),
         tag=u + "sel")



class _ColsView:
    def __init__(self, t, c0):
        self.t, self.c0 = t, c0

    def __getitem__(self, key):
        rows, cols = key
        c = slice(cols.start + self.c0, cols.stop + self.c0)
        return self.t[rows, c]


def load_cols(ctx, dst, src_flat_ap, n, bufw, after=(), extra_reads=()):
    p, Aa, Pm, PS = ctx["p"], ctx["Aa"], ctx["Pm"], ctx["PS"]
    nrow = (n + 1023) // 1024
    stage = Aa["csrep"]
    for r in range(nrow):
        w = min(1024, n - r * 1024)
        p.dma("sync", stage[0:1, 0:w],
              mkap(src_flat_ap, r * 1024, [(w, 1), (1, w)]),
              reads=list(extra_reads), writes=[bufw + "stg", "csrep"], after=after,
              tag=bufw + "stg")
        for cj in range(w // 128):
            j = r * 8 + cj
            q = j % 4
            p.op("tensor", lambda e, cj=cj, q=q: e.transpose(
                    PS[7][:, 256 + q * 32:256 + q * 32 + 1],
                    stage[0:1, cj * 128:(cj + 1) * 128], Pm["ident32"][0:1, 0:1]),
                 reads=[bufw + "stg", "csrep", "c_ident32"], writes=psq(7, 0, 4),
                 tag=bufw + "tr")
            p.op("vector", lambda e, j=j, q=q: e.tensor_copy(
                    dst[:, j:j + 1], PS[7][:, 256 + q * 32:256 + q * 32 + 1]),
                 reads=psq(7, 2 + q // 2), writes=[bufw], tag=bufw + "ev")


def wquant(ctx):
    p, g, Pm, Aa, PS = ctx["p"], ctx["g"], ctx["Pm"], ctx["Aa"], ctx["PS"]
    WQ = WQ_SHARD_ELEMS

    def one_tile(u, src_hbm_ap, ncols, dst_dma_fn, reads_extra):
        src = Aa["x2pT"][:, 0:ncols]
        p.dma("sync", src, src_hbm_ap, writes=[u + "src", "wq_src"], tag=u + "ld")
        if WQL < 2:
            return
        scr = dict(xn=Aa["csrep"][:, 0:ncols], a=Aa["yw"][:, 0:ncols],
                   acc=Aa["yw"][:, 1024:1024 + ncols],
                   hi=Aa["x2pT"][:, 1024:1024 + ncols],
                   sc=(Aa["xsD"], 0, 64, 128),
                   m=Aa["xst"].bitcast(I32)[:, 0:ncols])
        quant_chain(ctx, "vector", src, Aa["ycore"][:, 0:ncols], ncols, scr,
                    [u + "src", "wq_src"] + reads_extra, u, dst_extra=("wq_dst",))
        if WQL < 3:
            return
        ntile = ncols // 128
        for b0 in range(0, ntile, 4):
            nb = min(4, ntile - b0)
            for j in range(nb):
                i = b0 + j
                p.op("tensor", lambda e, i=i, j=j: e.transpose(
                        PS[0].bitcast(BF16)[:, j * 128:j * 128 + 128],
                        Aa["ycore"][:, i * 128:(i + 1) * 128], Pm["ident"][:, :]),
                     reads=[u, "wq_dst", "c_ident"], writes=[f"ps0q{j}", "wqbank"],
                     tag=u + "tr")
            if WQL >= 4:
                p.op("scalar", lambda e, b0=b0, nb=nb: e.copy(
                        Aa["ztil"][:, b0 * 128:(b0 + nb) * 128],
                        PS[0].bitcast(BF16)[:, 0:nb * 128]),
                     reads=[f"ps0q{j}" for j in range(nb)] + ["wqbank"],
                     writes=[u + "tT"] + ["wq_ztil" + str(b0 + j) for j in range(nb)],
                     tag=u + "ev")
            if WQL >= 5:
                for j in range(nb):
                    dst_dma_fn(b0 + j)

    for (mat, base) in (("wg", 0), ("wu", 384 * D)):
        for r in range(3):
            u = f"q{mat}{r}"

            def dst_fn(i, r=r, base=base, u=u):
                p.dma("scalar",
                      mkap(g["cc_wq_in"].ap(), base + i * 128 * 384 + r * 128,
                           [(384, 128), (1, 128)]),
                      Aa["ztil"][:, i * 128:(i + 1) * 128],
                      reads=[u + "tT", "wq_ztil" + str(i)], writes=["wqin"],
                      tag=u + "st")

            src_hbm = g[mat[0:2] + "_shard" if False else ("wg_shard" if mat == "wg" else "wu_shard")][ts(r), :]
            one_tile(u, src_hbm, D, dst_fn, [])
    wdbase = 2 * 384 * D
    for (si, (c0, w)) in enumerate(((0, 1024), (1024, 1024), (2048, 768))):
        u = f"qwd{si}"

        def dst_fn(i, c0=c0, u=u):
            icc = (c0 // 128) + i
            p.dma("scalar",
                  mkap(g["cc_wq_in"].ap(), wdbase + icc * 128 * 128, [(128, 128), (1, 128)]),
                  Aa["ztil"][:, i * 128:(i + 1) * 128],
                  reads=[u + "tT", "wq_ztil" + str(i)], writes=["wqin"],
                  tag=u + "st")

        one_tile(u, g["wd_shard"][:, c0:c0 + w], w, dst_fn, [])

    _last = p.ops[-1]
    for r in ("csrep", "yw", "x2pT", "xsD", "xst", "ycore", "ztil"):
        p.buf(r).readers.append(_last)
    p.coll(lambda e: e.collective_compute(
            "AllGather", ALU.bypass, replica_groups=[list(range(NCORES))],
            ins=[g["cc_wq_in"].ap().opt()], outs=[g["cc_wq_out"].ap().opt()]),
           reads=["wqin"], writes=["wqout"], tag="ag_wq")


def frontend_sup(ctx, s):
    p, g, Pm, Aa, PS = ctx["p"], ctx["g"], ctx["Pm"], ctx["Aa"], ctx["PS"]
    T = ctx["T"]
    if s == 0:
        p.op("vector", lambda e: e.memset(Pm["halo"][:, :], 0.0), writes=["halo_all"],
             tag="halo0")
        p.op("vector", lambda e: e.memset(Pm["S"][:, :], 0.0), writes=["S"], tag="S0")
        for cc in range(18):
            p.buf(f"halo{cc}").writer = p.bufs["halo_all"].writer

    C = Pm["cols"]
    for tt in range(2):
        j = 2 * s + tt
        xb = "xtok"
        p.dma("sync", Aa["x_tok"][:, :], g["x"][ts(j), :], writes=[xb, "u" + xb],
              tag="ld_x")
        p.op("scalar", lambda e: e.activation(Aa["x_tok"][:, :], Aa["x_tok"][:, :],
                                              ACT.Identity, bias=0.0, scale=1.0,
                                              accum_out=C[:, 0:1]),
             reads=[xb], writes=[xb, "lnsum"], tag="ln_s1")
        p.op("scalar", lambda e, tt=tt: e.activation(
                Aa["ztil"][:, tt * 1024:(tt + 1) * 1024], Aa["x_tok"][:, :],
                ACT.Square, bias=0.0, scale=1.0, accum_out=C[:, 1:2]),
             reads=[xb], writes=["lnsq", "ztil"], tag="ln_s2")
        p.op("vector", lambda e: e.tensor_scalar(C[:, 2:3], C[:, 0:1], 1.0 / D, None,
                                                 ALU.mult),
             reads=["lnsum"], writes=["lnm"], tag="ln_m")
        p.op("vector", lambda e: e.tensor_tensor(C[:, 3:4], C[:, 2:3], C[:, 2:3],
                                                 ALU.mult),
             reads=["lnm"], writes=["lnm2"], tag="ln_m2")
        p.op("vector", lambda e: e.scalar_tensor_tensor(C[:, 4:5], C[:, 1:2], 1.0 / D,
                                                        C[:, 3:4], ALU.mult, ALU.subtract),
             reads=["lnsq", "lnm2"], writes=["lnvar"], tag="ln_var")
        p.op("scalar", lambda e: e.activation(C[:, 5:6], C[:, 4:5], ACT.Sqrt,
                                              bias=Pm["kconst"][:, 0:1], scale=1.0),
             reads=["lnvar", "kconst"], writes=["lnsd"], tag="ln_sd")
        p.op("vector", lambda e: e.reciprocal(C[:, 6:7], C[:, 5:6]),
             reads=["lnsd"], writes=["lnrstd"], tag="ln_rstd")
        if FRL == 11:
            continue
        for k in range(8):
            bk, qt = 4 + k // 4, k % 4
            p.op("tensor", lambda e, k=k, bk=bk, qt=qt: e.transpose(
                    PS[bk][:, qt * 128:(qt + 1) * 128],
                    Aa["x_tok"][:, k * 128:(k + 1) * 128], Pm["ident32"][:, :]),
                 reads=[xb, "c_ident32"], writes=psq(bk, 0, 4), tag="tr_x")
        if FRL == 12:
            continue
        p.op("vector", lambda e: e.tensor_scalar(Aa["x_tok"][:, :], Aa["x_tok"][:, :],
                                                 C[:, 2:3], C[:, 6:7],
                                                 ALU.subtract, ALU.mult),
             reads=[xb, "lnm", "lnrstd"], writes=["u" + xb, xb], tag="ln_u")
        for k in range(8):
            bk, qt = 6 + k // 4, k % 4
            p.op("tensor", lambda e, k=k, bk=bk, qt=qt: e.transpose(
                    PS[bk][:, qt * 128:(qt + 1) * 128],
                    Aa["x_tok"][:, k * 128:(k + 1) * 128], Pm["ident32"][:, :]),
                 reads=["u" + xb, "c_ident32"], writes=psq(bk, 0, 4), tag="tr_u")
        if FRL == 13:
            continue
        for k in range(8):
            bku, qtu = 6 + k // 4, k % 4
            bkx, qtx = 4 + k // 4, k % 4
            hsl = Aa["hT"][:, k * SUP + tt * 128: k * SUP + tt * 128 + 128]
            x2sl = Aa["x2pT"][:, k * SUP + tt * 128: k * SUP + tt * 128 + 128]
            p.op("scalar", lambda e, k=k, bku=bku, qtu=qtu, hsl=hsl: e.activation(
                    hsl, PS[bku][:, qtu * 128:(qtu + 1) * 128], ACT.Identity,
                    bias=Pm["modc"][:, k:k + 1], scale=Pm["modc"][:, 8 + k:9 + k]),
                 reads=psq(bku, qtu) + ["modc"], writes=["hT"], tag="h_ev")
            if FRL == 14:
                continue
            p.op("scalar", lambda e, k=k, tt=tt, bku=bku, qtu=qtu: e.activation(
                    Aa["ztil"][:, 2048 + k * SUP + tt * 128: 2048 + k * SUP + tt * 128 + 128],
                    PS[bku][:, qtu * 128:(qtu + 1) * 128], ACT.Square,
                    bias=Pm["modc"][:, k:k + 1], scale=Pm["modc"][:, 8 + k:9 + k]),
                 reads=psq(bku, qtu) + ["modc"], writes=["ztil"], tag="h_sq")
            if FRL == 15:
                continue
            p.op("scalar", lambda e, k=k, bkx=bkx, qtx=qtx, x2sl=x2sl: e.copy(
                    x2sl, PS[bkx][:, qtx * 128:(qtx + 1) * 128]),
                 reads=psq(bkx, qtx), writes=["x2pT"], tag="x2p0")
            if FRL == 16:
                continue
            p.op("vector", lambda e, k=k, hsl=hsl, x2sl=x2sl:
                 e.scalar_tensor_tensor(
                    x2sl, hsl, Pm["modc"][:, 16 + k:17 + k], x2sl,
                    ALU.mult, ALU.add),
                 reads=["hT", "modc", "x2pT"], writes=["x2pT"], tag="x2p1")
    if FRL < 2:
        return
    for k in range(8):
        p.op("tensor", lambda e, k=k: e.matmul(
                PS[0][0:1, 0:256], Pm["onescol"][:, :],
                Aa["ztil"][:, 2048 + k * SUP: 2048 + (k + 1) * SUP],
                start=(k == 0), stop=(k == 7)),
             reads=["ztil", "c_onescol"], writes=psq(0, 0, 4), tag="mm_rms")
    p.op("scalar", lambda e: e.activation(Pm["srow32"][0:1, 640:896], PS[0][0:1, 0:256],
                                          ACT.Sqrt, bias=Pm["kconst"][0:1, 0:1],
                                          scale=Pm["kconst"][0:1, 1:2]),
         reads=psq(0, 0, 2) + ["kconst"], writes=["rmsd"], tag="rms_sd")
    p.op("vector", lambda e: e.reciprocal(Pm["srow32"][0:1, 896:1152],
                                          Pm["srow32"][0:1, 640:896]),
         reads=["rmsd"], writes=["rmsr"], tag="rms_r")
    p.op("tensor", lambda e: e.matmul(PS[1][:, 0:256], Pm["onesrow32"][:, :],
                                      Pm["srow32"][0:1, 896:1152], start=True, stop=True),
         reads=["rmsr", "c_onesrow32"], writes=psq(1, 0, 2), tag="mm_rrep")
    for k in range(8):
        p.op("vector", lambda e, k=k: e.tensor_tensor(
                Aa["hT"][:, k * SUP:(k + 1) * SUP], Aa["hT"][:, k * SUP:(k + 1) * SUP],
                PS[1][:, 0:256], ALU.mult),
             reads=["hT"] + psq(1, 0, 2), writes=["hT"], tag="xn1")
        p.op("vector", lambda e, k=k: e.tensor_scalar(
                Aa["hT"][:, k * SUP:(k + 1) * SUP], Aa["hT"][:, k * SUP:(k + 1) * SUP],
                Pm["normw"][:, k:k + 1], None, ALU.mult),
             reads=["hT", "normw"], writes=["hT"], tag="xn2")
    if s == 0:
        dump(ctx, "hT", Aa["hT"][:, :], ["hT"])
        dump(ctx, "xnT", Aa["hT"][:, :], ["hT"])

    if FRL < 3:
        return
    def _cs_rows():
        for ci in range(2):
            c = 2 * s + ci
            b0 = ci * L
            p.op("vector", lambda e, b0=b0: e.tensor_scalar(
                    Pm["srow32"][:, 256:256 + L], Aa["dsup"][:, b0:b0 + L],
                    Pm["negA"][:, :], None, ALU.mult),
                 reads=["dsup", "negA"], writes=["dA"], tag="dA")
            p.op("vector", lambda e, b0=b0: e.tensor_tensor_scan(
                    Aa["cssup"][:, b0:b0 + L], Pm["srow32"][:, 256:256 + L],
                    Pm["srow32"][:, 256:256 + L], 0.0, ALU.add, ALU.bypass),
                 reads=["dA"], writes=["cssup"], tag="cs")
            p.op("scalar", lambda e, b0=b0: e.activation(
                    Aa["essup"][:, b0:b0 + L], Aa["cssup"][:, b0:b0 + L], ACT.Exp,
                    bias=0.0, scale=1.0),
                 reads=["cssup"], writes=["essup"], tag="ecs")
            p.op("vector", lambda e, b0=b0: e.tensor_copy(
                    Pm["srow32"][:, 384:385], Aa["cssup"][:, b0 + L - 1:b0 + L]),
                 reads=["cssup"], writes=["ncl"], tag="ncl")
            p.op("scalar", lambda e, b0=b0: e.activation(
                    Aa["dssup"][:, b0:b0 + L], Aa["cssup"][:, b0:b0 + L], ACT.Exp,
                    bias=Pm["srow32"][:, 384:385], scale=Pm["kconst"][0:32, 3:4]),
                 reads=["cssup", "ncl", "kconst"], writes=["dssup"], tag="ds")
            p.op("scalar", lambda e, b0=b0: e.activation(
                    Pm["srow32"][:, 385:386], Aa["cssup"][:, b0 + L - 1:b0 + L], ACT.Exp,
                    bias=0.0, scale=1.0),
                 reads=["cssup"], writes=["dcc"], tag="dc")
            p.op("vector", lambda e, b0=b0: e.tensor_tensor(
                    Pm["srow32"][:, 512:512 + L], Aa["dsup"][:, b0:b0 + L],
                    Aa["dssup"][:, b0:b0 + L], ALU.mult),
                 reads=["dsup", "dssup"], writes=["ddt"], tag="ddt")
            p.dma("scalar", mkap(g["hbm_rows"].ap(), c * 8192, [(128, 32), (1, 128)]),
                  Aa["cssup"][:, b0:b0 + L], reads=["cssup"], writes=[f"cshbm{c}"], tag="st_cs")
            p.dma("scalar", mkap(g["hbm_rows"].ap(), c * 8192 + 4096, [(128, 32), (1, 128)]),
                  Aa["essup"][:, b0:b0 + L], reads=["essup"], writes=[f"eshbm{c}"], tag="st_es")

    for oc in [34] + list(range(34)):
        bk = 2 + oc % 6
        wdt = 128 if oc < 34 else 32
        pslice = PS[bk][0:wdt, 0:256]
        pb = psq(bk, 0, 4)
        for k in range(8):
            p.op("tensor", lambda e, oc=oc, k=k, pslice=pslice, wdt=wdt: e.matmul(
                    pslice, Aa["w_inproj"][:, k * DINPROJ + oc * 128:
                                           k * DINPROJ + oc * 128 + wdt],
                    Aa["hT"][:, k * SUP:(k + 1) * SUP],
                    start=(k == 0), stop=(k == 7)),
                 reads=["hT", "winproj"], writes=pb, tag=f"mm_ip{oc}")
        if FRL < 4:
            continue
        if oc < 16:
            p.op("scalar", lambda e, oc=oc, pslice=pslice: e.activation(
                    Aa["ztil"][:, oc * SUP:(oc + 1) * SUP], pslice, ACT.Silu,
                    bias=0.0, scale=1.0),
                 reads=pb, writes=["ztil"], tag="z_silu")
        elif oc < 34:
            cc = oc - 16
            acc = Aa["acc0"]
            accb = "acc0"
            wv = Pm["convw"]
            p.op("vector", lambda e, cc=cc, acc=acc, pslice=pslice: e.tensor_scalar(
                    acc[:, 0:SUP], pslice, wv[:, cc * 4 + 3:cc * 4 + 4],
                    Pm["convb"][:, cc:cc + 1], ALU.mult, ALU.add),
                 reads=pb + ["convw", "convb"], writes=[accb], tag="cv3")
            for kk in range(3):
                dsh = 3 - kk
                p.op("vector", lambda e, cc=cc, kk=kk, dsh=dsh, acc=acc, pslice=pslice:
                     e.scalar_tensor_tensor(
                        acc[:, dsh:SUP], pslice[:, 0:SUP - dsh],
                        wv[:, cc * 4 + kk:cc * 4 + kk + 1], acc[:, dsh:SUP],
                        ALU.mult, ALU.add),
                     reads=pb + ["convw", accb], writes=[accb], tag=f"cv{kk}")
                p.op("vector", lambda e, cc=cc, kk=kk, dsh=dsh, acc=acc:
                     e.scalar_tensor_tensor(
                        acc[:, 0:dsh], Pm["halo"][:, cc * 3 + kk:cc * 3 + kk + dsh],
                        wv[:, cc * 4 + kk:cc * 4 + kk + 1], acc[:, 0:dsh],
                        ALU.mult, ALU.add),
                     reads=["convw", f"halo{cc}", accb], writes=[accb], tag=f"cvh{kk}")
            p.op("vector", lambda e, cc=cc, pslice=pslice: e.tensor_copy(
                    Pm["halo"][:, cc * 3:cc * 3 + 3], pslice[:, SUP - 3:SUP]),
                 reads=pb, writes=[f"halo{cc}"], tag="cv_halo")
            if cc < 16:
                dst = Aa["xst"][:, cc * SUP:(cc + 1) * SUP]
                wb = "xst"
            else:
                dst = Aa["bct"][:, (cc - 16) * SUP:(cc - 15) * SUP]
                wb = "bct"
            p.op("scalar", lambda e, acc=acc, dst=dst: e.activation(
                    dst, acc[:, 0:SUP], ACT.Silu, bias=0.0, scale=1.0),
                 reads=[accb], writes=[wb], tag="cv_silu")
        else:
            p.op("scalar", lambda e, pslice=pslice: e.activation(
                    Pm["srow32"][:, 0:SUP], pslice, ACT.Exp,
                    bias=Pm["dtb"][:, :], scale=1.0),
                 reads=pb + ["dtb"], writes=["dtexp"], tag="dt_exp")
            p.op("scalar", lambda e: e.activation(Aa["dsup"][:, :], Pm["srow32"][:, 0:SUP],
                                                  ACT.Ln, bias=1.0, scale=1.0),
                 reads=["dtexp"], writes=["dsup"], tag="dt_ln")
            if FRL >= 5:
                _cs_rows()
    if s == 0:
        dump(ctx, "xst", Aa["xst"][:, :], ["xst"])
        dump(ctx, "bct", Aa["bct"][:, :], ["bct"])
        dump(ctx, "dsup", Aa["dsup"][:, :], ["dsup"])
        dump(ctx, "cssup", Aa["cssup"][:, :], ["cssup"])


def ssd_chunk(ctx, c):
    p, g, Pm, Aa, PS = ctx["p"], ctx["g"], ctx["Pm"], ctx["Aa"], ctx["PS"]
    s, ci = c // 2, c % 2
    b0 = ci * L

    srcs = [(Aa["cssup"][:, b0:b0 + L], "cssup", Aa["cstok"], "cstok"),
            (Aa["dsup"][:, b0:b0 + L], "dsup", Aa["dttok"], "dttok"),
            (Aa["dssup"][:, b0:b0 + L], "dssup", Aa["dstok"], "dstok"),
            (Pm["srow32"][:, 512 + ci * L:512 + ci * L + L] if False else
             Pm["srow32"][:, 512:512 + L], "ddt", Aa["ddtok"], "ddtok")]
    p.op("vector", lambda e: e.tensor_tensor(
            Pm["srow32"][:, 512:512 + L], Aa["dsup"][:, b0:b0 + L],
            Aa["dssup"][:, b0:b0 + L], ALU.mult),
         reads=["dsup", "dssup"], writes=["ddt"], tag="ddt")
    for (i, (srcap, rb, dstt, db)) in enumerate(srcs):
        p.op("tensor", lambda e, srcap=srcap, i=i: e.transpose(
                PS[7][:, i * 128:i * 128 + 32], srcap, Pm["ident32"][0:32, 0:32]),
             reads=[rb, "c_ident32"], writes=psq(7, 0, 4), tag="tr_row")
        p.op("vector", lambda e, dstt=dstt, i=i: e.tensor_copy(
                dstt[:, :], PS[7][:, i * 128:i * 128 + 32]),
             reads=psq(7, i), writes=[db], tag="ev_row")
    p.op("scalar", lambda e: e.activation(
            Pm["srow32"][:, 385:386], Aa["cssup"][:, b0 + L - 1:b0 + L], ACT.Exp,
            bias=0.0, scale=1.0),
         reads=["cssup"], writes=["dcc"], tag="dc")
    p.op("tensor", lambda e: e.transpose(PS[7][0:1, 448:480], Pm["srow32"][:, 385:386],
                                         Pm["ident32"][0:32, 0:32]),
         reads=["dcc", "c_ident32"], writes=psq(7, 0, 4), tag="tr_dc")
    p.op("vector", lambda e: e.tensor_copy(Pm["srow32"][0:1, 1408:1440],
                                           PS[7][0:1, 448:480]),
         reads=psq(7, 3), writes=["dcrow"], tag="ev_dc")
    p.op("tensor", lambda e: e.matmul(PS[7][:, 448:480], Pm["onesrow32"][:, :],
                                      Pm["srow32"][0:1, 1408:1440], start=True, stop=True),
         reads=["dcrow", "c_onesrow32"], writes=psq(7, 0, 4), tag="mm_dcrep")
    p.op("vector", lambda e: e.tensor_copy(Pm["dcrep"][:, :], PS[7][:, 448:480]),
         reads=psq(7, 3), writes=["dcrep"], tag="ev_dcrep")

    p.op("tensor", lambda e: e.matmul(PS[0][:, 0:128],
                                      Aa["bct"][:, ci * L:ci * L + L],
                                      Aa["bct"][:, SUP + ci * L:SUP + ci * L + L],
                                      start=True, stop=True),
         reads=["bct"], writes=psq(0, 0, 2), tag="mm_G")
    p.op("vector", lambda e: e.tensor_tensor(Aa["Gt"][:, :], PS[0][:, 0:128],
                                             Pm["tril"][:, :], ALU.mult),
         reads=psq(0, 0) + ["c_tril"], writes=["Gt"], tag="ev_G")
    p.op("tensor", lambda e: e.transpose(PS[0].bitcast(BF16)[:, 256:384],
                                         Aa["bct"][:, ci * L:ci * L + L],
                                         Pm["ident"][:, :]),
         reads=["bct", "c_ident"], writes=psq(0, 0, 2), tag="tr_btok")
    p.op("vector", lambda e: e.tensor_copy(Aa["btok"][:, :],
                                           PS[0].bitcast(BF16)[:, 256:384]),
         reads=psq(0, 1), writes=["btok"], tag="ev_btok")
    for w in range(4):
        for i in range(4):
            kk = 4 * w + i
            p.op("tensor", lambda e, kk=kk, i=i: e.transpose(
                    PS[1].bitcast(BF16)[:, i * 128:i * 128 + 128],
                    Aa["xst"][:, kk * SUP + ci * L: kk * SUP + ci * L + L],
                    Pm["ident"][:, :]),
                 reads=["xst", "c_ident"], writes=psq(1, i // 2), tag="tr_xs")
        p.op("vector", lambda e, w=w: e.tensor_tensor(
                Aa["xd"][:, w * 512:(w + 1) * 512],
                PS[1].bitcast(BF16)[:, 0:512],
                apx(Aa["dttok"], 8 * w, [(1, 8), (0, 64)]), ALU.mult),
             reads=psq(1, 0, 2) + ["dttok"], writes=["xd"], tag="ev_xd")
        p.op("vector", lambda e, w=w: e.tensor_tensor(
                Aa["xdp"][:, w * 512:(w + 1) * 512], Aa["xd"][:, w * 512:(w + 1) * 512],
                apx(Aa["dstok"], 8 * w, [(1, 8), (0, 64)]), ALU.mult),
             reads=["xd", "dstok"], writes=["xdp"], tag="ev_xdp")

    for qh in range(4):
        p.dma("sync", Aa["csrep"][:, :],
              mkap(g["hbm_rows"].ap(), c * 8192 + qh * 1024,
                   [(0, 128), (128, 8), (1, 128)]),
              reads=[f"cshbm{c}"], writes=["csrep"], tag="bc_cs")
        for eh in range(2):
            p.dma("sync", Aa["erep2"][eh * 64:(eh + 1) * 64, :],
                  mkap(g["hbm_rows"].ap(), c * 8192 + 4096 + qh * 1024 + eh * 128,
                       [(0, 64), (256, 4), (1, 128)]),
                  reads=[f"eshbm{c}"], writes=["erep2"], tag="bc_e")
        p.op("vector", lambda e, qh=qh: e.tensor_tensor(
                Aa["csrep"][:, :], Aa["csrep"][:, :],
                apx(Aa["cstok"], qh * 8, [(1, 8), (0, 128)]), ALU.subtract),
             reads=["csrep", "cstok"], writes=["csrep"], tag="seg")
        p.op("vector", lambda e: e.tensor_scalar(Aa["csrep"][:, :], Aa["csrep"][:, :],
                                                 0.0, None, ALU.min),
             reads=["csrep"], writes=["csrep"], tag="seg_cl")
        p.op("scalar", lambda e: e.activation(Aa["csrep"][:, :], Aa["csrep"][:, :],
                                              ACT.Exp, bias=0.0, scale=1.0),
             reads=["csrep"], writes=["csrep"], tag="seg_exp")
        p.op("vector", lambda e: e.tensor_tensor(
                Aa["Mt"][:, :], Aa["csrep"][:, :],
                apx(Aa["Gt"], 0, [(0, 8), (1, 128)]), ALU.mult),
             reads=["csrep", "Gt"], writes=["Mt"], tag="M")
        for i in range(4):
            cc = qh * 4 + i
            for half in range(2):
                pr = slice(half * 64, half * 64 + 64)
                p.op("vector", lambda e, cc=cc, i=i, pr=pr, half=half: e.tensor_scalar(
                        Aa["xsD"][pr, i * 128:(i + 1) * 128],
                        Aa["xst"][pr, cc * SUP + ci * L: cc * SUP + ci * L + L],
                        Pm["Dcol"][pr, 2 * cc + half:2 * cc + half + 1], None, ALU.mult),
                     reads=["xst", "Dcol"], writes=["xsD"], tag="xsD")
        for j in range(4):
            jj = qh * 4 + j
            for half in range(2):
                h = 2 * jj + half
                if c > 0:
                    p.op("tensor", lambda e, h=h, j=j, half=half: e.matmul(
                            PS[2][half * 64:half * 64 + 64, j * 128:(j + 1) * 128],
                            Aa["Sb"][:, h * 64:(h + 1) * 64],
                            Aa["bct"][:, SUP + ci * L:SUP + ci * L + L],
                            start=True, stop=True, tile_position=(0, half * 64)),
                         reads=["Sb", "bct"], writes=psq(2, j), tag="mm_yo")
                p.op("tensor", lambda e, h=h, j=j, half=half: e.matmul(
                        PS[4][half * 64:half * 64 + 64, j * 128:(j + 1) * 128],
                        Aa["xd"][:, h * 64:(h + 1) * 64],
                        Aa["Mt"][:, j * 256 + half * 128: j * 256 + half * 128 + 128],
                        start=True, stop=True, tile_position=(0, half * 64)),
                     reads=["xd", "Mt"], writes=psq(4, j), tag="mm_yd")
        ysl = Aa["yw"][:, qh * 512:(qh + 1) * 512]
        if c > 0:
            p.op("vector", lambda e, ysl=ysl: e.tensor_tensor(
                    ysl, PS[2][:, :], Aa["erep2"][:, :], ALU.mult),
                 reads=psq(2, 0, 4) + ["erep2"], writes=["yw"], tag="as_yo")
            p.op("vector", lambda e, ysl=ysl: e.tensor_tensor(
                    ysl, ysl, PS[4][:, :], ALU.add),
                 reads=psq(4, 0, 4) + ["yw"], writes=["yw"], tag="as_yd")
        else:
            p.op("vector", lambda e, ysl=ysl: e.tensor_copy(ysl, PS[4][:, :]),
                 reads=psq(4, 0, 4), writes=["yw"], tag="as_yd0")
        p.op("vector", lambda e, ysl=ysl: e.tensor_tensor(
                ysl, ysl, Aa["xsD"][:, :], ALU.add),
             reads=["xsD", "yw"], writes=["yw"], tag="as_xsD")
        p.op("vector", lambda e, ysl=ysl, qh=qh: e.tensor_tensor(
                ysl, ysl,
                apx(Aa["ztil"], (qh * 4) * SUP + ci * L, [(SUP, 4), (1, 128)]),
                ALU.mult),
             reads=["ztil", "yw"], writes=["yw"], tag="as_z")
    p.op("scalar", lambda e: e.activation(Aa["hT"][:, :], Aa["yw"][:, :], ACT.Square,
                                          bias=0.0, scale=1.0),
         reads=["yw"], writes=["hT"], tag="ysq")
    for i in range(16):
        p.op("tensor", lambda e, i=i: e.matmul(
                PS[6][0:1, 0:128], Pm["onescol"][:, :],
                Aa["hT"][:, i * 128:(i + 1) * 128],
                start=(i == 0), stop=(i == 15)),
             reads=["hT", "c_onescol"], writes=psq(6, 0, 4), tag="mm_ysq")
    p.op("scalar", lambda e: e.activation(Pm["srow32"][0:1, 1152:1280], PS[6][0:1, 0:128],
                                          ACT.Sqrt, bias=Pm["kconst"][0:1, 0:1],
                                          scale=Pm["kconst"][0:1, 2:3]),
         reads=psq(6, 0) + ["kconst"], writes=["yrms"], tag="yrms_sd")
    p.op("vector", lambda e: e.reciprocal(Pm["srow32"][0:1, 1280:1408],
                                          Pm["srow32"][0:1, 1152:1280]),
         reads=["yrms"], writes=["yrmsr"], tag="yrms_r")
    p.op("tensor", lambda e: e.matmul(PS[6][:, 128:256], Pm["onesrow32"][:, :],
                                      Pm["srow32"][0:1, 1280:1408], start=True, stop=True),
         reads=["yrmsr", "c_onesrow32"], writes=psq(6, 1), tag="mm_yrrep")
    p.op("vector", lambda e: e.tensor_copy(Aa["rstd_rep"][:, :], PS[6][:, 128:256]),
         reads=psq(6, 1), writes=["rstd_rep"], tag="ev_yrrep")
    p.op("vector", lambda e: e.tensor_tensor(
            Aa["ycore"][:, :], Aa["yw"][:, :],
            apx(Aa["rstd_rep"], 0, [(0, 16), (1, 128)]), ALU.mult),
         reads=["yw", "rstd_rep"], writes=["ycore"], tag="ycore")
    for w in range(4):
        bk = 2 + w
        p.op("tensor", lambda e, w=w, bk=bk: e.matmul(
                PS[bk][:, :], Aa["btok"][:, :], Aa["xdp"][:, w * 512:(w + 1) * 512],
                start=True, stop=True),
             reads=["btok", "xdp"], writes=psq(bk, 0, 4), tag="mm_st")
        p.op("vector", lambda e, w=w: e.tensor_tensor(
                Pm["S"][:, w * 512:(w + 1) * 512], Pm["S"][:, w * 512:(w + 1) * 512],
                apx(Pm["dcrep"], 8 * w, [(1, 8), (0, 64)]), ALU.mult),
             reads=["dcrep", "S"], writes=["S"], tag="S_dec")
        p.op("vector", lambda e, w=w, bk=bk: e.tensor_tensor(
                Pm["S"][:, w * 512:(w + 1) * 512], Pm["S"][:, w * 512:(w + 1) * 512],
                PS[bk][:, :], ALU.add),
             reads=psq(bk, 0, 4) + ["S"], writes=["S"], tag="S_add")
        p.op("scalar", lambda e, w=w: e.copy(
                Aa["Sb"][:, w * 512:(w + 1) * 512], Pm["S"][:, w * 512:(w + 1) * 512]),
             reads=["S"], writes=["Sb"], tag="S_cast")
    if c == 0:
        dump(ctx, "ycore0", Aa["ycore"][:, :], ["ycore"])
        dump(ctx, "yw0", Aa["yw"][:, :], ["yw"])
        dump(ctx, "xd0", Aa["xd"][:, :], ["xd"])
        dump(ctx, "Mt0", Aa["Mt"][:, :], ["Mt"])
    if c == 1:
        dump(ctx, "S1", Pm["S"][:, :], ["S"])
        dump(ctx, "ycore1", Aa["ycore"][:, :], ["ycore"])


def chunk_epilogue(ctx, c):
    p, g, Pm, Aa, PS = ctx["p"], ctx["g"], ctx["Pm"], ctx["Aa"], ctx["PS"]
    T = ctx["T"]
    s, ci = c // 2, c % 2
    for kd in range(8):
        bk = 6 + kd % 2
        for kh in range(16):
            p.op("tensor", lambda e, kd=kd, kh=kh, bk=bk: e.matmul(
                    PS[bk][:, 0:128],
                    Aa["w_outproj"][:, kh * D + kd * 128: kh * D + kd * 128 + 128],
                    Aa["ycore"][:, kh * 128:(kh + 1) * 128],
                    start=(kh == 0), stop=(kh == 15)),
                 reads=["ycore", "woutproj"], writes=psq(bk, 0, 4), tag="mm_op")
        p.op("scalar", lambda e, kd=kd, bk=bk: e.copy(
                Aa["yTb"][:, kd * 128:(kd + 1) * 128], PS[bk][:, 0:128]),
             reads=psq(bk, 0), writes=["yTb"], tag="ev_y")
        p.op("vector", lambda e, kd=kd, ci=ci: e.scalar_tensor_tensor(
                Aa["x2pT"][:, kd * SUP + ci * L: kd * SUP + ci * L + L],
                Aa["yTb"][:, kd * 128:(kd + 1) * 128], Pm["modc"][:, 16 + kd:17 + kd],
                Aa["x2pT"][:, kd * SUP + ci * L: kd * SUP + ci * L + L],
                ALU.mult, ALU.add),
             reads=["yTb", "modc", "x2pT"], writes=["x2pT"], tag="x2p2")
    for hc in range(2):
        for kd in range(8):
            p.op("tensor", lambda e, hc=hc, kd=kd: e.matmul(
                    PS[hc][:, 0:128],
                    Aa["w_sc1"][:, kd * SC_HID + hc * 128: kd * SC_HID + hc * 128 + 128],
                    Aa["yTb"][:, kd * 128:(kd + 1) * 128],
                    start=(kd == 0), stop=(kd == 7)),
                 reads=["yTb", "wsc1"], writes=psq(hc, 0, 4), tag="mm_s1")
        p.op("scalar", lambda e, hc=hc: e.activation(
                Aa["r1"][:, hc * 128:(hc + 1) * 128], PS[hc][:, 0:128],
                ACT.Relu, bias=0.0, scale=1.0),
             reads=psq(hc, 0), writes=["r1"], tag="ev_s1")
    for hc in range(2):
        p.op("tensor", lambda e, hc=hc: e.matmul(
                PS[0][0:1, 256:384], Aa["w_s2"][:, hc:hc + 1],
                Aa["r1"][:, hc * 128:(hc + 1) * 128],
                start=(hc == 0), stop=(hc == 1)),
             reads=["r1", "ws2"], writes=psq(0, 0, 4), tag="mm_s2")
    p.op("scalar", lambda e: e.activation(
            Pm["srow32"][0:1, 1440:1568], PS[0][0:1, 256:384],
            ACT.Sigmoid, bias=0.0, scale=1.0),
         reads=psq(0, 2), writes=["scrow"], tag="ev_s2")
    p.dma("scalar", mkap(g["cc_sc_in"].ap(), c * 128, [(128, 1), (1, 128)]),
          Pm["srow32"][0:1, 1440:1568], reads=["scrow"], writes=["scin"], tag="st_sc")
    for kd in range(8):
        p.op("tensor", lambda e, kd=kd: e.matmul(
                PS[1][0:64, 0:128], Aa["w_summ"][:, kd * SUMD:(kd + 1) * SUMD],
                Aa["yTb"][:, kd * 128:(kd + 1) * 128],
                start=(kd == 0), stop=(kd == 7)),
             reads=["yTb", "wsumm"], writes=psq(1, 0, 4), tag="mm_sm")
    p.op("vector", lambda e: e.tensor_copy(Pm["sb6464"][:, :], PS[1][0:64, 0:128]),
         reads=psq(1, 0), writes=["sb6464"], tag="ev_sm")
    p.op("tensor", lambda e: e.transpose(PS[1].bitcast(BF16)[:, 512:576],
                                         Pm["sb6464"][:, :], Pm["ident"][0:64, 0:64]),
         reads=["sb6464", "c_ident"], writes=psq(1, 2), tag="tr_sm")
    p.op("vector", lambda e: e.tensor_copy(Aa["r1"][:, 0:64],
                                           PS[1].bitcast(BF16)[:, 512:576]),
         reads=psq(1, 2), writes=["r1"], tag="ev_sm2")
    p.dma("scalar", g["summ_spill"][ts(c), :], Aa["r1"][:, 0:64],
          reads=["r1"], writes=["smspill"], tag="st_sm")
    for kd in range(8):
        p.op("tensor", lambda e, kd=kd: e.matmul(
                PS[0][0:64, 0:128], Aa["w_q"][:, kd * SUMD:(kd + 1) * SUMD],
                Aa["yTb"][:, kd * 128:(kd + 1) * 128],
                start=(kd == 0), stop=(kd == 7)),
             reads=["yTb", "wqw"], writes=psq(0, 0, 4), tag="mm_q")
    p.op("scalar", lambda e: e.copy(Pm["sb6464"][:, :], PS[0][0:64, 0:128]),
         reads=psq(0, 0), writes=["sb6464"], tag="ev_q")
    p.dma("scalar", mkap(g["q_spill"].ap(), c * 128, [(T, 64), (1, 128)]),
          Pm["sb6464"][:, :], reads=["sb6464"], writes=["qspill"], tag="st_q")
    p.dma("sync", mkap(g["y_spill"].ap(), c * 128, [(T, 128), (128 * T, 8), (1, 128)]),
          Aa["yTb"][:, :], reads=["yTb"], writes=["yspill"], tag="st_y")
    if ci == 1:
        p.dma("sync", mkap(g["x2p_spill"].ap(), s * SUP,
                             [(T, 128), (128 * T, 8), (1, SUP)]),
              Aa["x2pT"][:, :], reads=["x2pT"], writes=["x2spill"], tag="st_x2p")
    if c == 0:
        dump(ctx, "yTb0", Aa["yTb"][:, :], ["yTb"])


def pool_exchange(ctx):
    p, g, Pm, Aa, Cc, PS = ctx["p"], ctx["g"], ctx["Pm"], ctx["Aa"], ctx["Cc"], ctx["PS"]
    T, NTILE, SCPAD = ctx["T"], ctx["NTILE"], ctx["SCPAD"]
    ALLT = NCORES * T
    fenceA = p.fence()
    _orig_op = p.op

    def _serial_op(eng, fn, reads=(), writes=(), after=(), **kw):
        prev = p.ops[-1] if p.ops else None
        o = _orig_op(eng, fn, reads=reads, writes=writes,
                     after=list(after) + ([prev] if prev is not None else []), **kw)
        return o
    p.op = _serial_op
    try:
        _pool_exchange_body(ctx, fenceA)
    finally:
        p.op = _orig_op


def _pool_exchange_body(ctx, fenceA):
    p, g, Pm, Aa, Cc, PS = ctx["p"], ctx["g"], ctx["Pm"], ctx["Aa"], ctx["Cc"], ctx["PS"]
    T, NTILE, SCPAD = ctx["T"], ctx["NTILE"], ctx["SCPAD"]
    ALLT = NCORES * T

    p.coll(lambda e: e.collective_compute(
            "AllGather", ALU.bypass, replica_groups=[list(range(NCORES))],
            ins=[g["cc_sc_in"].ap().opt()], outs=[g["cc_sc_out"].ap().opt()]),
           reads=["scin", "scpad"], writes=["scout"], tag="ag_sc")
    ALLP = min(128, ALLT // 128)
    ALLF = ALLT // ALLP
    PPC = T // ALLF
    p.dma("sync", Cc["allsc"][0:ALLP, 0:ALLF],
          mkap(g["cc_sc_out"].ap(), 0, [(SCPAD, NCORES), (ALLF, PPC), (1, ALLF)]),
          reads=["scout"], writes=["allsc"], after=fenceA, tag="ld_allsc")
    p.op("vector", lambda e: e.reduce_sum(Pm["cols"][0:ALLP, 8:9],
                                          Cc["allsc"][0:ALLP, 0:ALLF], AX.X),
         reads=["allsc"], writes=["gsum1"], tag="gsum1")
    p.op("tensor", lambda e: e.transpose(PS[0][0:1, 384:384 + ALLP],
                                         Pm["cols"][0:ALLP, 8:9],
                                         Pm["ident32"][0:ALLP, 0:ALLP]),
         reads=["gsum1", "c_ident32"], writes=psq(0, 3), tag="tr_sums")
    p.op("vector", lambda e: e.reduce_sum(Pm["srow32"][0:1, 1568:1569],
                                          PS[0][0:1, 384:384 + ALLP], AX.X),
         reads=psq(0, 3), writes=["gsum"], tag="gsum")

    if PXL < 2:
        return
    def level(pfx, np_, width, cur0, pong0, col0, niter=8):
        cur, pong = cur0, pong0
        for i in range(niter):
            p.op("vector", lambda e, cur=cur, np_=np_, width=width: e.max(
                    out=Pm["m8"][0:np_, :], in_=Cc["allsc"][0:np_, cur:cur + width]),
                 reads=[pfx + f"c{cur}"], writes=["m8v"], tag=pfx + "mx")
            p.op("vector", lambda e, i=i, np_=np_: e.tensor_copy(
                    Cc["allsc"][0:np_, col0 + 8 * i: col0 + 8 * i + 8], Pm["m8"][0:np_, :]),
                 reads=["m8v"], writes=[pfx + "C"], tag=pfx + "cp")
            if i < niter - 1:
                p.op("vector", lambda e, cur=cur, pong=pong, np_=np_, width=width:
                     e.match_replace(out=Cc["allsc"][0:np_, pong:pong + width],
                                     in_to_replace=Pm["m8"][0:np_, :],
                                     in_values=Cc["allsc"][0:np_, cur:cur + width],
                                     imm_value=-1e20),
                     reads=["m8v", pfx + f"c{cur}"], writes=[pfx + f"c{pong}"],
                     tag=pfx + "mr")
                cur, pong = pong, cur

    assert ALLF == 128
    p.buf("L1c0").writer = p.bufs["allsc"].writer
    level("L1", ALLP, 128, 0, 128, 256)
    p.dma("scalar", mkap(g["lred"].ap(), 0, [(64, ALLP), (1, 64)]),
          Cc["allsc"][0:ALLP, 256:320], reads=["L1C"], writes=["lr1"], tag="st_l1")
    n = ALLP * 64
    src_off, dst_off, prev_store = 0, n, "lr1"
    li = 2
    lastC = "L1C"
    pyr0 = ["L1c0"]
    while n > 64:
        np_ = max(1, n // 256)
        width = n // np_
        pfx = f"L{li}"
        p.dma("sync", Cc["allsc"][0:np_, 0:width],
              mkap(g["lred"].ap(), src_off, [(width, np_), (1, width)]),
              reads=[prev_store], writes=[pfx + "c0"], tag="ld_" + pfx)
        pyr0.append(pfx + "c0")
        level(pfx, np_, width, 0, 256, 512)
        lastC = pfx + "C"
        out_n = np_ * 64
        if out_n > 64:
            p.dma("scalar", mkap(g["lred"].ap(), dst_off, [(64, np_), (1, 64)]),
                  Cc["allsc"][0:np_, 512:576], reads=[pfx + "C"], writes=["lr" + pfx],
                  tag="st_" + pfx)
            prev_store = "lr" + pfx
            src_off, dst_off = dst_off, dst_off + out_n
        n = out_n
        li += 1
    if PXL < 3:
        return
    p.op("tensor", lambda e: e.matmul(PS[0][:, 392:393], Pm["onesrow32"][:, :],
                                      Cc["allsc"][0:1, 512 + 49:512 + 50],
                                      start=True, stop=True),
         reads=[lastC, "c_onesrow32"], writes=psq(0, 3), tag="mm_v50")
    p.op("vector", lambda e: e.tensor_copy(Pm["cols"][:, 9:10], PS[0][:, 392:393]),
         reads=psq(0, 3), writes=["v50"], tag="ev_v50")
    dump(ctx, "v50", Pm["cols"][:, 9:10], ["v50"])
    p.dma("sync", Cc["allsc"][0:ALLP, 0:128],
          mkap(g["cc_sc_out"].ap(), 0, [(SCPAD, NCORES), (ALLF, PPC), (1, ALLF)]),
          reads=["scout"], writes=["allsc2", "allsc"] + pyr0, tag="ld_allsc2")
    p.op("vector", lambda e: e.tensor_scalar(Cc["allsc"][0:ALLP, 128:256],
                                             Cc["allsc"][0:ALLP, 0:128],
                                             Pm["cols"][0:ALLP, 9:10], None, ALU.is_ge),
         reads=["allsc2", "v50"], writes=["mask_"], tag="mask1")
    p.op("vector", lambda e: e.scalar_tensor_tensor(
            Cc["allsc"][0:ALLP, 128:256], Cc["allsc"][0:ALLP, 0:128], TAU1,
            Cc["allsc"][0:ALLP, 128:256], ALU.is_gt, ALU.mult),
         reads=["allsc2", "mask_"], writes=["mask"], tag="mask2")
    p.op("vector", lambda e: e.reduce_sum(Pm["cols"][0:ALLP, 10:11],
                                          Cc["allsc"][0:ALLP, 128:256], AX.X),
         reads=["mask"], writes=["rowsum"], tag="rowsum")
    p.op("vector", lambda e: e.tensor_copy(Pm["colsB"][0:ALLP, 0:1],
                                           Pm["cols"][0:ALLP, 10:11]),
         reads=["rowsum"], writes=["rowsumB"], tag="rowsumB")
    p.op("tensor", lambda e: e.matmul(PS[0][0:ALLP, 393:394],
                                      Pm["strictup"][0:ALLP, 0:ALLP],
                                      Pm["colsB"][0:ALLP, 0:1], start=True, stop=True),
         reads=["rowsumB", "c_strictup"], writes=psq(0, 3), tag="mm_pfx")
    p.op("vector", lambda e: e.tensor_copy(Pm["cols"][0:ALLP, 11:12],
                                           PS[0][0:ALLP, 393:394]),
         reads=psq(0, 3), writes=["prefix"], tag="ev_pfx")
    p.op("tensor", lambda e: e.transpose(PS[0][0:1, 256:256 + ALLP],
                                         Pm["cols"][0:ALLP, 10:11],
                                         Pm["ident32"][0:ALLP, 0:ALLP]),
         reads=["rowsum", "c_ident32"], writes=psq(0, 2), tag="tr_cnt")
    p.op("vector", lambda e: e.reduce_sum(Pm["srow32"][0:1, 1569:1570],
                                          PS[0][0:1, 256:256 + ALLP], AX.X),
         reads=psq(0, 2), writes=["count"], tag="count")
    p.op("vector", lambda e: e.tensor_tensor_scan(
            Cc["allsc"][0:ALLP, 256:384], Cc["allsc"][0:ALLP, 128:256],
            Cc["allsc"][0:ALLP, 128:256], 0.0, ALU.add, ALU.bypass),
         reads=["mask"], writes=["incl", "L1C"], tag="incl")
    p.op("vector", lambda e: e.tensor_scalar(Cc["allsc"][0:ALLP, 256:384],
                                             Cc["allsc"][0:ALLP, 256:384],
                                             Pm["cols"][0:ALLP, 11:12], -1.0,
                                             ALU.add, ALU.add),
         reads=["incl", "prefix"], writes=["slots"], tag="slots")
    p.op("vector", lambda e: e.tensor_tensor(Cc["allsc"][0:ALLP, 256:384],
                                             Cc["allsc"][0:ALLP, 256:384],
                                             Cc["allsc"][0:ALLP, 128:256], ALU.mult),
         reads=["slots", "mask"], writes=["sm1"], tag="sm1")
    p.op("vector", lambda e: e.scalar_tensor_tensor(
            Cc["allsc"][0:ALLP, 256:384], Cc["allsc"][0:ALLP, 128:256], -1.0,
            Cc["allsc"][0:ALLP, 256:384], ALU.add, ALU.add),
         reads=["mask", "sm1"], writes=["stilde"], tag="stilde")
    p.op("vector", lambda e: e.tensor_copy(Cc["yrel"][0:ALLP, 0:128],
                                           Cc["allsc"][0:ALLP, 256:384]),
         reads=["stilde"], writes=["stildeB"], after=fenceA, tag="stildeB")
    dump(ctx, "stildeD", Cc["allsc"][0:ALLP, 256:384], ["stilde"])
    if PXL < 4:
        return
    p.dma("sync", Cc["summ_tok"][:, :],
          mkap(g["summ_spill"].ap(), 0, [(SUMD, 128), (128 * SUMD, NTILE), (1, SUMD)]),
          reads=["smspill"], writes=["summ_tok"], after=fenceA, tag="ld_summ_tok")
    p.dma("gpsimd", Cc["vw"][:, :], g["v_w"][:, :], writes=["wvw"], after=fenceA,
          tag="ld_vw")
    p.dma("sync", Pm["colsB"][:, 0:NTILE], g["rowsel"][:, :],
          writes=["rowsel", "rowsumB"], tag="ld_rowsel")
    NT = ctx["NTILE"]
    for j in range(NT):
        p.op("tensor", lambda e, j=j: e.matmul(
                PS[2][:, 0:1], Cc["yrel"][0:ALLP, 0:128], Pm["colsB"][0:ALLP, j:j + 1],
                start=True, stop=True),
             reads=["stildeB", "rowsel"], writes=psq(2, 0), tag="mm_scol")
        p.op("vector", lambda e: e.tensor_scalar(Cc["prb"][:, 0:64], Pm["iota64"][:, :],
                                                 PS[2][:, 0:1], None, ALU.is_equal),
             reads=psq(2, 0) + ["c_iota64"], writes=["PT"], tag="PT")
        p.op("tensor", lambda e, j=j: e.matmul(
                PS[1][0:64, 0:64], Cc["prb"][:, 0:64],
                Cc["summ_tok"][:, j * SUMD:(j + 1) * SUMD],
                start=(j == 0), stop=(j == NT - 1)),
             reads=["PT", "summ_tok"], writes=psq(1, 0, 4), tag="mm_pool")
    p.op("vector", lambda e: e.tensor_copy(Pm["pool"][:, :], PS[1][0:64, 0:64]),
         reads=psq(1, 0), writes=["pool_part"], tag="ev_pool")
    dump(ctx, "poolpart", Pm["pool"][:, :], ["pool_part"])
    dump(ctx, "summtok", Cc["summ_tok"][:, :], ["summ_tok"])
    p.dma("scalar", mkap(g["cc_pool_in"].ap(), 0, [(64, 64), (1, 64)]),
          Pm["pool"][:, :], reads=["pool_part"], writes=["poolin"], tag="st_pool")
    if PXL < 5:
        return
    p.coll(lambda e: e.collective_compute(
            "AllReduce", ALU.add, replica_groups=[list(range(NCORES))],
            ins=[g["cc_pool_in"].ap().opt()], outs=[g["cc_pool_out"].ap().opt()]),
           reads=["poolin", "poolpad"], writes=["poolout"], tag="ar_pool")
    p.dma("sync", Pm["pool"][:, :], mkap(g["cc_pool_out"].ap(), 0, [(64, 64), (1, 64)]),
          reads=["poolout"], writes=["pool"], tag="ld_pool")
    dump(ctx, "pool", Pm["pool"][:, :], ["pool"])
    p.op("vector", lambda e: e.tensor_scalar(Pm["srow32"][0:1, 1600:1664],
                                             Pm["iota64"][0:1, :],
                                             Pm["srow32"][0:1, 1569:1570], -1e9,
                                             ALU.is_ge, ALU.mult),
         reads=["count", "c_iota64"], writes=["mrow"], tag="mrow")
    p.op("tensor", lambda e: e.matmul(PS[0][:, 256:320], Pm["onesrow32"][:, :],
                                      Pm["srow32"][0:1, 1600:1664], start=True, stop=True),
         reads=["mrow", "c_onesrow32"], writes=psq(0, 2), tag="mm_mrep")
    p.op("vector", lambda e: e.tensor_copy(Pm["maskrep"][:, :], PS[0][:, 256:320]),
         reads=psq(0, 2), writes=["maskrep"], tag="ev_mrep")
    p.op("vector", lambda e: e.tensor_scalar(Pm["srow32"][0:1, 1570:1571],
                                             Pm["srow32"][0:1, 1568:1569],
                                             1.0 / ALLT, TAU2, ALU.mult, ALU.is_gt),
         reads=["gsum"], writes=["c1"], tag="c1")
    p.op("vector", lambda e: e.scalar_tensor_tensor(
            Pm["srow32"][0:1, 1571:1572], Pm["srow32"][0:1, 1569:1570], 0.5,
            Pm["srow32"][0:1, 1570:1571], ALU.is_gt, ALU.mult),
         reads=["count", "c1"], writes=["cond"], tag="cond")
    p.op("tensor", lambda e: e.matmul(PS[0][:, 320:321], Pm["onesrow32"][:, :],
                                      Pm["srow32"][0:1, 1571:1572], start=True, stop=True),
         reads=["cond", "c_onesrow32"], writes=psq(0, 2), tag="mm_cond")
    p.op("vector", lambda e: e.tensor_copy(Pm["cols"][:, 12:13], PS[0][:, 320:321]),
         reads=psq(0, 2), writes=["condcol"], tag="ev_cond")
    p.op("vector", lambda e: e.tensor_tensor(Pm["gacond"][:, :], Pm["modc"][:, 16:24],
                                             Pm["cols"][:, 12:13].to_broadcast([128, 8]),
                                             ALU.mult),
         reads=["condcol", "modc"], writes=["gacond"], tag="gacond")
    p.op("tensor", lambda e: e.transpose(PS[0][0:64, 384:448], Pm["pool"][:, :],
                                         Pm["ident32"][0:64, 0:64]),
         reads=["pool", "c_ident32"], writes=psq(0, 3), tag="tr_pool")
    p.op("vector", lambda e: e.tensor_copy(Pm["poolT"][:, :], PS[0][0:64, 384:448]),
         reads=psq(0, 3), writes=["poolT"], tag="ev_poolT")
    p.op("tensor", lambda e: e.matmul(PS[0][0:64, 0:64], Pm["kw"][:, :], Pm["poolT"][:, :],
                                      start=True, stop=True),
         reads=["poolT", "wkw"], writes=psq(0, 0), tag="mm_kk")
    p.op("scalar", lambda e: e.copy(Pm["kkT"][:, :], PS[0][0:64, 0:64]),
         reads=psq(0, 0), writes=["kkT"], tag="ev_kk")
    for v2 in range(2):
        p.op("tensor", lambda e, v2=v2: e.matmul(
                PS[1][0:64, :], Pm["poolT"][:, :], Cc["vw"][:, v2 * 512:(v2 + 1) * 512],
                start=True, stop=True),
             reads=["poolT", "wvw"], writes=psq(1, 0, 4), tag="mm_vv")
        p.op("scalar", lambda e, v2=v2: e.copy(Cc["vvb"][:, v2 * 512:(v2 + 1) * 512],
                                               PS[1][0:64, :]),
             reads=psq(1, 0, 4), writes=["vvb"], tag="ev_vv")


def phase_c(ctx):
    p, g, Pm, Cc, PS = ctx["p"], ctx["g"], ctx["Pm"], ctx["Cc"], ctx["PS"]
    T = ctx["T"]
    fenceA = p.fence()
    for kc in range(16):
        p.dma("gpsimd", Cc["w_gate"][:, kc * D:(kc + 1) * D], g["gate_w"][ts(kc), :],
              writes=["wgate"], after=fenceA if kc == 0 else (), tag="ld_gate")
    p.dma("sync", Cc["qT"][:, :], g["q_spill"][:, :], reads=["qspill"],
          writes=["qTfull"], after=fenceA, tag="ld_qT")
    for bt in range(T // 512):
        p.dma("sync", Cc["yrel"][:, :],
              mkap(g["y_spill"].ap(), bt * 512, [(T, 128), (128 * T, 8), (1, 512)]),
              reads=["yspill"], writes=["yrel", "stildeB"], tag="ld_yrel")
        p.dma("sync", Cc["x2slab"][:, :],
              mkap(g["x2p_spill"].ap(), bt * 512, [(T, 128), (128 * T, 8), (1, 512)]),
              reads=["x2spill"], writes=["x2slab"], tag="ld_x2")
        for tt in range(4):
            t0 = bt * 512 + tt * 128
            p.op("tensor", lambda e, t0=t0: e.matmul(
                    PS[0][:, 0:64], Cc["qT"][:, t0:t0 + 128], Pm["kkT"][:, :],
                    start=True, stop=True),
                 reads=["qTfull", "kkT"], writes=psq(0, 0, 2), tag="mm_lg")
            p.op("vector", lambda e: e.scalar_tensor_tensor(
                    Cc["logits"][:, :], PS[0][:, 0:64], 0.25, Pm["maskrep"][:, :],
                    ALU.mult, ALU.add),
                 reads=psq(0, 0) + ["maskrep"], writes=["lg"], tag="lg")
            if bt == 0 and tt == 0:
                dump(ctx, "lgD", Cc["logits"][:, :], ["lg"])
            p.op("vector", lambda e: e.reduce_max(Pm["cols"][:, 13:14], Cc["logits"][:, :],
                                                  AX.X, negate=True),
                 reads=["lg"], writes=["negm"], tag="negm")
            p.op("scalar", lambda e: e.activation(Cc["logits"][:, :], Cc["logits"][:, :],
                                                  ACT.Exp, bias=Pm["cols"][:, 13:14],
                                                  scale=1.0, accum_out=Pm["cols"][:, 14:15]),
                 reads=["lg", "negm"], writes=["lg", "den"], tag="lexp")
            p.op("vector", lambda e: e.reciprocal(Pm["cols"][:, 15:16], Pm["cols"][:, 14:15]),
                 reads=["den"], writes=["rden"], tag="rden")
            p.op("vector", lambda e: e.tensor_scalar(Cc["prb"][:, 0:64], Cc["logits"][:, :],
                                                     Pm["cols"][:, 15:16], None, ALU.mult),
                 reads=["lg", "rden"], writes=["prbC"], tag="prb")
            if bt == 0 and tt == 0:
                dump(ctx, "prbD", Cc["prb"][:, 0:64], ["prbC"])
            p.op("tensor", lambda e: e.transpose(PS[0].bitcast(BF16)[0:64, 256:384],
                                                 Cc["prb"][:, 0:64], Pm["ident"][:, :]),
                 reads=["prbC", "c_ident"], writes=psq(0, 0, 2), tag="tr_prb")
            p.op("vector", lambda e, t0=t0: e.tensor_copy(
                    Cc["probsT"][:, t0:t0 + 128], PS[0].bitcast(BF16)[0:64, 256:384]),
                 reads=psq(0, 1), writes=["probsT"], tag="ev_prb")
        for kd in range(8):
            bk = 2 + kd % 2
            p.op("tensor", lambda e, kd=kd, bt=bt, bk=bk: e.matmul(
                    PS[bk][:, :], Cc["vvb"][:, kd * 128:(kd + 1) * 128],
                    Cc["probsT"][:, bt * 512:(bt + 1) * 512], start=True, stop=True),
                 reads=["probsT", "vvb"], writes=psq(bk, 0, 4), tag="mm_rt"),
            p.op("scalar", lambda e, kd=kd, bk=bk: e.copy(
                    Cc["retrT"][:, kd * 512:(kd + 1) * 512], PS[bk][:, :]),
                 reads=psq(bk, 0, 4), writes=["retrT"], tag="ev_rt")
            if bt == 0 and kd == 0:
                dump(ctx, "retrD", Cc["retrT"][:, 0:512], ["retrT"])
        for kd in range(8):
            bk = 4 + kd % 2
            for kc in range(16):
                rhs = (Cc["yrel"][:, kc * 512:(kc + 1) * 512] if kc < 8 else
                       Cc["retrT"][:, (kc - 8) * 512:(kc - 7) * 512])
                rb = "yrel" if kc < 8 else "retrT"
                p.op("tensor", lambda e, kd=kd, kc=kc, bk=bk, rhs=rhs: e.matmul(
                        PS[bk][:, :], Cc["w_gate"][:, kc * D + kd * 128:
                                                   kc * D + kd * 128 + 128],
                        rhs, start=(kc == 0), stop=(kc == 15)),
                     reads=[rb, "wgate"], writes=psq(bk, 0, 4), tag="mm_g")
            p.op("scalar", lambda e, bk=bk: e.activation(Cc["gterm"][:, :], PS[bk][:, :],
                                                         ACT.Sigmoid, bias=0.0, scale=1.0),
                 reads=psq(bk, 0, 4), writes=["gterm"], tag="ev_g")
            p.op("vector", lambda e, kd=kd: e.tensor_tensor(
                    Cc["gterm"][:, :], Cc["gterm"][:, :],
                    Cc["retrT"][:, kd * 512:(kd + 1) * 512], ALU.mult),
                 reads=["gterm", "retrT"], writes=["gterm"], tag="gr"),
            if bt == 0 and kd == 0:
                dump(ctx, "gtD", Cc["gterm"][:, :], ["gterm"])
            p.op("vector", lambda e, kd=kd: e.scalar_tensor_tensor(
                    Cc["x2slab"][:, kd * 512:(kd + 1) * 512], Cc["gterm"][:, :],
                    Pm["gacond"][:, kd:kd + 1], Cc["x2slab"][:, kd * 512:(kd + 1) * 512],
                    ALU.mult, ALU.add),
                 reads=["gterm", "gacond", "x2slab"], writes=["x2slab"], tag="x2fin")
        p.dma("scalar", mkap(g["x2p_spill"].ap(), bt * 512,
                             [(T, 128), (128 * T, 8), (1, 512)]),
              Cc["x2slab"][:, :], reads=["x2slab"], writes=["x2spill"], tag="st_x2")
    dump(ctx, "probsT", Cc["probsT"][:, :], ["probsT"])


def phase_b(ctx):
    p, g, Pm, Bb, PS = ctx["p"], ctx["g"], ctx["Pm"], ctx["Bb"], ctx["PS"]
    T = ctx["T"]
    NT512 = T // 512
    fenceC = p.fence()
    p.dma("sync", Bb["cb1rep"][:, :], mkap(g["mod_hbm"].ap(), 4 * D, [(0, 128), (1, D)]),
          reads=["modhbm2"], writes=["cb1rep"], after=fenceC, tag="bc_cb")
    p.op("vector", lambda e: e.tensor_scalar(Bb["cb1rep"][:, :], Bb["cb1rep"][:, :],
                                             1.0, None, ALU.add),
         reads=["cb1rep"], writes=["cb1rep"], tag="cb1")
    p.dma("sync", Bb["sbrep"][:, :], mkap(g["mod_hbm"].ap(), 3 * D, [(0, 128), (1, D)]),
          reads=["modhbm1"], writes=["sbrep"], after=fenceC, tag="bc_sb")

    WQ = WQ_SHARD_ELEMS
    for tg in range(max(1, T // 1024)):
        TGW = min(1024, T)
        for sub in range(TGW // 512):
            bts = tg * 2 + sub
            if BBL == 10:
                continue
            p.dma("sync", Bb["x2f"][:, :],
                  mkap(g["x2p_spill"].ap(), bts * 512, [(T, 128), (128 * T, 8), (1, 512)]),
                  reads=["x2spill"], writes=["x2f"], tag="ld_x2f")
            if BBL == 105:
                continue
            for kd in range(8):
                bk = 6 + kd % 2
                for q4 in range(4):
                    p.op("tensor", lambda e, kd=kd, q4=q4, bk=bk: e.transpose(
                            PS[bk][:, q4 * 128:q4 * 128 + 128],
                            Bb["x2f"][:, kd * 512 + q4 * 128: kd * 512 + q4 * 128 + 128],
                            Pm["ident32"][:, :]),
                         reads=["x2f", "c_ident32"], writes=psq(bk, 0, 4), tag="tr_x2")
                p.op("scalar", lambda e, kd=kd, bk=bk: e.copy(
                        apx(Bb["x2tok"], kd * 128, [(D, 4), (1, 128)]),
                        PS[bk][:, 0:512]),
                     reads=psq(bk, 0, 4), writes=["x2tok"], tag="ev_x2")
            if BBL == 11:
                continue
            Cx = Pm["cols"]
            for tt in range(4):
                xt = Bb["x2tok"][:, tt * D:(tt + 1) * D]
                p.op("scalar", lambda e, xt=xt: e.activation(
                        xt, xt, ACT.Identity, bias=0.0, scale=1.0, accum_out=Cx[:, 0:1]),
                     reads=["x2tok"], writes=["x2tok", "l2sum"], tag="l2_s1")
                p.op("scalar", lambda e, xt=xt, tt=tt: e.activation(
                        Bb["h2q"][:, tt * D:(tt + 1) * D], xt, ACT.Square,
                        bias=0.0, scale=1.0, accum_out=Cx[:, 1:2]),
                     reads=["x2tok"], writes=["l2sq", "h2qdump"], tag="l2_s2")
                p.op("vector", lambda e: e.tensor_scalar(Cx[:, 2:3], Cx[:, 0:1],
                                                         1.0 / D, None, ALU.mult),
                     reads=["l2sum"], writes=["l2m"], tag="l2_m")
                p.op("vector", lambda e: e.tensor_tensor(Cx[:, 3:4], Cx[:, 2:3],
                                                         Cx[:, 2:3], ALU.mult),
                     reads=["l2m"], writes=["l2m2"], tag="l2_m2")
                p.op("vector", lambda e: e.scalar_tensor_tensor(
                        Cx[:, 4:5], Cx[:, 1:2], 1.0 / D, Cx[:, 3:4],
                        ALU.mult, ALU.subtract),
                     reads=["l2sq", "l2m2"], writes=["l2var"], tag="l2_var")
                p.op("scalar", lambda e: e.activation(Cx[:, 5:6], Cx[:, 4:5], ACT.Sqrt,
                                                      bias=Pm["kconst"][:, 0:1], scale=1.0),
                     reads=["l2var", "kconst"], writes=["l2sd"], tag="l2_sd")
                p.op("vector", lambda e: e.reciprocal(Cx[:, 6:7], Cx[:, 5:6]),
                     reads=["l2sd"], writes=["l2rstd"], tag="l2_rstd")
                p.op("vector", lambda e, xt=xt: e.tensor_scalar(
                        Bb["h2"][:, 0:D], xt, Cx[:, 2:3], Cx[:, 6:7],
                        ALU.subtract, ALU.mult),
                     reads=["x2tok", "l2m", "l2rstd"], writes=["h2u"], tag="l2_u")
                p.op("vector", lambda e: e.tensor_tensor(Bb["h2"][:, 0:D], Bb["h2"][:, 0:D],
                                                         Bb["cb1rep"][:, :], ALU.mult),
                     reads=["h2u", "cb1rep"], writes=["h2u"], tag="l2_cb")
                p.op("vector", lambda e: e.tensor_tensor(Bb["h2"][:, 0:D], Bb["h2"][:, 0:D],
                                                         Bb["sbrep"][:, :], ALU.add),
                     reads=["h2u", "sbrep"], writes=["h2m"], tag="l2_sb")
                if BBL == 12:
                    continue
                scr = dict(xn=Bb["q1"][:, :], a=Bb["q2"][:, :], acc=Bb["q3"][:, :],
                           hi=Bb["h2"].bitcast(F32)[:, 0:D] if False else None,
                           sc=(Bb["qsc"], 0, 64, 0), m=Bb["qm"][:, :])
                scr["hi"] = Bb["hffn"][:, 2048:3072]
                scr["sc"] = (Bb["qsc"], 0, 64, 0)
                quant_chain(ctx, "vector", Bb["h2"][:, 0:D],
                            Bb["h2q"][:, tt * D:(tt + 1) * D], D, scr, ["h2m"], "qh2")
            if BBL == 13:
                continue
            for kd in range(8):
                bk = 6 + kd % 2
                for tt4 in range(4):
                    p.op("tensor", lambda e, kd=kd, tt4=tt4, bk=bk: e.transpose(
                            PS[bk].bitcast(BF16)[:, tt4 * 128:tt4 * 128 + 128],
                            Bb["h2q"][:, tt4 * D + kd * 128: tt4 * D + kd * 128 + 128],
                            Pm["ident"][:, :]),
                         reads=["qh2", "c_ident"], writes=psq(bk, 0, 4), tag="tr_h2q")
                p.op("scalar", lambda e, kd=kd, sub=sub, bk=bk: e.copy(
                        Bb["h2qT"][:, kd * 1024 + sub * 512:
                                   kd * 1024 + sub * 512 + 512],
                        PS[bk].bitcast(BF16)[:, 0:512]),
                     reads=psq(bk, 0, 4), writes=["h2qT"], tag="ev_h2q")
            if tg == 0 and sub == 0:
                dump(ctx, "h2qD", Bb["h2q"][:, :], ["qh2"])
        if BBL < 2:
            continue
        for sh in range(NCORES):
            ns = FFN_SH[sh]
            nch = ns // 128
            base_icc = sum(FFN_SH[:sh]) // 128
            wgu_t = Bb["wgu"] if sh % 2 == 0 else Bb["wguB"]
            wgu_b = f"wgu{sh % 2}"
            hffn_t = Bb["hffn"] if sh % 2 == 0 else Bb["hffnB"]
            hffn_b = f"hffn{sh % 2}"
            for kd in range(8):
                p.dma("sync", wgu_t[:, kd * 384:kd * 384 + ns],
                      mkap(g["cc_wq_out"].ap(), sh * WQ + kd * 128 * 384,
                           [(384, 128), (1, ns)]),
                      reads=["wqout"], writes=[wgu_b], tag="ld_wg")
                p.dma("sync", wgu_t[:, 8 * 384 + kd * 384: 8 * 384 + kd * 384 + ns],
                      mkap(g["cc_wq_out"].ap(), sh * WQ + 384 * D + kd * 128 * 384,
                           [(384, 128), (1, ns)]),
                      reads=["wqout"], writes=[wgu_b], tag="ld_wu")
            for nl in range(nch):
                for t5 in range(TGW // 512):
                    for (mi, mb) in ((0, 0), (1, 8 * 384)):
                        bk = mi
                        for kd in range(8):
                            p.op("tensor", lambda e, nl=nl, t5=t5, kd=kd, mb=mb, bk=bk,
                                 wgu_t=wgu_t: e.matmul(
                                    PS[bk][:, :],
                                    wgu_t[:, mb + kd * 384 + nl * 128:
                                          mb + kd * 384 + nl * 128 + 128],
                                    Bb["h2qT"][:, kd * 1024 + t5 * 512:
                                               kd * 1024 + t5 * 512 + 512],
                                    start=(kd == 0), stop=(kd == 7)),
                                 reads=["h2qT", wgu_b], writes=psq(bk, 0, 4), tag="mm_gu")
                    p.op("scalar", lambda e: e.activation(Bb["ftmp"][:, :], PS[0][:, :],
                                                          ACT.Silu, bias=0.0, scale=1.0),
                         reads=psq(0, 0, 4), writes=["gsilu", "fout"], tag="ev_gs")
                    p.op("vector", lambda e, nl=nl, t5=t5, hffn_t=hffn_t: e.tensor_tensor(
                            hffn_t[:, nl * 1024 + t5 * 512: nl * 1024 + t5 * 512 + 512],
                            Bb["ftmp"][:, :], PS[1][:, :], ALU.mult),
                         reads=["gsilu"] + psq(1, 0, 4), writes=[hffn_b], tag="hffn")
                    if tg == 0 and sh == 0 and nl == 0 and t5 == 0:
                        dump(ctx, "hffnD", Bb["hffn"][:, 0:512], ["hffn0"])
            if BBL < 3:
                continue
            for tt8 in range(TGW // 128):
                for nl in range(nch):
                    p.op("tensor", lambda e, nl=nl, tt8=tt8, hffn_t=hffn_t: e.transpose(
                            PS[2][:, nl * 128:(nl + 1) * 128],
                            hffn_t[:, nl * 1024 + tt8 * 128: nl * 1024 + tt8 * 128 + 128],
                            Pm["ident32"][:, :]),
                         reads=[hffn_b, "c_ident32"], writes=psq(2, nl), tag="tr_hf")
                scr = dict(xn=Bb["q1"][:, 0:ns], a=Bb["q2"][:, 0:ns], acc=Bb["q3"][:, 0:ns],
                           hi=Bb["outtok"][:, 0:ns], sc=(Bb["qsc"], 0, 64, 0),
                           m=Bb["qm"][:, 0:ns])
                quant_chain(ctx, "vector", PS[2][:, 0:ns], Bb["hfq"][:, 0:ns], ns, scr,
                            psq(2, 0, nch), "qhf")
                for nl in range(nch):
                    p.op("tensor", lambda e, nl=nl: e.transpose(
                            PS[3].bitcast(BF16)[:, nl * 128:nl * 128 + 128],
                            Bb["hfq"][:, nl * 128:(nl + 1) * 128], Pm["ident"][:, :]),
                         reads=["qhf", "c_ident"], writes=psq(3, 0, 4), tag="tr_hq")
                p.op("scalar", lambda e, nch=nch, tt8=tt8, base_icc=base_icc: e.copy(
                        apx(Bb["hq"], base_icc * 1024 + tt8 * 128, [(1024, nch), (1, 128)]),
                        PS[3].bitcast(BF16)[:, 0:nch * 128]),
                     reads=psq(3, 0, 4), writes=[f"hq{sh}"], tag="ev_hq")
        if BBL < 4:
            continue
        for kd in range(8):
            p.dma("sync", Bb["wd"][:, :],
                  mkap(g["cc_wq_out"].ap(), kd * WQ + 2 * 384 * D,
                       [(128, 128), (16384, 22), (1, 128)]),
                  reads=["wqout"], writes=["wdb"], tag="ld_wd")
            for t5 in range(TGW // 512):
                bk = 4 + t5 % 2
                for icc in range(22):
                    p.op("tensor", lambda e, icc=icc, t5=t5, bk=bk: e.matmul(
                            PS[bk][:, :], Bb["wd"][:, icc * 128:(icc + 1) * 128],
                            Bb["hq"][:, icc * 1024 + t5 * 512: icc * 1024 + t5 * 512 + 512],
                            start=(icc == 0), stop=(icc == 21)),
                         reads=[f"hq{s_}" for s_ in range(NCORES)] + ["wdb"],
                         writes=psq(bk, 0, 4), tag="mm_dn")
                bts = tg * 2 + t5
                p.dma("sync", Bb["ftmp"][:, :],
                      mkap(g["x2p_spill"].ap(), kd * 128 * T + bts * 512,
                           [(T, 128), (1, 512)]),
                      reads=["x2spill"], writes=["x2fd", "gsilu", "fout"],
                      tag="ld_x2d")
                p.op("vector", lambda e, kd=kd, bk=bk: e.scalar_tensor_tensor(
                        Bb["ftmp"][:, :], PS[bk][:, :], Pm["modc"][:, 40 + kd:41 + kd],
                        Bb["ftmp"][:, :], ALU.mult, ALU.add),
                     reads=psq(bk, 0, 4) + ["modc", "x2fd"], writes=["fout"], tag="fout")
                if tg == 0 and kd == 0 and t5 == 0:
                    dump(ctx, "foutD", Bb["ftmp"][:, :], ["fout"])
                    dump(ctx, "hqD", Bb["hq"][:, 0:512], ["hq"])
                obk = 6 + kd % 2
                for q4 in range(4):
                    p.op("tensor", lambda e, q4=q4, obk=obk: e.transpose(
                            PS[obk][:, q4 * 128:(q4 + 1) * 128],
                            Bb["ftmp"][:, q4 * 128:(q4 + 1) * 128], Pm["ident32"][:, :]),
                         reads=["fout", "c_ident32"], writes=psq(obk, 0, 4), tag="tr_out")
                p.op("scalar", lambda e, obk=obk: e.copy(
                        Bb["outtok"][:, 0:512], PS[obk][:, 0:512]),
                     reads=psq(obk, 0, 4), writes=["outtok"], tag="ev_out")
                p.dma("scalar",
                      mkap(g["out"].ap(), bts * 512 * D + kd * 128,
                           [(D, 128), (128 * D, 4), (1, 128)]),
                      Bb["outtok"][:, 0:512], reads=["outtok"], writes=["outhbm"],
                      tag="st_out")


_CACHE = {}


def _prep(inputs, b, T):
    f32 = lambda a: np.ascontiguousarray(np.asarray(a, dtype=np.float32))
    wg = f32(inputs["L0_ffn_w_gate"])
    wu = f32(inputs["L0_ffn_w_up"])
    wd = f32(inputs["L0_ffn_w_down"])
    off = [0]
    for sgl in FFN_SH:
        off.append(off[-1] + sgl)
    def pad384(m):
        out = np.zeros((384, D), np.float32)
        out[:m.shape[0]] = m
        return out
    rowsel = np.zeros((128, T // 128), ml_dtypes.bfloat16)
    for j in range(T // 128):
        rowsel[(b * T + j * 128) // 128, j] = 1
    return {
        "x": f32(inputs["L0_attn_x"][b]),
        "norm_w": f32(inputs["L0_attn_norm_w"]),
        "in_proj_w": f32(inputs["L0_attn_in_proj_w"]),
        "conv_w": f32(inputs["L0_attn_conv_w"]),
        "conv_b": f32(inputs["L0_attn_conv_b"]),
        "dt_bias": f32(inputs["L0_attn_dt_bias"]),
        "A_log": f32(inputs["L0_attn_A_log"]),
        "Dvec": f32(inputs["L0_attn_D"]),
        "ssm_norm_w": f32(inputs["L0_attn_ssm_norm_w"]),
        "out_proj_w": f32(inputs["L0_attn_out_proj_w"]),
        "scorer_w1": f32(inputs["L0_attn_scorer_w1"]),
        "scorer_w2": f32(inputs["L0_attn_scorer_w2"]).reshape(SC_HID),
        "summ_w": f32(inputs["L0_attn_summ_w"]),
        "q_w": f32(inputs["L0_attn_q_w"]),
        "k_w": f32(inputs["L0_attn_k_w"]),
        "v_w": f32(inputs["L0_attn_v_w"]),
        "gate_w": f32(inputs["L0_attn_gate_w"]),
        "cvec": f32(inputs["L0_c"][b]),
        "wmod": f32(inputs["L0_wmod"]),
        "wg_shard": pad384(wg[off[b]:off[b + 1]]),
        "wu_shard": pad384(wu[off[b]:off[b + 1]]),
        "wd_shard": np.ascontiguousarray(wd[b * WD_SH:(b + 1) * WD_SH]),
        "rowsel": rowsel,
    }


def kernel(**inputs):
    T = int(np.asarray(inputs["L0_attn_x"]).shape[1])
    key = (T, tuple(sorted(n for (n, _, _) in DEBUG)))
    if key not in _CACHE:
        _CACHE[key] = build(T=T, dbg=DEBUG)
    nc = _CACHE[key]
    in_maps = [_prep(inputs, b, T) for b in range(NCORES)]
    res = run_bass_kernel_spmd(nc, in_maps, core_ids=list(range(NCORES)))
    out = np.stack([np.asarray(r["out"], dtype=np.float32) for r in res.results])
    if DEBUG:
        kernel._dbg = res.results
    return out
